# version 18
# baseline (speedup 1.0000x reference)
import numpy as np
import ml_dtypes
import concourse.bass as bass
import concourse.mybir as mybir
from concourse.bass_utils import run_bass_kernel_spmd
from concourse.ap import AP
from contextlib import ExitStack

import concourse.bass_utils as _bu
_orig_gwa = _bu.get_walrus_args
def _gwa(*a, **k):
    return [x.replace("--enable-ldw-opt=false", "--enable-ldw-opt=true") if isinstance(x, str) else x
            for x in _orig_gwa(*a, **k)]
_bu.get_walrus_args = _gwa

F32 = mybir.dt.float32
BF16 = mybir.dt.bfloat16
AF = mybir.ActivationFunctionType
ALU = mybir.AluOpType

HID = 2048
NH = 16
FD = 768
FC = 6
EPS_LN = 1e-5
EPS_AT = 1e-8
EPS_L2 = 1e-12
INTER = 8192
AR_COLS = 12288 + 96
WGU_ELEMS = HID * 2 * INTER
WD_ELEMS = INTER * HID
WTOT = WGU_ELEMS + WD_ELEMS
ATT_OFF = {"wk": 0, "wv": 2048 * 512, "wmb": 2 * 2048 * 512,
           "mem4": 2 * 2048 * 512 + 2048 * 2048,
           "wq": 2 * 2048 * 512 + 2048 * 2048 + 768 * 2048,
           "wg": 2 * 2048 * 512 + 2 * 2048 * 2048 + 768 * 2048}
ATT_ELEMS = 2 * 2048 * 512 + 3 * 2048 * 2048 + 768 * 2048
AX = mybir.AxisListType


class Prog:
    def __init__(self, nc, es, sem_names):
        self.nc = nc
        self.sems = {n: es.enter_context(nc.semaphore(n)) for n in sem_names}
        self.counts = {n: 0 for n in sem_names}
        self.steps = []
        self.last_wait = {}

    def step(self, eng, emit, waits=(), sem=None, inc=0):
        tok = None
        if sem is not None:
            self.counts[sem] += inc
            tok = (sem, self.counts[sem])
        self.steps.append((eng, tuple(w for w in waits if w is not None), emit, sem, inc))
        return tok

    def tok(self, sem):
        return (sem, self.counts[sem])

    def emit_engine(self, eng_name, eng):
        for e, waits, emit, sem, inc in self.steps:
            if e != eng_name:
                continue
            for (ws, wv) in waits:
                if wv <= 0:
                    continue
                if self.last_wait.get((eng_name, ws), -1) >= wv:
                    continue
                self.last_wait[(eng_name, ws)] = wv
                eng.wait_ge(self.sems[ws], wv)
            inst = emit(eng)
            if sem is not None and inc > 0:
                assert inst is not None
                inst.then_inc(self.sems[sem], inc)


def build_program(n_cores=8, n_tiles=16, debug_outs=False):
    NT = n_tiles
    NTOK = NT * 128
    SUP = 2
    NSUP = NT // SUP
    TG = min(1024, NTOK)
    NG = NTOK // TG
    TILES_G = TG // 128
    TS = (TG + 511) // 512
    TSW = TG // TS

    nc = bass.Bass("TRN2", target_bir_lowering=False)

    def ein(name, shape, dt=F32):
        return nc.dram_tensor(name, list(shape), dt, kind="ExternalInput").ap()

    xT = ein("xT", (HID, NTOK))
    x_in = ein("x_in", (NTOK, HID))
    mem4 = ein("mem4", (FD, HID))
    nn96 = ein("nn96", (128, 96))
    nn96b = ein("nn96b", (128, 96))
    biasA = ein("biasA", (128, 3072), BF16)
    biasQ = ein("biasQ", (128, HID), BF16)
    biasG = ein("biasG", (128, HID), BF16)
    wconv = ein("wconv", (128, WTOT // n_cores // 128))
    watt = ein("watt", (ATT_ELEMS,))
    idb = ein("idb", (128, 128), BF16)
    id32 = ein("id32", (128, 128), F32)
    ones_col = ein("ones_col", (128, 1), F32)
    ones_row = ein("ones_row", (1, 128), F32)

    y2_out = nc.dram_tensor("y2", [NTOK, HID], F32, kind="ExternalOutput").ap()
    if debug_outs:
        dbg_ar = nc.dram_tensor("dbg_ar", [128, AR_COLS], F32, kind="ExternalOutput").ap()
        dbg_mh = nc.dram_tensor("dbg_mh", [NTOK, HID], F32, kind="ExternalOutput").ap()
        dbg_y1 = nc.dram_tensor("dbg_y1", [NTOK, HID], F32, kind="ExternalOutput").ap()

    ag_in = nc.dram_tensor("ag_in", [WTOT // n_cores], BF16, kind="Internal").ap()
    att_bf = nc.dram_tensor("att_bf", [ATT_ELEMS], BF16, kind="Internal").ap()
    ag_out = nc.dram_tensor("ag_out", [WTOT], BF16, kind="Internal").ap()
    ar_in = nc.dram_tensor("ar_in", [128, AR_COLS], F32, kind="Internal").ap()
    ar_out = nc.dram_tensor("ar_out", [128, AR_COLS], F32, kind="Internal").ap()
    mh_sp = nc.dram_tensor("mh_sp", [NTOK, HID], F32, kind="Internal").ap()
    y1_sp = nc.dram_tensor("y1_sp", [NTOK, HID], F32, kind="Internal").ap()
    ln2_sp = nc.dram_tensor("ln2_sp", [NTOK, HID], BF16, kind="Internal").ap()
    stats_dram = nc.dram_tensor("stats_dram", [NT, 256], F32, kind="Internal").ap()

    sem_names = ["sx0", "sx1", "sxx0", "sxx1", "sld", "sgA", "sgA2", "sgQ", "sgG", "scvA", "scvB",
                 "scv0", "scv1", "scs0", "scs1", "sag", "sar", "sda",
                 "sstm", "ssty", "sstl", "sstr", "sdbg", "smh0", "smh1", "sda1",
                 "sy10", "sl2t", "sfg0", "sfg1", "sfg2", "sfg3", "sfd0", "sfd1",
                 "sout", "sarld", "sarn", "srw0", "srw1", "sdbg0", "sdbg1", "sdb2",
                 "T", "V", "S"]

    es = ExitStack()
    with es:
        p = Prog(nc, es, sem_names)

        def alloc(name, shape, dt=F32, stack=None):
            return (stack or es).enter_context(nc.sbuf_tensor(name, list(shape), dt))

        id_b = alloc("id_b", (128, 128), BF16)
        id_f = alloc("id_f", (128, 128), F32)
        onc = alloc("onc", (128, 1), F32)
        onr = alloc("onr", (1, 128), F32)
        stats_cols = alloc("stats_cols", (128, NT * 2), F32)
        srow = alloc("srow", (1, 384), F32)
        rowbuf = alloc("rowbuf", (1, 256), F32)
        rowb2 = [alloc(f"rowb2_{k}", (1, 256), F32) for k in range(2)]
        dummyV = alloc("dummyV", (1, 8), F32)
        dummyS = alloc("dummyS", (1, 8), F32)

        p.step("sync", lambda e: e.dma_start(out=id_b[:], in_=idb), sem="sld", inc=16)
        p.step("sync", lambda e: e.dma_start(out=id_f[:], in_=id32), sem="sld", inc=16)
        p.step("sync", lambda e: e.dma_start(out=onc[:], in_=ones_col), sem="sld", inc=16)
        p.step("sync", lambda e: e.dma_start(out=onr[:], in_=ones_row), sem="sld", inc=16)
        t_ld0 = p.tok("sld")

        def cast3d(dst, src, cols, nchunk, stride):
            return lambda e: e.dma_start(
                out=dst[:, 0:nchunk * cols],
                in_=AP(src.tensor, 0, [[stride, 128], [128 * stride, nchunk], [1, cols]]))

        def ld3d(dst, off, cols, nchunk, stride):
            return lambda e: e.dma_start(
                out=dst[:, 0:nchunk * cols],
                in_=AP(att_bf.tensor, off, [[stride, 128], [128 * stride, nchunk], [1, cols]]))

        def barrier():
            stores = [p.tok(s) for s in ("sstm", "ssty", "sstl", "sstr",
                                         "sdbg0", "sdbg1", "sdb2", "sda", "scs0", "scs1")]
            tT = p.step("tensor", lambda e: e.ldweights(id_b[:]),
                waits=[p.tok("V"), p.tok("S")] + stores, sem="T", inc=1)
            tV = p.step("vector", lambda e: e.memset(dummyV[0:1, 0:1], 0.0),
                        waits=[p.tok("T")], sem="V", inc=1)
            tS = p.step("scalar", lambda e: e.memzero(dummyS[0:1, 0:1]),
                        waits=[p.tok("T")], sem="S", inc=1)
            return [tT, tV, tS, p.tok("V"), p.tok("S")]


        esA = ExitStack()
        tA = {}
        with esA:
            wk_b = alloc("wk_b", (128, 16 * 512), BF16, esA)
            wv_b = alloc("wv_b", (128, 16 * 512), BF16, esA)
            wmb_b = alloc("wmb_b", (128, 16 * 2048), BF16, esA)
            mem4_b = alloc("mem4_b", (128, FC * 2048), BF16, esA)
            nn96b_b = alloc("nn96b_b", (128, 96), BF16, esA)
            biasA_s = alloc("biasA_s", (128, 3072), BF16, esA)
            xT_t = [alloc("xT_0", (128, HID), F32, esA)]
            scr = alloc("scr", (128, HID), F32, esA)
            ln1xT = [alloc(f"ln1xT_{k}", (128, HID), BF16, esA) for k in range(2)]
            kf = alloc("kf", (128, 512), F32, esA)
            vf = [alloc(f"vf_{k}", (128, 512), F32, esA) for k in range(2)]
            mbp = alloc("mbp", (128, 2048), F32, esA)
            x2k = alloc("x2k", (128, 4 * 512), BF16, esA)
            featk = [alloc(f"featk_{k}", (128, 4 * FD), BF16, esA) for k in range(2)]
            mkT = alloc("mkT", (128, 4 * FD), BF16, esA)
            wmv4 = [alloc(f"wmv4_{k}", (128, 4 * 512), BF16, esA) for k in range(2)]
            coefb = alloc("coefb", (128, 16), BF16, esA)
            sm = alloc("sm", (128, 80), F32, esA)
            pvt = alloc("pvt", (128, 512), F32, esA)
            dstg = [alloc(f"dstg_{k}", (128, 512), F32, esA) for k in range(2)]

            psStat = esA.enter_context(nc.psum_tensor("psStat", [128, 512], F32))
            psBC = esA.enter_context(nc.psum_tensor("psBC", [128, 256], F32))
            psN = [esA.enter_context(nc.psum_tensor(f"psN{k}", [128, 512], F32)) for k in range(4)]
            psT = esA.enter_context(nc.psum_tensor("psT", [128, 1024], BF16))
            psDN = esA.enter_context(nc.psum_tensor("psDN", [128, 96], F32))

            KV_ELEMS = 2 * 2048 * 512
            t_cv1 = p.step("gpsimd", lambda e: e.dma_start(
                out=AP(att_bf.tensor, 0, [[1, KV_ELEMS]]),
                in_=AP(watt.tensor, 0, [[1, KV_ELEMS]])), sem="scvA", inc=16)
            p.step("gpsimd", lambda e: e.dma_start(
                out=AP(att_bf.tensor, KV_ELEMS, [[1, ATT_ELEMS - KV_ELEMS]]),
                in_=AP(watt.tensor, KV_ELEMS, [[1, ATT_ELEMS - KV_ELEMS]])), sem="scvB", inc=16)
            p.step("gpsimd", lambda e: e.dma_start(out=nn96b_b[:], in_=nn96b), sem="scvB", inc=16)
            tok_nn = p.tok("scvB")
            p.step("sync", ld3d(wk_b, ATT_OFF["wk"], 512, 16, 512),
                   waits=[t_cv1], sem="sgA", inc=16)
            p.step("sync", ld3d(wv_b, ATT_OFF["wv"], 512, 16, 512),
                   waits=[t_cv1], sem="sgA", inc=16)
            p.step("sync", lambda e: e.dma_start(out=biasA_s[:], in_=biasA), sem="sgA", inc=16)
            tok_wA = p.tok("sgA")
            p.step("sync", ld3d(wmb_b, ATT_OFF["wmb"], 2048, 16, 2048),
                   waits=[tok_nn], sem="sgA2", inc=16)
            p.step("sync", ld3d(mem4_b, ATT_OFF["mem4"], 2048, 6, 2048),
                   waits=[tok_nn], sem="sgA2", inc=16)
            tok_wA2 = p.tok("sgA2")

            SLICE = WTOT // n_cores
            t_cv = p.step("gpsimd", lambda e: e.dma_start(out=ag_in, in_=wconv), sem="scv0", inc=16)
            t_conv = [t_cv]
            if n_cores > 1:
                t_ag = p.step("gpsimd", lambda e: e.collective_compute(
                    "AllGather", ALU.bypass, replica_groups=[list(range(n_cores))],
                    ins=[ag_in], outs=[ag_out]), waits=t_conv, sem="sag", inc=1)
            else:
                t_ag = p.step("gpsimd", lambda e: e.dma_start(out=ag_out, in_=ag_in),
                              waits=t_conv, sem="sag", inc=16)

            def emit_frontA(i):
                F = {}
                par = i % 2
                def ldxT(e, i=i):
                    return e.dma_start(out=xT_t[0][:],
                                       in_=AP(xT.tensor, i * 128,
                                              [[NTOK, 128], [128 * NTOK, 16], [1, 128]]))
                p.step("sync", ldxT, waits=[tA.get(("xT_free", i - 1))], sem="sx0", inc=16)
                t_x = p.tok("sx0")

                t_sq = p.step("scalar", lambda e: e.activation(out=scr[:], in_=xT_t[0][:], func=AF.Square),
                              waits=[t_x, tA.get(("scr_free", i - 1))], sem="S", inc=1)
                def em_stats(e):
                    for c in range(16):
                        e.matmul(psStat[0:1, 0:128], onc[:], xT_t[0][:, c * 128:(c + 1) * 128],
                                 start=(c == 0), stop=False)
                    inst = None
                    for c in range(16):
                        inst = e.matmul(psStat[0:1, 128:256], onc[:], scr[:, c * 128:(c + 1) * 128],
                                        start=False, stop=(c == 15))
                    return inst
                t_st = p.step("tensor", em_stats,
                              waits=[t_x, t_sq, t_ld0, tA.get(("psStat_free", i - 1))], sem="T", inc=1)
                t1a = p.step("vector", lambda e: e.tensor_scalar(
                    out=srow[0:1, 0:256], in0=psStat[0:1, 0:256],
                    scalar1=1.0 / HID, scalar2=None, op0=ALU.mult), waits=[t_st], sem="V", inc=1)
                t1 = p.step("vector", lambda e: e.tensor_tensor(
                    out=srow[0:1, 256:384], in0=srow[0:1, 0:128],
                    in1=srow[0:1, 0:128], op=ALU.mult), waits=[t1a], sem="V", inc=1)
                t2a = p.step("vector", lambda e: e.tensor_tensor(
                    out=srow[0:1, 128:256], in0=srow[0:1, 128:256],
                    in1=srow[0:1, 256:384], op=ALU.subtract), waits=[t1], sem="V", inc=1)
                t2 = p.step("vector", lambda e: e.tensor_scalar(
                    out=srow[0:1, 256:384], in0=srow[0:1, 128:256],
                    scalar1=EPS_LN, scalar2=None, op0=ALU.add), waits=[t2a], sem="V", inc=1)
                t3 = p.step("scalar", lambda e: e.activation(out=srow[0:1, 256:384],
                                                             in_=srow[0:1, 256:384], func=AF.Sqrt),
                            waits=[t2], sem="S", inc=1)
                t4a = p.step("vector", lambda e: e.reciprocal(
                    out=rowbuf[0:1, 0:128], in_=srow[0:1, 256:384]),
                    waits=[t3, tA.get(("rowbuf_used", i - 1)), tA.get(("rowbuf_used2", i - 1))],
                    sem="V", inc=1)
                t4 = p.step("vector", lambda e: e.scalar_tensor_tensor(
                    out=rowbuf[0:1, 128:256], in0=srow[0:1, 0:128], scalar=-1.0,
                    in1=rowbuf[0:1, 0:128], op0=ALU.mult, op1=ALU.mult),
                    waits=[t4a], sem="V", inc=1)
                t_rst = p.step("scalar", lambda e, i=i: e.dma_start(
                    out=AP(stats_dram.tensor, i * 256, [[256, 1], [1, 256]]), in_=rowbuf[0:1, :]),
                    waits=[t4], sem="sstr", inc=16)
                tA[("rows_stored", i)] = t_rst
                def em_stt(e, i=i):
                    e.matmul(psStat[0:128, 256:257], rowbuf[0:1, 0:128], id_f[0:1, 0:1],
                             is_transpose=True, start=True, stop=False)
                    e.matmul(psStat[0:128, 257:258], rowbuf[0:1, 128:256], id_f[0:1, 0:1],
                             is_transpose=True, start=False, stop=True)
                    return e.matmul(psBC[:, 0:256], onr[:],
                                    rowbuf[0:1, 0:256], start=True, stop=True)
                t_bc = p.step("tensor", em_stt, waits=[t4, tA.get(("psBC_free", i - 1))], sem="T", inc=1)
                tA[("rowbuf_used", i)] = t_rst
                tA[("rowbuf_used2", i)] = t_bc
                t_sc = p.step("vector", lambda e, i=i: e.tensor_copy(
                    out=stats_cols[:, i * 2:(i + 1) * 2], in_=psStat[0:128, 256:258]),
                    waits=[t_bc], sem="V", inc=1)

                t_ln1a = p.step("vector", lambda e: e.tensor_tensor(
                    out=scr[:], in0=xT_t[0][:],
                    in1=AP(psBC, 0, [[256, 128], [0, 16], [1, 128]]), op=ALU.mult),
                    waits=[t_bc, t_st], sem="V", inc=1)
                t_ln1 = p.step("vector", lambda e, par=par: e.tensor_tensor(
                    out=ln1xT[par][:], in0=scr[:],
                    in1=AP(psBC, 128, [[256, 128], [0, 16], [1, 128]]), op=ALU.add),
                    waits=[t_ln1a, tA.get(("ln1xT_used", i - 2))], sem="V", inc=1)
                tA[("xT_free", i)] = t_ln1a
                tA[("scr_free", i)] = t_ln1
                tA[("psBC_free", i)] = t_ln1

                F.update(t_ln1=t_ln1, t_ln1a=t_ln1a, t_st=t_st, t_sc=t_sc, par=par)
                return F

            def emit_frontA2(i, F):
                par = i % 2
                t_ln1 = F["t_ln1"]
                def em_kv(e, par=par):
                    inst = None
                    for c in range(16):
                        lh = ln1xT[par][:, c * 128:(c + 1) * 128]
                        e.matmul(psN[0][:], lh, wk_b[:, c * 512:(c + 1) * 512],
                                 start=(c == 0), stop=(c == 15))
                        inst = e.matmul(psN[1][:], lh, wv_b[:, c * 512:(c + 1) * 512],
                                        start=(c == 0), stop=(c == 15))
                    return inst
                t_kv = p.step("tensor", em_kv,
                              waits=[t_ln1, tok_wA, tA.get(("psN01_free2", i - 1))], sem="T", inc=1)
                def em_kvev(e, par=par):
                    e.tensor_tensor(out=kf[:], in0=psN[0][:], in1=biasA_s[:, 0:512], op=ALU.add)
                    return e.tensor_tensor(out=vf[par][:], in0=psN[1][:], in1=biasA_s[:, 512:1024], op=ALU.add)
                t_kvev = p.step("vector", em_kvev,
                                waits=[t_kv, tA.get(("kf_used", i - 1)), tA.get(("vf_used", i - 2))],
                                sem="V", inc=1)
                def em_relu(e):
                    e.activation(out=AP(x2k, 0, [[4 * 512, 128], [512, 4], [256, 2], [1, 128]]),
                                 in_=AP(kf, 0, [[512, 128], [128, 4], [0, 2], [1, 128]]),
                                 func=AF.Relu)
                    return e.activation(out=AP(x2k, 128, [[4 * 512, 128], [512, 4], [256, 2], [1, 128]]),
                                        in_=AP(kf, 0, [[512, 128], [128, 4], [0, 2], [1, 128]]),
                                        func=AF.Relu, scale=-1.0)
                t_relu = p.step("scalar", em_relu, waits=[t_kvev, tA.get(("x2k_used", i - 1))], sem="S", inc=1)
                tA[("kf_used", i)] = t_relu
                F.update(t_kvev=t_kvev, t_relu=t_relu)
                return F

            def emit_frontA2b(i, F):
                par = i % 2
                t_relu = F["t_relu"]
                def em_feat(e, par=par):
                    return e.tensor_tensor(
                        out=featk[par][:],
                        in0=AP(x2k, 256, [[4 * 512, 128], [512, 4], [0, 3], [1, 256]]),
                        in1=AP(x2k, 255, [[4 * 512, 128], [512, 4], [-1, 3], [1, 256]]),
                        op=ALU.mult)
                t_feat = p.step("vector", em_feat,
                                waits=[t_relu, tA.get(("featk_free", i - 2))], sem="V", inc=1)
                tA[("x2k_used", i)] = t_feat
                F.update(t_feat=t_feat)
                return F

            def emit_backA(i, F, Fnext):
                par = i % 2
                t_feat = F["t_feat"]

                def mk_tp(rnd):
                    def em_tp(e, rnd=rnd, par=par):
                        inst = None
                        for k in range(8):
                            ch = rnd * 8 + k
                            inst = e.matmul(psT[:, k * 128:(k + 1) * 128],
                                            featk[par][:, ch * 128:(ch + 1) * 128],
                                            id_b[:], is_transpose=True,
                                            start=(k == 0), stop=(k == 7))
                        return inst
                    return em_tp

                def mk_tpev(rnd):
                    def em_tev(e, rnd=rnd):
                        return e.copy(out=mkT[:, rnd * 1024:(rnd + 1) * 1024], in_=psT[:])
                    return em_tev

                t_tp0 = p.step("tensor", mk_tp(0), waits=[t_feat, t_ld0, tA.get(("mkT_used", i - 1))],
                               sem="T", inc=1)
                t_tpev0 = p.step("scalar", mk_tpev(0), waits=[t_tp0], sem="S", inc=1)
                t_mbev = []
                t_mbmm = []
                t_tpev = [t_tpev0]
                for half in range(2):
                    def em_mb(e, half=half, par=par):
                        inst = None
                        for c in range(16):
                            lh = ln1xT[par][:, c * 128:(c + 1) * 128]
                            for s in range(2):
                                sl = half * 2 + s
                                inst = e.matmul(psN[2 + s][:], lh,
                                                wmb_b[:, c * 2048 + sl * 512:c * 2048 + (sl + 1) * 512],
                                                start=(c == 0), stop=(c == 15))
                        return inst
                    wmb_w = [F["t_ln1"], tok_wA2]
                    if half == 0:
                        wmb_w.append(tA.get(("psN23_free", i - 1)))
                    else:
                        wmb_w.append(t_mbev[0])
                    t_mbmm.append(p.step("tensor", em_mb, waits=wmb_w, sem="T", inc=1))
                    rnd = half + 1
                    t_tp = p.step("tensor", mk_tp(rnd), waits=[t_feat, t_tpev[rnd - 1]], sem="T", inc=1)
                    t_tpev.append(p.step("scalar", mk_tpev(rnd), waits=[t_tp], sem="S", inc=1))
                    def em_mbev(e, half=half):
                        e.tensor_tensor(out=mbp[:, half * 1024:half * 1024 + 512],
                                        in0=psN[2][:], in1=biasA_s[:, 1024 + half * 1024:1536 + half * 1024],
                                        op=ALU.add)
                        return e.tensor_tensor(out=mbp[:, half * 1024 + 512:half * 1024 + 1024],
                                               in0=psN[3][:], in1=biasA_s[:, 1536 + half * 1024:2048 + half * 1024],
                                               op=ALU.add)
                    t_mbev.append(p.step("vector", em_mbev,
                                         waits=[t_mbmm[half], tA.get(("mbp_used", i - 1)) if half == 0 else None],
                                         sem="V", inc=1))
                tA[("ln1xT_used", i)] = t_mbmm[1]
                t_sig = p.step("scalar", lambda e: e.activation(out=mbp[:], in_=mbp[:], func=AF.Sigmoid),
                               waits=[t_mbev[1], tA.get(("mbs_used", i - 1))], sem="S", inc=1)
                tA[("mbp_used", i)] = t_sig

                t_n2 = [t_feat, tA.get(("sm_used", i - 1)), t_sig]
                for j in range(4):
                    t_n2 = [p.step("scalar", lambda e, j=j, par=par: e.activation(
                        out=x2k[:, 0:FD], in_=featk[par][:, j * FD:(j + 1) * FD],
                        func=AF.Square, accum_out=sm[:, j:j + 1]),
                        waits=t_n2, sem="S", inc=1)]
                t_n2 = t_n2[0]
                t_sqn = p.step("scalar", lambda e: e.activation(out=sm[:, 4:8], in_=sm[:, 0:4], func=AF.Sqrt),
                               waits=[t_n2], sem="S", inc=1)
                t_iva = p.step("vector", lambda e: e.tensor_scalar(
                    out=sm[:, 4:8], in0=sm[:, 4:8], scalar1=EPS_L2, scalar2=None, op0=ALU.max),
                    waits=[t_sqn], sem="V", inc=1)
                t_ivb = p.step("vector", lambda e: e.reciprocal(out=sm[:, 8:12], in_=sm[:, 4:8]),
                               waits=[t_iva], sem="V", inc=1)
                t_invn = p.step("vector", lambda e: e.tensor_copy(
                    out=sm[:, 16:32], in_=AP(sm, 8, [[80, 128], [1, 4], [0, 4]])),
                    waits=[t_ivb], sem="V", inc=1)

                t_numMM = []
                for j in range(4):
                    def em_num(e, j=j):
                        inst = None
                        for c in range(6):
                            lh = mkT[:, (j * 6 + c) * 128:(j * 6 + c) * 128 + 128]
                            e.matmul(psN[j][:], lh,
                                     mem4_b[:, c * 2048 + j * 512:c * 2048 + (j + 1) * 512],
                                     start=(c == 0), stop=(c == 5))
                            inst = e.matmul(psStat[:, 258 + j * 4:258 + (j + 1) * 4], lh,
                                            nn96b_b[:, c * 16 + j * 4:c * 16 + (j + 1) * 4],
                                            start=(j == 0 and c == 0), stop=(j == 3 and c == 5))
                        return inst
                    w = [t_tpev[min(2, (j * 6 + 5) // 8)], tok_wA2, tok_nn]
                    if Fnext:
                        w.append(Fnext["t_sc"])
                    if j < 2:
                        w.append(F["t_kvev"])
                        w.append(tA.get((f"dev_b{j}", i - 1)))
                    else:
                        w.append(t_mbev[1])
                    t_numMM.append(p.step("tensor", em_num, waits=w, sem="T", inc=1))
                tA[("mkT_used", i)] = t_numMM[3]

                t_dn1 = p.step("vector", lambda e: e.tensor_tensor(
                    out=sm[:, 32:48], in0=psStat[:, 258:274], in1=sm[:, 16:32], op=ALU.mult),
                    waits=[t_numMM[3], t_invn], sem="V", inc=1)
                t_dn2 = p.step("vector", lambda e: e.tensor_scalar(
                    out=sm[:, 32:48], in0=sm[:, 32:48], scalar1=EPS_AT, scalar2=None, op0=ALU.add),
                    waits=[t_dn1], sem="V", inc=1)
                t_dn3 = p.step("vector", lambda e: e.reciprocal(out=sm[:, 48:64], in_=sm[:, 32:48]),
                               waits=[t_dn2], sem="V", inc=1)
                t_den = p.step("vector", lambda e: e.tensor_tensor(
                    out=sm[:, 48:64], in0=sm[:, 48:64], in1=sm[:, 16:32], op=ALU.mult),
                    waits=[t_dn3], sem="V", inc=1)
                tA[("psStat_free", i + 1)] = t_dn1
                tc = p.step("vector", lambda e: e.tensor_tensor(
                    out=sm[:, 64:80], in0=sm[:, 16:32], in1=sm[:, 16:32], op=ALU.mult),
                    waits=[t_den], sem="V", inc=1)
                tc = p.step("vector", lambda e: e.tensor_tensor(
                    out=sm[:, 64:80], in0=sm[:, 64:80],
                    in1=AP(sm, 0, [[80, 128], [1, 4], [0, 4]]), op=ALU.mult),
                    waits=[tc], sem="V", inc=1)
                tc = p.step("vector", lambda e: e.reciprocal(out=sm[:, 64:80], in_=sm[:, 64:80]),
                            waits=[tc], sem="V", inc=1)
                tc = p.step("vector", lambda e: e.tensor_tensor(
                    out=sm[:, 64:80], in0=sm[:, 64:80], in1=sm[:, 32:48], op=ALU.mult),
                    waits=[tc], sem="V", inc=1)
                tc = p.step("vector", lambda e: e.tensor_scalar(
                    out=sm[:, 64:80], in0=sm[:, 64:80], scalar1=-1.0, scalar2=1.0,
                    op0=ALU.mult, op1=ALU.add), waits=[tc], sem="V", inc=1)
                tc = p.step("vector", lambda e: e.tensor_scalar(
                    out=sm[:, 64:80], in0=sm[:, 64:80], scalar1=0.0, scalar2=1.0,
                    op0=ALU.max, op1=ALU.min), waits=[tc], sem="V", inc=1)
                t_coef = p.step("vector", lambda e: e.tensor_tensor(
                    out=coefb[:], in0=sm[:, 64:80], in1=sm[:, 16:32], op=ALU.mult),
                    waits=[tc, tA.get(("coef_used", i - 1))], sem="V", inc=1)

                t_wmv = []
                for j in range(4):
                    tw = p.step("vector", lambda e, j=j: e.tensor_tensor(
                        out=pvt[:], in0=psN[j][:],
                        in1=AP(sm, 48 + j * 4, [[80, 128], [1, 4], [0, 128]]), op=ALU.mult),
                        waits=[t_den, t_sig, t_numMM[j], t_wmv[-1] if t_wmv else None],
                        sem="V", inc=1)
                    tw = p.step("vector", lambda e, j=j, par=par: e.tensor_tensor(
                        out=pvt[:], in0=pvt[:],
                        in1=AP(vf[par], j * 128, [[512, 128], [0, 4], [1, 128]]), op=ALU.subtract),
                        waits=[tw], sem="V", inc=1)
                    t_wmv.append(p.step("vector", lambda e, j=j, par=par: e.scalar_tensor_tensor(
                        out=wmv4[par][:, j * 512:(j + 1) * 512],
                        in0=pvt[:], scalar=sm[:, 8 + j:9 + j],
                        in1=mbp[:, j * 512:(j + 1) * 512],
                        op0=ALU.mult, op1=ALU.mult),
                        waits=[tw, tA.get(("wmv_free", i - 2))], sem="V", inc=1))
                tA[("vf_used", i)] = t_wmv[3]
                tA[("mbs_used", i)] = t_wmv[3]
                tA[("sm_used", i)] = t_wmv[3]
                tA[("psN01_free2", i)] = t_wmv[1]
                tA[("psN23_free", i)] = t_wmv[3]
                psN01_free = t_wmv[1]

                for j in range(4):
                    tA[("wmvprev", (i, j))] = t_wmv[j]
                tA[("back_refs", i)] = dict(t_wmv=t_wmv, t_coef=t_coef, t_numMM=t_numMM,
                                            psN01_free=psN01_free)

            def emit_tailA(i, Fnext=None):
                par = i % 2
                R = tA.pop(("back_refs", i))
                t_wmv, t_coef, t_numMM = R["t_wmv"], R["t_coef"], R["t_numMM"]
                psN01_free = R["psN01_free"]
                def em_dn(e, i=i, par=par):
                    inst = None
                    for j in range(4):
                        for c in range(6):
                            lh = featk[par][:, (j * 6 + c) * 128:(j * 6 + c) * 128 + 128]
                            inst = e.matmul(psDN[:, c * 16 + j * 4:c * 16 + (j + 1) * 4], lh,
                                            coefb[:, j * 4:(j + 1) * 4],
                                            start=(i == 0 and j == 0 and c == 0),
                                            stop=(i == NT - 1 and j == 3 and c == 5))
                    return inst
                t_dnm = p.step("tensor", em_dn, waits=[t_coef, t_numMM[3]], sem="T", inc=1)
                tA[("coef_used", i)] = t_dnm
                tA[("featk_free", i)] = t_dnm

                if i % SUP == SUP - 1:
                    si = i // SUP
                    t_delta_last = None
                    gi = 0
                    for j in range(4):
                        for rnd in range(6):
                            def em_dl(e, j=j, rnd=rnd, i=i):
                                inst = None
                                c = rnd
                                pb = psN[rnd % 2]
                                for t in range(SUP):
                                    tt = i - SUP + 1 + t
                                    inst = e.matmul(pb[:],
                                                    featk[tt % 2][:, (j * 6 + c) * 128:(j * 6 + c) * 128 + 128],
                                                    wmv4[tt % 2][:, j * 512:(j + 1) * 512],
                                                    start=(t == 0), stop=(t == SUP - 1))
                                return inst
                            w = [t_wmv[j], tA.get(("wmvprev", (i - 1, j)))]
                            if gi == 0:
                                w.append(psN01_free)
                            if gi < 2 and Fnext is not None:
                                w.append(Fnext["t_kvev"])
                            w.append(tA.get(("dev", (si, gi - 2))))
                            t_dl = p.step("tensor", em_dl, waits=w, sem="T", inc=1)
                            def em_dev(e, rnd=rnd, gi=gi):
                                return e.tensor_copy(out=dstg[gi % 2][:], in_=psN[rnd % 2][:])
                            t_dev = p.step("vector", em_dev,
                                           waits=[t_dl, tA.get(("dacc", (si, gi - 2))) or
                                                  tA.get(("dacc", (si - 1, 22 + gi)))],
                                           sem="V", inc=1)
                            tA[("dev", (si, gi))] = t_dev
                            tA[(f"dev_b{rnd % 2}", i)] = t_dev
                            def em_acc(e, j=j, rnd=rnd, si=si, gi=gi):
                                return e.dma_start(
                                    out=AP(ar_in.tensor, rnd * 2048 + j * 512,
                                           [[AR_COLS, 128], [1, 512]]),
                                    in_=dstg[gi % 2][:],
                                    accum_op=(ALU.bypass if si == 0 else ALU.add))
                            wacc = [t_dev]
                            if si >= 1:
                                wacc.append(tA[("acc", (si - 1, j, rnd))])
                            t_acc = p.step("gpsimd", em_acc, waits=wacc,
                                           sem="sda" if gi % 2 == 0 else "sda1", inc=16)
                            tA[("acc", (si, j, rnd))] = t_acc
                            tA[("dacc", (si, gi))] = t_acc
                            gi += 1
                        t_delta_last = tA[("dev", (si, gi - 1))]
                        tA[("acc_last", si)] = t_acc
                    tA[("psN01_free2", i)] = t_delta_last
                    tA[("featk_free", i)] = t_delta_last
                    tA[("featk_free", i - 1)] = t_delta_last
                    tA[("wmv_free", i)] = t_delta_last
                    tA[("wmv_free", i - 1)] = t_delta_last


            FA = {}
            for i in range(NT + 1):
                if i < NT:
                    FA[i] = emit_frontA(i)
                if i >= 1:
                    emit_backA(i - 1, FA[i - 1], FA.get(i))
                if i < NT:
                    FA[i].update(emit_frontA2(i, FA[i]))
                if i >= 1:
                    emit_tailA(i - 1, FA.get(i))
                if i < NT:
                    FA[i].update(emit_frontA2b(i, FA[i]))
                if i >= 1:
                    FA.pop(i - 1, None)

            t_dnev = p.step("vector", lambda e: e.tensor_copy(out=dstg[0][:, 0:96], in_=psDN[:]),
                            waits=[p.tok("T"), tA[("acc", (NSUP - 1, 3, 5))], tA[("acc", (NSUP - 1, 3, 4))]], sem="V", inc=1)
            t_dnst = p.step("gpsimd", lambda e: e.dma_start(
                out=AP(ar_in.tensor, 12288, [[AR_COLS, 128], [1, 96]]), in_=dstg[0][:, 0:96]),
                waits=[t_dnev], sem="sda", inc=16)
            if n_cores > 1:
                t_ar = p.step("gpsimd", lambda e: e.collective_compute(
                    "AllReduce", ALU.add, replica_groups=[list(range(n_cores))],
                    ins=[ar_in], outs=[ar_out]), waits=[t_dnst], sem="sar", inc=1)
            else:
                t_ar = p.step("gpsimd", lambda e: e.dma_start(out=ar_out, in_=ar_in),
                              waits=[t_dnst], sem="sar", inc=16)
            if debug_outs:
                t_dbgar = p.step("gpsimd", lambda e: e.dma_start(out=dbg_ar, in_=ar_out),
                                 waits=[t_ar], sem="sda", inc=16)

        t_rows_all = p.tok("sstr")
        t_bar1 = barrier()

        esQ = ExitStack()
        tQ = {}
        with esQ:
            wq_b = alloc("wq_b", (128, 16 * 2048), BF16, esQ)
            numa_rhs = alloc("numa_rhs", (128, NH * FC * 129), BF16, esQ)
            nn96_s = alloc("nn96_s", (128, 96), F32, esQ)
            arn = alloc("arn", (128, 96), F32, esQ)
            biasQ_s = alloc("biasQ_s", (128, 2048), BF16, esQ)
            xT_q = [alloc("xTq_0", (128, HID), F32, esQ)]
            ln1xTq = alloc("ln1xTq", (128, HID), BF16, esQ)
            qf = [alloc(f"qf_{k}", (128, 2048), F32, esQ) for k in range(2)]
            x2q = alloc("x2q", (128, 16 * 512), BF16, esQ)
            featq = alloc("featq", (128, NH * FD), BF16, esQ)
            mqT = alloc("mqT", (128, NH * FD), BF16, esQ)
            smq = alloc("smq", (128, 128), F32, esQ)
            mh_b = [alloc("mh_0", (128, 2048), F32, esQ)]
            arld = alloc("arld", (128, 512), F32, esQ)
            mmld = alloc("mmld", (128, 512), F32, esQ)


            psBCq = esQ.enter_context(nc.psum_tensor("psBCq", [128, 256], F32))
            psQb = [esQ.enter_context(nc.psum_tensor(f"psQ{k}", [128, 512], F32)) for k in range(2)]
            psTq = esQ.enter_context(nc.psum_tensor("psTq", [128, 1024], BF16))
            psAb = [esQ.enter_context(nc.psum_tensor(f"psA{k}", [128, 512], F32)) for k in range(4)]
            banks = psAb + psQb

            p.step("sync", ld3d(wq_b, ATT_OFF["wq"], 2048, 16, 2048), waits=t_bar1 + [tok_nn], sem="sgQ", inc=16)
            p.step("sync", lambda e: e.dma_start(out=biasQ_s[:], in_=biasQ), waits=t_bar1, sem="sld", inc=16)
            tok_wQ = p.tok("sgQ")
            tok_bQ = p.tok("sld")
            t_arw = [p.tok("sar")]
            p.step("sync", lambda e: e.dma_start(out=nn96_s[:], in_=nn96), waits=t_bar1, sem="sarld", inc=16)
            cmb_done = {}
            for c2 in range(24):
                c, h4 = divmod(c2, 4)
                def em_arl(e, c=c, h4=h4):
                    return e.dma_start(out=arld[:],
                                       in_=AP(ar_out.tensor, c * 2048 + h4 * 512,
                                              [[AR_COLS, 128], [1, 512]]))
                def em_mml(e, c=c, h4=h4):
                    return e.dma_start(out=mmld[:],
                                       in_=AP(mem4.tensor, c * 128 * 2048 + h4 * 512,
                                              [[2048, 128], [1, 512]]))
                w = t_arw + t_bar1 + ([cmb_done[c2 - 1]] if c2 >= 1 else [])
                tl1 = p.step("sync", em_arl, waits=w, sem="sarld", inc=16)
                tl2 = p.step("sync", em_mml, waits=[tl1], sem="sarld", inc=16)
                def em_cmb(e, c=c, h4=h4):
                    return e.tensor_tensor(
                        out=AP(numa_rhs, h4 * 4 * FC * 129 + c * 129,
                               [[NH * FC * 129, 128], [FC * 129, 4], [1, 128]]),
                        in0=AP(mmld, 0, [[512, 128], [128, 4], [1, 128]]),
                        in1=AP(arld, 0, [[512, 128], [128, 4], [1, 128]]),
                        op=ALU.subtract)
                cmb_done[c2] = p.step("vector", em_cmb, waits=[tl2] + t_bar1, sem="V", inc=1)
            cmb_done[11] = cmb_done[23]
            t_arn = p.step("sync", lambda e: e.dma_start(
                out=arn[:], in_=AP(ar_out.tensor, 12288, [[AR_COLS, 128], [1, 96]])),
                waits=t_arw + t_bar1 + [p.tok("sarld")], sem="sarn", inc=16)
            def em_nn(e):
                return e.tensor_tensor(
                    out=AP(numa_rhs, 128, [[NH * FC * 129, 128], [FC * 129, 16], [129, 6]]),
                    in0=AP(nn96_s, 0, [[96, 128], [6, 16], [1, 6]]),
                    in1=AP(arn, 0, [[96, 128], [1, 16], [16, 6]]),
                    op=ALU.add)
            t_nn = p.step("vector", em_nn, waits=[t_arn, cmb_done[23]], sem="V", inc=1)
            t_rhs_ready = [t_nn, cmb_done[23]]

            def emit_frontQ(i):
                F = {}
                par = i % 2
                def ldxT(e, i=i):
                    return e.dma_start(out=xT_q[0][:],
                                       in_=AP(xT.tensor, i * 128,
                                              [[NTOK, 128], [128 * NTOK, 16], [1, 128]]))
                p.step("sync", ldxT, waits=[tQ.get(("xT_free", i - 1)), t_bar1[0]], sem="sx0", inc=16)
                t_x = p.tok("sx0")
                def ldrw(e, i=i, par=par):
                    return e.dma_start(out=rowb2[par][0:1, :],
                                       in_=AP(stats_dram.tensor, i * 256, [[256, 1], [1, 256]]))
                p.step("sync", ldrw, waits=[t_rows_all, tQ.get(("rowb_used", i - 2))],
                       sem=f"srw{par}", inc=16)
                t_rw = p.tok(f"srw{par}")
                def em_bc(e, par=par):
                    return e.matmul(psBCq[:, 0:256], onr[:],
                                    rowb2[par][0:1, 0:256], start=True, stop=True)
                t_bc = p.step("tensor", em_bc, waits=[tQ.get(("psBC_free", i - 1)), t_rw] + t_bar1,
                              sem="T", inc=1)
                tQ[("rowb_used", i)] = t_bc
                t_ln1a = p.step("vector", lambda e, par=par: e.tensor_tensor(
                    out=qf[par][:], in0=xT_q[0][:],
                    in1=AP(psBCq, 0, [[256, 128], [0, 16], [1, 128]]), op=ALU.mult),
                    waits=[t_bc, t_x, tQ.get(("qf_used", i - 2))], sem="V", inc=1)
                t_ln1 = p.step("vector", lambda e, par=par: e.tensor_tensor(
                    out=ln1xTq[:], in0=qf[par][:],
                    in1=AP(psBCq, 128, [[256, 128], [0, 16], [1, 128]]), op=ALU.add),
                    waits=[t_ln1a, tQ.get(("ln1_used", i - 1))], sem="V", inc=1)
                tQ[("psBC_free", i)] = t_ln1
                tQ[("xT_free", i)] = t_ln1a

                F.update(t_ln1=t_ln1, t_x=t_x)
                return F

            def emit_frontQ2(i, F):
                par = i % 2
                t_ln1 = F["t_ln1"]
                t_qev = []
                t_qmm = []
                for half in range(2):
                    def em_q(e, half=half):
                        inst = None
                        for c in range(16):
                            lh = ln1xTq[:, c * 128:(c + 1) * 128]
                            for s in range(2):
                                sl = half * 2 + s
                                inst = e.matmul(psQb[s][:], lh,
                                                wq_b[:, c * 2048 + sl * 512:c * 2048 + (sl + 1) * 512],
                                                start=(c == 0), stop=(c == 15))
                        return inst
                    w = [t_ln1, tok_wQ]
                    if half == 0:
                        w.append(tQ.get(("psQ_free", i - 1)))
                    else:
                        w.append(t_qev[0])
                    t_qmm.append(p.step("tensor", em_q, waits=w, sem="T", inc=1))
                    def em_qev(e, half=half, par=par):
                        e.tensor_tensor(out=qf[par][:, half * 1024:half * 1024 + 512], in0=psQb[0][:],
                                        in1=biasQ_s[:, half * 1024:half * 1024 + 512], op=ALU.add)
                        return e.tensor_tensor(out=qf[par][:, half * 1024 + 512:half * 1024 + 1024],
                                               in0=psQb[1][:],
                                               in1=biasQ_s[:, half * 1024 + 512:half * 1024 + 1024], op=ALU.add)
                    t_qev.append(p.step("vector", em_qev,
                                        waits=[t_qmm[half], tok_bQ], sem="V", inc=1))
                tQ[("ln1_used", i)] = t_qmm[1]
                tQ[("psQ_free", i)] = t_qev[1]
                return dict(t_qev1=t_qev[1])

            def emit_backQ(i, F):
                par = i % 2
                def em_relu(e, par=par):
                    e.activation(out=AP(x2q, 0, [[16 * 512, 128], [512, 16], [256, 2], [1, 128]]),
                                 in_=AP(qf[par], 0, [[2048, 128], [128, 16], [0, 2], [1, 128]]),
                                 func=AF.Relu)
                    return e.activation(out=AP(x2q, 128, [[16 * 512, 128], [512, 16], [256, 2], [1, 128]]),
                                        in_=AP(qf[par], 0, [[2048, 128], [128, 16], [0, 2], [1, 128]]),
                                        func=AF.Relu, scale=-1.0)
                t_relu = p.step("scalar", em_relu, waits=[F["t_qev1"], tQ.get(("x2q_used", i - 1))] + t_bar1,
                                sem="S", inc=1)
                tQ[("qf_used", i)] = t_relu
                def em_feat(e):
                    return e.tensor_tensor(
                        out=featq[:],
                        in0=AP(x2q, 256, [[16 * 512, 128], [512, 16], [0, 3], [1, 256]]),
                        in1=AP(x2q, 255, [[16 * 512, 128], [512, 16], [-1, 3], [1, 256]]),
                        op=ALU.mult)
                t_feat = p.step("vector", em_feat, waits=[t_relu, tQ.get(("featq_used", i - 1))], sem="V", inc=1)
                tQ[("x2q_used", i)] = t_feat
                t_n2 = [t_feat, tQ.get(("smq_used", i - 1))]
                for h in range(NH):
                    t_n2 = [p.step("scalar", lambda e, h=h: e.activation(
                        out=x2q[:, 0:FD], in_=featq[:, h * FD:(h + 1) * FD],
                        func=AF.Square, accum_out=smq[:, h:h + 1]),
                        waits=t_n2, sem="S", inc=1)]
                t_n2 = t_n2[0]
                t_sqn = p.step("scalar", lambda e: e.activation(out=smq[:, 16:32], in_=smq[:, 0:16], func=AF.Sqrt),
                               waits=[t_n2], sem="S", inc=1)
                t_iv = p.step("vector", lambda e: e.tensor_scalar(
                    out=smq[:, 16:32], in0=smq[:, 16:32], scalar1=EPS_L2, scalar2=None, op0=ALU.max),
                    waits=[t_sqn], sem="V", inc=1)
                t_invn = p.step("vector", lambda e: e.reciprocal(out=smq[:, 32:48], in_=smq[:, 16:32]),
                                waits=[t_iv], sem="V", inc=1)

                def mk_tpq(rnd):
                    def em_tp(e, rnd=rnd):
                        inst = None
                        for k in range(8):
                            ch = rnd * 8 + k
                            inst = e.matmul(psTq[:, k * 128:(k + 1) * 128],
                                            featq[:, ch * 128:(ch + 1) * 128],
                                            id_b[:], is_transpose=True,
                                            start=(k == 0), stop=(k == 7))
                        return inst
                    return em_tp

                def mk_tpqev(rnd):
                    def em_tev(e, rnd=rnd):
                        if rnd % 2:
                            return e.tensor_copy(out=mqT[:, rnd * 1024:(rnd + 1) * 1024], in_=psTq[:])
                        return e.copy(out=mqT[:, rnd * 1024:(rnd + 1) * 1024], in_=psTq[:])
                    return em_tev

                def mk_na(heads, b3):
                    def em_na(e, heads=heads, b3=b3):
                        pb = psAb[b3]
                        inst = None
                        first = True
                        for hh, h in enumerate(heads):
                            for c in range(6):
                                lh = mqT[:, (h * 6 + c) * 128:(h * 6 + c) * 128 + 128]
                                inst = e.matmul(pb[:, hh * 129:hh * 129 + 129], lh,
                                                numa_rhs[:, (h * 6 + c) * 129:(h * 6 + c) * 129 + 129],
                                                start=first, stop=(hh == len(heads) - 1 and c == 5))
                                first = False
                        return inst
                    return em_na

                tpev = {}
                prev = tQ.get(("mqT_used", i - 1))
                t_raw = {}
                t_na = []
                t_dgs = []
                na_blocks = [([0, 1, 2], 0), ([3, 4, 5], 1), ([6, 7], 2),
                             ([8, 9, 10], 0), ([11, 12, 13], 1), ([14, 15], 2)]
                next_na = 0
                for rnd in range(12):
                    t_tp = p.step("tensor", mk_tpq(rnd), waits=[t_feat, tpev.get(rnd - 1, prev)],
                                  sem="T", inc=1)
                    tpev[rnd] = p.step("vector" if rnd % 2 else "scalar", mk_tpqev(rnd),
                                       waits=[t_tp], sem="V" if rnd % 2 else "S", inc=1)
                    while next_na < 6 and (na_blocks[next_na][0][-1] * 6 + 5) // 8 <= rnd:
                        bi = next_na
                        heads, b3 = na_blocks[bi]
                        w = [tpev[rnd]] + t_rhs_ready
                        if bi < 3:
                            w.append(tQ.get(("na_used", (i - 1, bi % 3))))
                        else:
                            w.append(t_raw[bi - 3])
                            w.append(t_dgs[bi - 3])
                        t_na.append(p.step("tensor", mk_na(heads, b3), waits=w, sem="T", inc=1))
                        def em_raw(e, heads=heads, b3=b3):
                            return e.copy(
                                out=AP(mh_b[0], heads[0] * 128, [[2048, 128], [128, len(heads)], [1, 128]]),
                                in_=AP(psAb[b3], 0, [[512, 128], [129, len(heads)], [1, 128]]))
                        t_raw[bi] = p.step("scalar", em_raw,
                                           waits=[t_na[bi], tQ.get(("mh_free", i - 1)) if bi == 0 else None],
                                           sem="S", inc=1)
                        tQ[("na_used", (i, bi % 3))] = t_raw[bi]
                        def em_dgb(e, heads=heads, b3=b3):
                            return e.tensor_copy(out=smq[:, 48 + heads[0]:48 + heads[0] + len(heads)],
                                                 in_=AP(psAb[b3], 128, [[512, 128], [129, len(heads)]]))
                        t_dgs.append(p.step("vector", em_dgb,
                                            waits=[t_na[bi], tQ.get(("smq_used", i - 1)) if bi == 0 else None],
                                            sem="V", inc=1))
                        next_na += 1
                t_s = p.step("vector", lambda e: e.tensor_tensor(
                    out=smq[:, 64:80], in0=smq[:, 48:64], in1=smq[:, 32:48], op=ALU.mult),
                    waits=t_dgs + [t_invn], sem="V", inc=1)
                t_s = p.step("vector", lambda e: e.tensor_scalar(
                    out=smq[:, 64:80], in0=smq[:, 64:80], scalar1=EPS_AT, scalar2=None, op0=ALU.add),
                    waits=[t_s], sem="V", inc=1)
                t_s = p.step("vector", lambda e: e.reciprocal(out=smq[:, 80:96], in_=smq[:, 64:80]),
                             waits=[t_s], sem="V", inc=1)
                t_s = p.step("vector", lambda e: e.tensor_tensor(
                    out=smq[:, 96:112], in0=smq[:, 80:96], in1=smq[:, 32:48], op=ALU.mult),
                    waits=[t_s], sem="V", inc=1)
                def em_mh(e):
                    inst = None
                    for h in range(NH):
                        inst = e.tensor_scalar(out=mh_b[0][:, h * 128:(h + 1) * 128],
                                               in0=mh_b[0][:, h * 128:(h + 1) * 128],
                                               scalar1=smq[:, 96 + h:97 + h], scalar2=None,
                                               op0=ALU.mult)
                    return inst
                t_mh = p.step("vector", em_mh, waits=[t_s, t_raw[5]], sem="V", inc=1)
                tQ[("mqT_used", i)] = t_na[5]
                tQ[("smq_used", i)] = t_mh
                def em_mhst(e, i=i):
                    return e.dma_start(out=AP(mh_sp.tensor, i * 128 * 2048, [[2048, 128], [1, 2048]]),
                                       in_=mh_b[0][:])
                t_mhs = p.step("scalar", em_mhst, waits=[t_mh], sem="sstm", inc=16)
                tQ[("mh_free", i)] = t_mhs
                tQ[("mh_done", i)] = t_mhs

            FQ = {}
            for i in range(NT + 1):
                if i < NT:
                    FQ[i] = emit_frontQ(i)
                if i >= 1:
                    emit_backQ(i - 1, FQ[i - 1])
                if i < NT:
                    FQ[i].update(emit_frontQ2(i, FQ[i]))
                if i >= 1:
                    FQ.pop(i - 1, None)

        t_mh_all = p.tok("sstm")
        t_bar2 = barrier()

        esG = ExitStack()
        tG = {}
        with esG:
            wg_b = alloc("wg_b", (128, 16 * 2048), BF16, esG)
            biasG_s = alloc("biasG_s", (128, 2048), BF16, esG)
            xT_g = [alloc(f"xTg_{k}", (128, HID), F32, esG) for k in range(2)]
            x_g = [alloc(f"xg_{k}", (128, HID), F32, esG) for k in range(2)]
            mh_g = [alloc(f"mhg_{k}", (128, HID), F32, esG) for k in range(2)]
            ln1tg = alloc("ln1tg", (128, HID), F32, esG)
            ln1xTg = alloc("ln1xTg", (128, HID), BF16, esG)
            ln1f = alloc("ln1f", (128, HID), F32, esG)
            gsig = alloc("gsig", (128, HID), F32, esG)
            gpre = [alloc(f"gpre_{k}", (128, HID), F32, esG) for k in range(2)]
            att = alloc("att", (128, HID), F32, esG)
            attsq = alloc("attsq", (128, HID), F32, esG)
            y1b = alloc("y1b", (128, HID), F32, esG)
            ln2b = alloc("ln2b", (128, HID), BF16, esG)
            smg = alloc("smg", (128, 16), F32, esG)

            psBCg = esG.enter_context(nc.psum_tensor("psBCg", [128, 256], F32))
            psG = [esG.enter_context(nc.psum_tensor(f"psG{k}", [128, 512], F32)) for k in range(2)]

            p.step("sync", ld3d(wg_b, ATT_OFF["wg"], 2048, 16, 2048), waits=t_bar2 + [tok_nn], sem="sgG", inc=16)
            tok_wG = p.tok("sgG")
            p.step("sync", lambda e: e.dma_start(out=biasG_s[:], in_=biasG), waits=t_bar2, sem="sld", inc=16)
            tok_bG = p.tok("sld")

            def emit_frontG(i):
                F = {}
                par = i % 2
                def ldxT(e, i=i, par=par):
                    return e.dma_start(out=xT_g[par][:],
                                       in_=AP(xT.tensor, i * 128,
                                              [[NTOK, 128], [128 * NTOK, 16], [1, 128]]))
                p.step("sync", ldxT, waits=[tG.get(("xT_free", i - 2)), t_bar2[0]], sem=f"sx{par}", inc=16)
                t_xT = p.tok(f"sx{par}")
                def ldx(e, i=i, par=par):
                    return e.dma_start(out=x_g[par][:],
                                       in_=AP(x_in.tensor, i * 128 * 2048, [[2048, 128], [1, 2048]]))
                p.step("sync", ldx, waits=[tG.get(("x_free", i - 2))], sem=f"sxx{par}", inc=16)
                t_xn = p.tok(f"sxx{par}")
                def ldmh(e, i=i, par=par):
                    return e.dma_start(out=mh_g[par][:],
                                       in_=AP(mh_sp.tensor, i * 128 * 2048, [[2048, 128], [1, 2048]]))
                p.step("sync", ldmh, waits=[t_mh_all, tG.get(("mhg_free", i - 2)),
                                            tG.get(("mhg_freeD", i - 2))],
                       sem=f"smh{par}", inc=16)
                t_mh = p.tok(f"smh{par}")
                def ldrw(e, i=i, par=par):
                    return e.dma_start(out=rowb2[par][0:1, :],
                                       in_=AP(stats_dram.tensor, i * 256, [[256, 1], [1, 256]]))
                p.step("sync", ldrw, waits=[tG.get(("rowb_used", i - 2)), t_rows_all, t_bar2[1]],
                       sem=f"srw{par}", inc=16)
                t_rw = p.tok(f"srw{par}")
                def em_bc(e, par=par):
                    return e.matmul(psBCg[:, 0:256], onr[:],
                                    rowb2[par][0:1, 0:256], start=True, stop=True)
                t_bc = p.step("tensor", em_bc, waits=[tG.get(("psBC_free", i - 1)), t_rw] + t_bar2,
                              sem="T", inc=1)
                tG[("rowb_used", i)] = t_bc
                t_ln1a = p.step("vector", lambda e, par=par: e.tensor_tensor(
                    out=ln1tg[:], in0=xT_g[par][:],
                    in1=AP(psBCg, 0, [[256, 128], [0, 16], [1, 128]]), op=ALU.mult),
                    waits=[t_bc, t_xT, tG.get(("ln1tg_used", i - 1))], sem="V", inc=1)
                t_ln1 = p.step("vector", lambda e: e.tensor_tensor(
                    out=ln1xTg[:], in0=ln1tg[:],
                    in1=AP(psBCg, 128, [[256, 128], [0, 16], [1, 128]]), op=ALU.add),
                    waits=[t_ln1a, tG.get(("ln1_used", i - 1))], sem="V", inc=1)
                tG[("psBC_free", i)] = t_ln1
                tG[("xT_free", i)] = t_ln1a
                tG[("ln1tg_used", i)] = t_ln1

                F.update(t_ln1=t_ln1, t_mh=t_mh, t_xn=t_xn)
                return F

            def emit_frontG2(i, F):
                par = i % 2
                t_ln1 = F["t_ln1"]
                t_gev = []
                t_gmm = []
                for half in range(2):
                    def em_g(e, half=half):
                        inst = None
                        for c in range(16):
                            lh = ln1xTg[:, c * 128:(c + 1) * 128]
                            for s in range(2):
                                sl = half * 2 + s
                                inst = e.matmul(psG[s][:], lh,
                                                wg_b[:, c * 2048 + sl * 512:c * 2048 + (sl + 1) * 512],
                                                start=(c == 0), stop=(c == 15))
                        return inst
                    w = [t_ln1, tok_wG]
                    if half == 0:
                        w.append(tG.get(("psG_free", i - 1)))
                    else:
                        w.append(t_gev[0])
                    t_gmm.append(p.step("tensor", em_g, waits=w, sem="T", inc=1))
                    def em_gev(e, half=half, par=par):
                        e.tensor_tensor(out=gpre[par][:, half * 1024:half * 1024 + 512], in0=psG[0][:],
                                        in1=biasG_s[:, half * 1024:half * 1024 + 512], op=ALU.add)
                        return e.tensor_tensor(out=gpre[par][:, half * 1024 + 512:half * 1024 + 1024],
                                               in0=psG[1][:],
                                               in1=biasG_s[:, half * 1024 + 512:half * 1024 + 1024], op=ALU.add)
                    t_gev.append(p.step("vector", em_gev,
                                        waits=[t_gmm[half], tok_bG,
                                               tG.get(("gpre_used", i - 2)) if half == 0 else None],
                                        sem="V", inc=1))
                tG[("ln1_used", i)] = t_gmm[1]
                tG[("psG_free", i)] = t_gev[1]
                return dict(t_gev1=t_gev[1])

            def emit_backG(i, F):
                par = i % 2
                t_sig = p.step("scalar", lambda e, par=par: e.activation(
                    out=gsig[:], in_=gpre[par][:], func=AF.Sigmoid),
                    waits=[F["t_gev1"], tG.get(("gsig_used", i - 1))], sem="S", inc=1)
                tG[("gpre_used", i)] = t_sig
                def em_ln1f(e, par=par, i=i):
                    return e.tensor_scalar(out=ln1f[:], in0=x_g[par][:],
                                           scalar1=stats_cols[:, i * 2:i * 2 + 1],
                                           scalar2=stats_cols[:, i * 2 + 1:i * 2 + 2],
                                           op0=ALU.mult, op1=ALU.add)
                t_ln1f = p.step("vector", em_ln1f, waits=[F["t_xn"], tG.get(("ln1f_used", i - 1))],
                                sem="V", inc=1)
                t_gm = p.step("vector", lambda e, par=par: e.tensor_tensor(
                    out=gsig[:], in0=gsig[:], in1=mh_g[par][:], op=ALU.mult),
                    waits=[t_sig, F["t_mh"]], sem="V", inc=1)
                t_att = p.step("vector", lambda e: e.tensor_tensor(
                    out=att[:], in0=ln1f[:], in1=gsig[:], op=ALU.add),
                    waits=[t_gm, t_ln1f, tG.get(("att_used", i - 1))], sem="V", inc=1)
                tG[("gsig_used", i)] = t_att
                tG[("ln1f_used", i)] = t_att
                tG[("mhg_free", i)] = t_gm
                if debug_outs:
                    tG[("mhg_freeD", i)] = p.step("scalar", lambda e, i=i, par=par: e.dma_start(
                        out=AP(dbg_mh.tensor, i * 128 * 2048, [[2048, 128], [1, 2048]]), in_=mh_g[par][:]),
                        waits=[t_gm], sem=f"sdbg{par}", inc=16)
                t_asq = p.step("scalar", lambda e: e.activation(out=attsq[:], in_=att[:], func=AF.Square,
                                                                accum_out=smg[:, 1:2]),
                               waits=[t_att, tG.get(("smg_used", i - 1))], sem="S", inc=1)
                t_as = p.step("vector", lambda e: e.reduce_sum(out=smg[:, 0:1], in_=att[:], axis=AX.X),
                              waits=[t_att], sem="V", inc=1)
                t_av = p.step("vector", lambda e: e.tensor_scalar(
                    out=smg[:, 2:4], in0=smg[:, 0:2], scalar1=1.0 / HID, scalar2=None, op0=ALU.mult),
                    waits=[t_as, t_asq], sem="V", inc=1)
                t_av = p.step("vector", lambda e: e.tensor_tensor(
                    out=smg[:, 4:5], in0=smg[:, 2:3], in1=smg[:, 2:3], op=ALU.mult),
                    waits=[t_av], sem="V", inc=1)
                t_av = p.step("vector", lambda e: e.tensor_tensor(
                    out=smg[:, 5:6], in0=smg[:, 3:4], in1=smg[:, 4:5], op=ALU.subtract),
                    waits=[t_av], sem="V", inc=1)
                t_av = p.step("vector", lambda e: e.tensor_scalar(
                    out=smg[:, 5:6], in0=smg[:, 5:6], scalar1=EPS_LN, scalar2=None, op0=ALU.add),
                    waits=[t_av], sem="V", inc=1)
                t_avs = p.step("scalar", lambda e: e.activation(out=smg[:, 6:7], in_=smg[:, 5:6], func=AF.Sqrt),
                               waits=[t_av], sem="S", inc=1)
                t_avr = p.step("vector", lambda e: e.reciprocal(out=smg[:, 7:8], in_=smg[:, 6:7]),
                               waits=[t_avs], sem="V", inc=1)
                t_y1a = p.step("vector", lambda e: e.tensor_scalar(
                    out=att[:], in0=att[:], scalar1=smg[:, 2:3], scalar2=smg[:, 7:8],
                    op0=ALU.subtract, op1=ALU.mult), waits=[t_avr], sem="V", inc=1)
                t_y1 = p.step("vector", lambda e, par=par: e.tensor_tensor(
                    out=y1b[:], in0=x_g[par][:], in1=att[:], op=ALU.add),
                    waits=[t_y1a, tG.get(("y1_used", i - 1)), tG.get(("y1_usedD", i - 1))],
                    sem="V", inc=1)
                tG[("att_used", i)] = t_y1
                tG[("x_free", i)] = t_y1
                t_y1sq = p.step("scalar", lambda e: e.activation(out=attsq[:], in_=y1b[:], func=AF.Square,
                                                                 accum_out=smg[:, 9:10]),
                                waits=[t_y1], sem="S", inc=1)
                tG[("smg_used", i)] = t_y1sq
                t_y1s = p.step("vector", lambda e: e.reduce_sum(out=smg[:, 8:9], in_=y1b[:], axis=AX.X),
                               waits=[t_y1], sem="V", inc=1)
                t_v2 = p.step("vector", lambda e: e.tensor_scalar(
                    out=smg[:, 10:12], in0=smg[:, 8:10], scalar1=1.0 / HID, scalar2=None, op0=ALU.mult),
                    waits=[t_y1s, t_y1sq], sem="V", inc=1)
                t_v2 = p.step("vector", lambda e: e.tensor_tensor(
                    out=smg[:, 12:13], in0=smg[:, 10:11], in1=smg[:, 10:11], op=ALU.mult),
                    waits=[t_v2], sem="V", inc=1)
                t_v2 = p.step("vector", lambda e: e.tensor_tensor(
                    out=smg[:, 13:14], in0=smg[:, 11:12], in1=smg[:, 12:13], op=ALU.subtract),
                    waits=[t_v2], sem="V", inc=1)
                t_v2 = p.step("vector", lambda e: e.tensor_scalar(
                    out=smg[:, 13:14], in0=smg[:, 13:14], scalar1=EPS_LN, scalar2=None, op0=ALU.add),
                    waits=[t_v2], sem="V", inc=1)
                t_v2s = p.step("scalar", lambda e: e.activation(out=smg[:, 14:15], in_=smg[:, 13:14], func=AF.Sqrt),
                               waits=[t_v2], sem="S", inc=1)
                t_v2r = p.step("vector", lambda e: e.reciprocal(out=smg[:, 15:16], in_=smg[:, 14:15]),
                               waits=[t_v2s], sem="V", inc=1)
                t_ln2 = p.step("vector", lambda e: e.tensor_scalar(
                    out=ln2b[:], in0=y1b[:], scalar1=smg[:, 10:11], scalar2=smg[:, 15:16],
                    op0=ALU.subtract, op1=ALU.mult),
                    waits=[t_v2r, tG.get(("ln2b_free", i - 1))], sem="V", inc=1)
                def em_y1st(e, i=i):
                    return e.dma_start(out=AP(y1_sp.tensor, i * 128 * 2048, [[2048, 128], [1, 2048]]), in_=y1b[:])
                t_y1st = p.step("scalar", em_y1st, waits=[t_y1], sem="ssty", inc=16)
                tG[("y1_used", i)] = t_y1st
                if debug_outs:
                    tG[("y1_usedD", i)] = p.step("scalar", lambda e, i=i: e.dma_start(
                        out=AP(dbg_y1.tensor, i * 128 * 2048, [[2048, 128], [1, 2048]]), in_=y1b[:]),
                        waits=[t_y1], sem="sdb2", inc=16)
                def em_l2st(e, i=i):
                    return e.dma_start(out=AP(ln2_sp.tensor, i * 128 * 2048, [[2048, 128], [1, 2048]]), in_=ln2b[:])
                t_l2st = p.step("scalar", em_l2st, waits=[t_ln2], sem="sstl", inc=16)
                tG[("ln2b_free", i)] = t_l2st
                tG[("ln2_done", i)] = t_l2st

            FG = {}
            for i in range(NT + 1):
                if i < NT:
                    FG[i] = emit_frontG(i)
                if i >= 1:
                    emit_backG(i - 1, FG[i - 1])
                if i < NT:
                    FG[i].update(emit_frontG2(i, FG[i]))
                if i >= 1:
                    FG.pop(i - 1, None)

        t_ln2_all = p.tok("sstl")
        t_y1_all = p.tok("ssty")
        t_bar3 = barrier()

        esF = ExitStack()
        tF = {}
        with esF:
            ICB = INTER // 2048
            ln2xT = alloc("ln2xT", (128, 16 * TG), BF16, esF)
            wgu_c = [alloc(f"wgu_{k}", (128, 16 * 128), BF16, esF) for k in range(4)]
            wd_cs = [alloc(f"wdcs_{k}", (128, 16 * 512), BF16, esF) for k in range(2)]
            hT = alloc("hT", (128, 16 * TG), BF16, esF)
            acc = alloc("acc", (128, 16 * TG), F32, esF)
            silu_t = alloc("silu_t", (128, 512), F32, esF)
            ffsq = alloc("ffsq", (128, 512), F32, esF)
            srow2 = alloc("srow2", (1, 1024), F32, esF)
            scp2 = alloc("scp2", (2, 128), F32, esF)
            stc2 = alloc("stc2", (128, 2), F32, esF)
            y1_f = [alloc("y1f_0", (128, 2048), F32, esF)]
            y2_b = [alloc("y2_0", (128, 2048), F32, esF)]

            psGU = [esF.enter_context(nc.psum_tensor(f"psGU{k}", [128, 512], F32)) for k in range(4)]
            psD = [esF.enter_context(nc.psum_tensor(f"psD{k}", [128, 512], F32)) for k in range(2)]
            psS2 = esF.enter_context(nc.psum_tensor("psS2", [128, 512], F32))
            psB2 = esF.enter_context(nc.psum_tensor("psB2", [128, 512], F32))

            for g in range(NG):
                for c in range(16):
                    def em_l2t(e, g=g, c=c):
                        return e.dma_start_transpose(
                            ln2xT[:, c * TG:(c + 1) * TG],
                            AP(ln2_sp.tensor, g * TG * 2048 + c * 128, [[2048, TG], [1, 128]]))
                    w = [t_ln2_all] + t_bar3
                    if g > 0:
                        w.append(tF[("ln2xT_free", g - 1)])
                    p.step("sync", em_l2t, waits=w, sem="sl2t", inc=16)
                t_l2t = p.tok("sl2t")

                h_hist = []
                for icb in range(ICB):
                    for s in range(16):
                        qb = s % 2
                        osl = icb * 2048 + s * 128
                        def em_wg(e, osl=osl, qb=qb):
                            return e.dma_start(
                                out=wgu_c[qb * 2][:],
                                in_=AP(ag_out.tensor, osl,
                                       [[2 * INTER, 128], [128 * 2 * INTER, 16], [1, 128]]))
                        def em_wu(e, osl=osl, qb=qb):
                            return e.dma_start(
                                out=wgu_c[qb * 2 + 1][:],
                                in_=AP(ag_out.tensor, INTER + osl,
                                       [[2 * INTER, 128], [128 * 2 * INTER, 16], [1, 128]]))
                        w = [t_ag, tF.get(("wgu_free", (g, icb, s - 2)))]
                        if s < 2:
                            w.append(tF.get(("wgu_free", (g, icb - 1, s + 14))) if icb > 0 else
                                     (tF.get(("wgu_free", (g - 1, ICB - 1, s + 14))) if g > 0 else None))
                        t_wg = p.step("sync", em_wg, waits=w, sem=f"sfg{qb * 2}", inc=16)
                        t_wu = p.step("sync", em_wu, waits=[], sem=f"sfg{qb * 2 + 1}", inc=16)
                        bs = psGU if s % 2 == 0 else [psD[0], psD[1], psS2, psB2]
                        def em_gu(e, qb=qb, bs=bs):
                            inst = None
                            for c in range(16):
                                for ts in range(TS):
                                    e.matmul(bs[ts][:, 0:TSW],
                                             wgu_c[qb * 2][:, c * 128:(c + 1) * 128],
                                             ln2xT[:, c * TG + ts * TSW:c * TG + (ts + 1) * TSW],
                                             start=(c == 0), stop=(c == 15))
                                for ts in range(TS):
                                    inst = e.matmul(bs[TS + ts][:, 0:TSW],
                                                    wgu_c[qb * 2 + 1][:, c * 128:(c + 1) * 128],
                                                    ln2xT[:, c * TG + ts * TSW:c * TG + (ts + 1) * TSW],
                                                    start=(c == 0), stop=(c == 15))
                            return inst
                        w = [t_wu, t_wg, t_l2t]
                        if icb == 0 and s == 0:
                            w.append(tF.get(("epi_done", g - 1)))
                        if s == 0:
                            w.append(tF.get(("down_done", (g, icb - 1))) if icb > 0 else
                                     (tF.get(("down_done", (g - 1, ICB - 1))) if g > 0 else None))
                        t_gu = p.step("tensor", em_gu, waits=w, sem="T", inc=1)
                        t_h_last = None
                        slb = [silu_t, ffsq]
                        for ts in range(TS):
                            t_si = p.step("scalar", lambda e, bs=bs, ts=ts: e.activation(
                                out=slb[ts][:, 0:TSW], in_=bs[ts][:, 0:TSW], func=AF.Silu),
                                waits=[t_gu, tF.get(("silu_free", ts))], sem="S", inc=1)
                            def em_h(e, s=s, ts=ts, bs=bs):
                                return e.tensor_tensor(
                                    out=hT[:, s * TG + ts * TSW:s * TG + (ts + 1) * TSW],
                                    in0=slb[ts][:, 0:TSW],
                                    in1=bs[TS + ts][:, 0:TSW], op=ALU.mult)
                            t_h = p.step("vector", em_h, waits=[t_si], sem="V", inc=1)
                            h_hist.append(t_h)
                            tF[("silu_free", ts)] = t_h
                            t_h_last = t_h
                        tF[("wgu_free", (g, icb, s))] = t_h_last
                    t_h_all = t_h_last

                    t_down_last = None
                    t_dmm_last = None
                    for hs in range(4):
                        qw = hs % 2
                        def em_wdl(e, g=g, icb=icb, hs=hs, qw=qw):
                            return e.dma_start(
                                out=wd_cs[qw][:],
                                in_=AP(ag_out.tensor, WGU_ELEMS + icb * 2048 * 2048 + hs * 512,
                                       [[2048, 128], [128 * 2048, 16], [1, 512]]))
                        w = [t_ag, tF.get(("wd_free", (g, icb, hs - 2)))]
                        if hs < 2:
                            w.append(tF.get(("wd_free", (g, icb - 1, hs + 2))) if icb > 0 else
                                     (tF.get(("wd_free", (g - 1, ICB - 1, hs + 2))) if g > 0 else None))
                        t_wd = p.step("sync", em_wdl, waits=w, sem=f"sfd{qw}", inc=16)
                        for hcl in range(4):
                            hc = hs * 4 + hcl
                            bp = [psD[0], psD[1]] if hc % 2 == 0 else [psS2, psB2]
                            def em_d(e, hcl=hcl, qw=qw, bp=bp):
                                inst = None
                                for ic in range(16):
                                    for ts in range(TS):
                                        inst = e.matmul(bp[ts][:, 0:TSW],
                                                        wd_cs[qw][:, ic * 512 + hcl * 128:ic * 512 + (hcl + 1) * 128],
                                                        hT[:, ic * TG + ts * TSW:ic * TG + (ts + 1) * TSW],
                                                        start=(ic == 0), stop=(ic == 15))
                                return inst
                            w = [t_wd, t_h_all,
                                 tF.get(("d_ev", (g, icb, (hc - 2) * TS))),
                                 tF.get(("d_ev", (g, icb, (hc - 2) * TS + TS - 1)))]
                            t_dmm = p.step("tensor", em_d, waits=w, sem="T", inc=1)
                            t_dmm_last = t_dmm
                            for ts in range(TS):
                                def em_dev(e, hc=hc, ts=ts, bp=bp, icb=icb):
                                    dst = acc[:, hc * TG + ts * TSW:hc * TG + (ts + 1) * TSW]
                                    if icb == 0:
                                        return e.tensor_copy(out=dst, in_=bp[ts][:, 0:TSW])
                                    return e.tensor_tensor(out=dst, in0=dst, in1=bp[ts][:, 0:TSW], op=ALU.add)
                                w2 = [t_dmm]
                                if icb == 0:
                                    w2.append(tF.get(("acc_free", g - 1)))
                                t_dev = p.step("vector", em_dev, waits=w2, sem="V", inc=1)
                                tF[("d_ev", (g, icb, hc * TS + ts))] = t_dev
                                t_down_last = t_dev
                        tF[("wd_free", (g, icb, hs))] = t_dmm_last
                    tF[("down_done", (g, icb))] = t_down_last
                    tF[("dmm_done", (g, icb))] = t_dmm_last
                tF[("ln2xT_free", g)] = h_hist[-1]
                t_acc_done = tF[("down_done", (g, ICB - 1))]

                t_epi_last = None
                for ts in range(TS):
                    t_mm_prev = tF.get(("psS2_free", (g, ts - 1))) or tF.get(("psS2_free", (g - 1, TS - 1)))
                    for hc in range(16):
                        t_q1 = p.step("scalar", lambda e, hc=hc, ts=ts: e.activation(
                            out=ffsq[:, 0:TSW], in_=acc[:, hc * TG + ts * TSW:hc * TG + (ts + 1) * TSW],
                            func=AF.Square),
                            waits=[t_acc_done, t_mm_prev], sem="S", inc=1)
                        def em_mm(e, hc=hc, ts=ts):
                            e.matmul(psS2[0:1, 0:TSW], onc[:],
                                     acc[:, hc * TG + ts * TSW:hc * TG + (ts + 1) * TSW],
                                     start=(hc == 0), stop=(hc == 15))
                            return e.matmul(psB2[0:1, 0:TSW], onc[:], ffsq[:, 0:TSW],
                                            start=(hc == 0), stop=(hc == 15))
                        t_mm_prev = p.step("tensor", em_mm, waits=[t_q1, t_acc_done], sem="T", inc=1)
                    t_stm = t_mm_prev
                    def em_f1(e, ts=ts):
                        e.tensor_scalar(out=srow2[0:1, 0:TSW], in0=psS2[0:1, 0:TSW],
                                        scalar1=1.0 / HID, scalar2=None, op0=ALU.mult)
                        return e.tensor_scalar(out=ffsq[0:1, 0:TSW], in0=psB2[0:1, 0:TSW],
                                               scalar1=1.0 / HID, scalar2=None, op0=ALU.mult)
                    t_f1 = p.step("vector", em_f1, waits=[t_stm], sem="V", inc=1)
                    t_f2 = p.step("vector", lambda e: e.tensor_tensor(
                        out=silu_t[0:1, 0:TSW], in0=srow2[0:1, 0:TSW],
                        in1=srow2[0:1, 0:TSW], op=ALU.mult), waits=[t_f1], sem="V", inc=1)
                    t_f2 = p.step("vector", lambda e: e.tensor_tensor(
                        out=ffsq[0:1, 0:TSW], in0=ffsq[0:1, 0:TSW],
                        in1=silu_t[0:1, 0:TSW], op=ALU.subtract), waits=[t_f2], sem="V", inc=1)
                    t_f2 = p.step("vector", lambda e: e.tensor_scalar(
                        out=ffsq[0:1, 0:TSW], in0=ffsq[0:1, 0:TSW],
                        scalar1=EPS_LN, scalar2=None, op0=ALU.add), waits=[t_f2], sem="V", inc=1)
                    t_f3 = p.step("scalar", lambda e: e.activation(out=ffsq[0:1, 0:TSW], in_=ffsq[0:1, 0:TSW],
                                                                   func=AF.Sqrt),
                                  waits=[t_f2], sem="S", inc=1)
                    t_f4 = p.step("vector", lambda e: e.reciprocal(out=srow2[0:1, TSW:2 * TSW],
                                                                   in_=ffsq[0:1, 0:TSW]),
                                  waits=[t_f3], sem="V", inc=1)
                    t_f5 = p.step("vector", lambda e: e.scalar_tensor_tensor(
                        out=srow2[0:1, 0:TSW], in0=srow2[0:1, 0:TSW], scalar=-1.0,
                        in1=srow2[0:1, TSW:2 * TSW], op0=ALU.mult, op1=ALU.mult),
                        waits=[t_f4], sem="V", inc=1)

                    t_sce = None
                    for tt in range(TSW // 128):
                        gi = g * TILES_G + ts * (TSW // 128) + tt
                        def em_sct(e, tt=tt):
                            e.matmul(psB2[0:128, 256:257],
                                     srow2[0:1, TSW + tt * 128:TSW + (tt + 1) * 128],
                                     id_f[0:1, 0:1], is_transpose=True, start=True, stop=False)
                            return e.matmul(psB2[0:128, 257:258],
                                            srow2[0:1, tt * 128:(tt + 1) * 128],
                                            id_f[0:1, 0:1], is_transpose=True, start=False, stop=True)
                        t_sct = p.step("tensor", em_sct,
                                       waits=[t_f5, t_f1 if tt == 0 else None,
                                              tF.get(("stc2ev_prev", 0))], sem="T", inc=1)
                        t_sce = p.step("vector", lambda e: e.tensor_copy(out=stc2[:], in_=psB2[0:128, 256:258]),
                                       waits=[t_sct, tF.get(("stc2_free", 0))], sem="V", inc=1)
                        tF[("scp2_free", 0)] = t_sce
                        tF[("stc2ev_prev", 0)] = t_sce
                        def em_y1l(e, gi=gi):
                            return e.dma_start(out=y1_f[0][:],
                                               in_=AP(y1_sp.tensor, gi * 128 * 2048, [[2048, 128], [1, 2048]]))
                        p.step("sync", em_y1l, waits=[t_y1_all, tF.get(("y1f_free", 0))],
                               sem="sy10", inc=16)
                        t_y1l = p.tok("sy10")
                        t_bt = []
                        for bb in range(4):
                            def em_bt(e, bb=bb, ts=ts, tt=tt):
                                inst = None
                                for k in range(4):
                                    hc = bb * 4 + k
                                    inst = e.matmul(psGU[bb][:, k * 128:(k + 1) * 128],
                                                    acc[:, hc * TG + ts * TSW + tt * 128:hc * TG + ts * TSW + (tt + 1) * 128],
                                                    id_f[:], is_transpose=True,
                                                    start=(k == 0), stop=(k == 3))
                                return inst
                            w = [t_acc_done, tF.get(("y2_prev", 0))]
                            if ts == 0 and tt == 0:
                                w.append(h_hist[-1])
                            t_bt.append(p.step("tensor", em_bt, waits=w, sem="T", inc=1))
                        def em_y2a(e):
                            inst = None
                            for bb in range(4):
                                inst = e.tensor_scalar(out=y2_b[0][:, bb * 512:(bb + 1) * 512], in0=psGU[bb][:],
                                                       scalar1=stc2[:, 0:1], scalar2=stc2[:, 1:2],
                                                       op0=ALU.mult, op1=ALU.add)
                            return inst
                        t_y2a = p.step("vector", em_y2a,
                                       waits=t_bt + [t_sce, tF.get(("y2st_prev", 0))],
                                       sem="V", inc=1)
                        t_y2 = p.step("vector", lambda e: e.tensor_tensor(
                            out=y2_b[0][:], in0=y2_b[0][:], in1=y1_f[0][:], op=ALU.add),
                            waits=[t_y2a, t_y1l], sem="V", inc=1)
                        tF[("y2_prev", 0)] = t_y2
                        tF[("y1f_free", 0)] = t_y2
                        tF[("stc2_free", 0)] = t_y2
                        def em_y2st(e, gi=gi):
                            return e.dma_start(out=AP(y2_out.tensor, gi * 128 * 2048, [[2048, 128], [1, 2048]]),
                                               in_=y2_b[0][:])
                        t_y2s = p.step("scalar", em_y2st, waits=[t_y2], sem="sout", inc=16)
                        tF[("y2st_prev", 0)] = t_y2s
                        t_epi_last = t_y2s
                    tF[("psS2_free", (g, ts))] = t_sce
                tF[("epi_done", g)] = t_epi_last
                tF[("acc_free", g)] = t_epi_last

            p.step("sync", lambda e: e.nop(),
                   waits=[p.tok("sout"), p.tok("sstm"), p.tok("ssty"), p.tok("sstl"),
                          p.tok("sstr"), p.tok("sdbg0"), p.tok("sdbg1"), p.tok("sdb2"), p.tok("sda")],
                   sem=None, inc=0)

        with nc.Block() as blk:
            blk.sync(lambda e: p.emit_engine("sync", e))
            blk.gpsimd(lambda e: p.emit_engine("gpsimd", e))
            blk.tensor(lambda e: p.emit_engine("tensor", e))
            blk.vector(lambda e: p.emit_engine("vector", e))
            blk.scalar(lambda e: p.emit_engine("scalar", e))

    return nc


_NC_CACHE = {}


def _prep_inputs(inputs, n_cores, n_tiles):
    NTOK_ALL = n_cores * n_tiles * 128
    f32 = np.float32
    X = np.asarray(inputs["L0_attn_hidden_states"], f32).reshape(NTOK_ALL, HID)
    XT = np.ascontiguousarray(X.T)
    Wq = np.asarray(inputs["L0_attn_Wq"], f32)
    Wk = np.asarray(inputs["L0_attn_Wk"], f32)
    Wv = np.asarray(inputs["L0_attn_Wv"], f32)
    Wg = np.asarray(inputs["L0_attn_Wg"], f32)
    Wmb = np.asarray(inputs["L0_attn_Wmb"], f32)
    bq = np.asarray(inputs["L0_attn_bq"], f32)
    bk = np.asarray(inputs["L0_attn_bk"], f32)
    bv = np.asarray(inputs["L0_attn_bv"], f32)
    bg = np.asarray(inputs["L0_attn_bg"], f32)
    bmb = np.asarray(inputs["L0_attn_bmb"], f32)
    memory = np.asarray(inputs["L0_attn_memory"], f32)
    norm = np.asarray(inputs["L0_attn_norm"], f32)
    Wgu = np.asarray(inputs["L0_ffn_w_gate_up"], f32)
    Wd = np.asarray(inputs["L0_ffn_w_down"], f32)

    mem4_h = np.ascontiguousarray(memory.transpose(1, 0, 2).reshape(FD, HID))
    shared = {
        "watt": np.concatenate([np.ascontiguousarray(Wk.T).ravel(),
                                np.ascontiguousarray(Wv.T).ravel(),
                                np.ascontiguousarray(Wmb.T).ravel(),
                                mem4_h.ravel(),
                                np.ascontiguousarray(Wq.T).ravel(),
                                np.ascontiguousarray(Wg.T).ravel()]),
        "mem4": mem4_h,
        "nn96": np.ascontiguousarray(norm.reshape(NH, FC, 128).transpose(2, 0, 1).reshape(128, 96)),
        "nn96b": np.ascontiguousarray(norm.reshape(NH, FC, 128).transpose(2, 1, 0).reshape(128, 96)),
        "biasA": np.ascontiguousarray(np.broadcast_to(
            np.concatenate([bk, bv, bmb])[None, :], (128, 3072))).astype(ml_dtypes.bfloat16),
        "biasQ": np.ascontiguousarray(np.broadcast_to(bq[None, :], (128, HID))).astype(ml_dtypes.bfloat16),
        "biasG": np.ascontiguousarray(np.broadcast_to(bg[None, :], (128, HID))).astype(ml_dtypes.bfloat16),
        "idb": np.eye(128, dtype=ml_dtypes.bfloat16),
        "id32": np.eye(128, dtype=f32),
        "ones_col": np.ones((128, 1), f32),
        "ones_row": np.ones((1, 128), f32),
    }
    wflat = np.concatenate([np.ascontiguousarray(Wgu.T).ravel(),
                            np.ascontiguousarray(Wd.T).ravel()])
    SLICE = WTOT // n_cores
    NTOK = n_tiles * 128
    in_maps = []
    for c in range(n_cores):
        m = dict(shared)
        m["xT"] = np.ascontiguousarray(XT[:, c * NTOK:(c + 1) * NTOK])
        m["x_in"] = np.ascontiguousarray(X[c * NTOK:(c + 1) * NTOK])
        m["wconv"] = np.ascontiguousarray(wflat[c * SLICE:(c + 1) * SLICE].reshape(128, SLICE // 128))
        in_maps.append(m)
    return in_maps


def kernel(**inputs):
    n_cores, n_tiles = 8, 16
    key = (n_cores, n_tiles)
    if key not in _NC_CACHE:
        _NC_CACHE[key] = build_program(n_cores=n_cores, n_tiles=n_tiles)
    nc = _NC_CACHE[key]
    in_maps = _prep_inputs(inputs, n_cores, n_tiles)
    res = run_bass_kernel_spmd(nc, in_maps, core_ids=list(range(n_cores)))
    out = np.concatenate([res.results[c]["y2"] for c in range(n_cores)], axis=0)
    return out.reshape(2, 8192, HID)


# version 19
# speedup vs baseline: 1.0057x; 1.0057x over previous
import numpy as np
import ml_dtypes
import concourse.bass as bass
import concourse.mybir as mybir
from concourse.bass_utils import run_bass_kernel_spmd
from concourse.ap import AP
from contextlib import ExitStack

import concourse.bass_utils as _bu
_orig_gwa = _bu.get_walrus_args
def _gwa(*a, **k):
    return [x.replace("--enable-ldw-opt=false", "--enable-ldw-opt=true") if isinstance(x, str) else x
            for x in _orig_gwa(*a, **k)]
_bu.get_walrus_args = _gwa

F32 = mybir.dt.float32
BF16 = mybir.dt.bfloat16
AF = mybir.ActivationFunctionType
ALU = mybir.AluOpType

HID = 2048
NH = 16
FD = 768
FC = 6
EPS_LN = 1e-5
EPS_AT = 1e-8
EPS_L2 = 1e-12
INTER = 8192
AR_COLS = 12288 + 96
WGU_ELEMS = HID * 2 * INTER
WD_ELEMS = INTER * HID
WTOT = WGU_ELEMS + WD_ELEMS
ATT_OFF = {"wk": 0, "wv": 2048 * 512, "wmb": 2 * 2048 * 512,
           "mem4": 2 * 2048 * 512 + 2048 * 2048,
           "wq": 2 * 2048 * 512 + 2048 * 2048 + 768 * 2048,
           "wg": 2 * 2048 * 512 + 2 * 2048 * 2048 + 768 * 2048}
ATT_ELEMS = 2 * 2048 * 512 + 3 * 2048 * 2048 + 768 * 2048
AX = mybir.AxisListType


class Prog:
    def __init__(self, nc, es, sem_names):
        self.nc = nc
        self.sems = {n: es.enter_context(nc.semaphore(n)) for n in sem_names}
        self.counts = {n: 0 for n in sem_names}
        self.steps = []
        self.last_wait = {}

    def step(self, eng, emit, waits=(), sem=None, inc=0):
        tok = None
        if sem is not None:
            self.counts[sem] += inc
            tok = (sem, self.counts[sem])
        self.steps.append((eng, tuple(w for w in waits if w is not None), emit, sem, inc))
        return tok

    def tok(self, sem):
        return (sem, self.counts[sem])

    def emit_engine(self, eng_name, eng):
        for e, waits, emit, sem, inc in self.steps:
            if e != eng_name:
                continue
            for (ws, wv) in waits:
                if wv <= 0:
                    continue
                if self.last_wait.get((eng_name, ws), -1) >= wv:
                    continue
                self.last_wait[(eng_name, ws)] = wv
                eng.wait_ge(self.sems[ws], wv)
            inst = emit(eng)
            if sem is not None and inc > 0:
                assert inst is not None
                inst.then_inc(self.sems[sem], inc)


def build_program(n_cores=8, n_tiles=16, debug_outs=False):
    NT = n_tiles
    NTOK = NT * 128
    SUP = 2
    NSUP = NT // SUP
    TG = min(1024, NTOK)
    NG = NTOK // TG
    TILES_G = TG // 128
    TS = (TG + 511) // 512
    TSW = TG // TS

    nc = bass.Bass("TRN2", target_bir_lowering=False)

    def ein(name, shape, dt=F32):
        return nc.dram_tensor(name, list(shape), dt, kind="ExternalInput").ap()

    xT = ein("xT", (HID, NTOK))
    x_in = ein("x_in", (NTOK, HID))
    mem4 = ein("mem4", (FD, HID))
    nn96 = ein("nn96", (128, 96))
    nn96b = ein("nn96b", (128, 96))
    biasA = ein("biasA", (128, 3072), BF16)
    biasQ = ein("biasQ", (128, HID), BF16)
    biasG = ein("biasG", (128, HID), BF16)
    wconv = ein("wconv", (128, WTOT // n_cores // 128))
    watt = ein("watt", (ATT_ELEMS,))
    idb = ein("idb", (128, 128), BF16)
    id32 = ein("id32", (128, 128), F32)
    ones_col = ein("ones_col", (128, 1), F32)
    ones_row = ein("ones_row", (1, 128), F32)

    y2_out = nc.dram_tensor("y2", [NTOK, HID], F32, kind="ExternalOutput").ap()
    if debug_outs:
        dbg_ar = nc.dram_tensor("dbg_ar", [128, AR_COLS], F32, kind="ExternalOutput").ap()
        dbg_mh = nc.dram_tensor("dbg_mh", [NTOK, HID], F32, kind="ExternalOutput").ap()
        dbg_y1 = nc.dram_tensor("dbg_y1", [NTOK, HID], F32, kind="ExternalOutput").ap()

    ag_in = nc.dram_tensor("ag_in", [WTOT // n_cores], BF16, kind="Internal").ap()
    att_bf = nc.dram_tensor("att_bf", [ATT_ELEMS], BF16, kind="Internal").ap()
    ag_out = nc.dram_tensor("ag_out", [WTOT], BF16, kind="Internal").ap()
    ar_in = nc.dram_tensor("ar_in", [128, AR_COLS], F32, kind="Internal").ap()
    ar_out = nc.dram_tensor("ar_out", [128, AR_COLS], F32, kind="Internal").ap()
    mh_sp = nc.dram_tensor("mh_sp", [NTOK, HID], F32, kind="Internal").ap()
    y1_sp = nc.dram_tensor("y1_sp", [NTOK, HID], F32, kind="Internal").ap()
    ln2_sp = nc.dram_tensor("ln2_sp", [NTOK, HID], BF16, kind="Internal").ap()
    stats_dram = nc.dram_tensor("stats_dram", [NT, 256], F32, kind="Internal").ap()

    sem_names = ["sx0", "sx1", "sxx0", "sxx1", "sld", "sgA", "sgA2", "sgQ", "sgG", "scvA", "scvB",
                 "scv0", "scv1", "scs0", "scs1", "sag", "sar", "sda",
                 "sstm", "ssty", "sstl", "sstr", "sdbg", "smh0", "smh1", "sda1",
                 "sy10", "sl2t", "sfg0", "sfg1", "sfg2", "sfg3", "sfd0", "sfd1",
                 "sout", "sarld", "sarn", "srw0", "srw1", "sdbg0", "sdbg1", "sdb2",
                 "T", "V", "S"]

    es = ExitStack()
    with es:
        p = Prog(nc, es, sem_names)

        def alloc(name, shape, dt=F32, stack=None):
            return (stack or es).enter_context(nc.sbuf_tensor(name, list(shape), dt))

        id_b = alloc("id_b", (128, 128), BF16)
        id_f = alloc("id_f", (128, 128), F32)
        onc = alloc("onc", (128, 1), F32)
        onr = alloc("onr", (1, 128), F32)
        stats_cols = alloc("stats_cols", (128, NT * 2), F32)
        srow = alloc("srow", (1, 384), F32)
        rowbuf = alloc("rowbuf", (1, 256), F32)
        rowb2 = [alloc(f"rowb2_{k}", (1, 256), F32) for k in range(2)]
        dummyV = alloc("dummyV", (1, 8), F32)
        dummyS = alloc("dummyS", (1, 8), F32)

        p.step("sync", lambda e: e.dma_start(out=id_b[:], in_=idb), sem="sld", inc=16)
        p.step("sync", lambda e: e.dma_start(out=id_f[:], in_=id32), sem="sld", inc=16)
        p.step("sync", lambda e: e.dma_start(out=onc[:], in_=ones_col), sem="sld", inc=16)
        p.step("sync", lambda e: e.dma_start(out=onr[:], in_=ones_row), sem="sld", inc=16)
        t_ld0 = p.tok("sld")

        def cast3d(dst, src, cols, nchunk, stride):
            return lambda e: e.dma_start(
                out=dst[:, 0:nchunk * cols],
                in_=AP(src.tensor, 0, [[stride, 128], [128 * stride, nchunk], [1, cols]]))

        def ld3d(dst, off, cols, nchunk, stride):
            return lambda e: e.dma_start(
                out=dst[:, 0:nchunk * cols],
                in_=AP(att_bf.tensor, off, [[stride, 128], [128 * stride, nchunk], [1, cols]]))

        def barrier():
            stores = [p.tok(s) for s in ("sstm", "ssty", "sstl", "sstr",
                                         "sdbg0", "sdbg1", "sdb2", "sda", "scs0", "scs1")]
            tT = p.step("tensor", lambda e: e.ldweights(id_b[:]),
                waits=[p.tok("V"), p.tok("S")] + stores, sem="T", inc=1)
            tV = p.step("vector", lambda e: e.memset(dummyV[0:1, 0:1], 0.0),
                        waits=[p.tok("T")], sem="V", inc=1)
            tS = p.step("scalar", lambda e: e.memzero(dummyS[0:1, 0:1]),
                        waits=[p.tok("T")], sem="S", inc=1)
            return [tT, tV, tS, p.tok("V"), p.tok("S")]


        esA = ExitStack()
        tA = {}
        with esA:
            wk_b = alloc("wk_b", (128, 16 * 512), BF16, esA)
            wv_b = alloc("wv_b", (128, 16 * 512), BF16, esA)
            wmb_b = alloc("wmb_b", (128, 16 * 2048), BF16, esA)
            mem4_b = alloc("mem4_b", (128, FC * 2048), BF16, esA)
            nn96b_b = alloc("nn96b_b", (128, 96), BF16, esA)
            biasA_s = alloc("biasA_s", (128, 3072), BF16, esA)
            xT_t = [alloc("xT_0", (128, HID), F32, esA)]
            scr = alloc("scr", (128, HID), F32, esA)
            ln1xT = [alloc(f"ln1xT_{k}", (128, HID), BF16, esA) for k in range(2)]
            kf = alloc("kf", (128, 512), F32, esA)
            vf = [alloc(f"vf_{k}", (128, 512), F32, esA) for k in range(2)]
            mbp = alloc("mbp", (128, 2048), F32, esA)
            x2k = alloc("x2k", (128, 4 * 512), BF16, esA)
            featk = [alloc(f"featk_{k}", (128, 4 * FD), BF16, esA) for k in range(2)]
            mkT = alloc("mkT", (128, 4 * FD), BF16, esA)
            wmv4 = [alloc(f"wmv4_{k}", (128, 4 * 512), BF16, esA) for k in range(2)]
            coefb = alloc("coefb", (128, 16), BF16, esA)
            sm = alloc("sm", (128, 80), F32, esA)
            pvt = alloc("pvt", (128, 512), F32, esA)
            dstg = [alloc(f"dstg_{k}", (128, 512), F32, esA) for k in range(2)]

            psStat = esA.enter_context(nc.psum_tensor("psStat", [128, 512], F32))
            psBC = esA.enter_context(nc.psum_tensor("psBC", [128, 256], F32))
            psN = [esA.enter_context(nc.psum_tensor(f"psN{k}", [128, 512], F32)) for k in range(4)]
            psT = esA.enter_context(nc.psum_tensor("psT", [128, 1024], BF16))
            psDN = esA.enter_context(nc.psum_tensor("psDN", [128, 96], F32))

            KV_ELEMS = 2 * 2048 * 512
            t_cv1 = p.step("gpsimd", lambda e: e.dma_start(
                out=AP(att_bf.tensor, 0, [[1, KV_ELEMS]]),
                in_=AP(watt.tensor, 0, [[1, KV_ELEMS]])), sem="scvA", inc=16)
            p.step("gpsimd", lambda e: e.dma_start(
                out=AP(att_bf.tensor, KV_ELEMS, [[1, ATT_ELEMS - KV_ELEMS]]),
                in_=AP(watt.tensor, KV_ELEMS, [[1, ATT_ELEMS - KV_ELEMS]])), sem="scvB", inc=16)
            p.step("gpsimd", lambda e: e.dma_start(out=nn96b_b[:], in_=nn96b), sem="scvB", inc=16)
            tok_nn = p.tok("scvB")
            p.step("sync", ld3d(wk_b, ATT_OFF["wk"], 512, 16, 512),
                   waits=[t_cv1], sem="sgA", inc=16)
            p.step("sync", ld3d(wv_b, ATT_OFF["wv"], 512, 16, 512),
                   waits=[t_cv1], sem="sgA", inc=16)
            p.step("sync", lambda e: e.dma_start(out=biasA_s[:], in_=biasA), sem="sgA", inc=16)
            tok_wA = p.tok("sgA")
            p.step("scalar", ld3d(wmb_b, ATT_OFF["wmb"], 2048, 16, 2048),
                   waits=[tok_nn], sem="sgA2", inc=16)
            p.step("scalar", ld3d(mem4_b, ATT_OFF["mem4"], 2048, 6, 2048),
                   waits=[tok_nn], sem="sgA2", inc=16)
            tok_wA2 = p.tok("sgA2")

            SLICE = WTOT // n_cores
            t_cv = p.step("gpsimd", lambda e: e.dma_start(out=ag_in, in_=wconv), sem="scv0", inc=16)
            t_conv = [t_cv]
            if n_cores > 1:
                t_ag = p.step("gpsimd", lambda e: e.collective_compute(
                    "AllGather", ALU.bypass, replica_groups=[list(range(n_cores))],
                    ins=[ag_in], outs=[ag_out]), waits=t_conv, sem="sag", inc=1)
            else:
                t_ag = p.step("gpsimd", lambda e: e.dma_start(out=ag_out, in_=ag_in),
                              waits=t_conv, sem="sag", inc=16)

            def emit_frontA(i):
                F = {}
                par = i % 2
                def ldxT(e, i=i):
                    return e.dma_start(out=xT_t[0][:],
                                       in_=AP(xT.tensor, i * 128,
                                              [[NTOK, 128], [128 * NTOK, 16], [1, 128]]))
                p.step("sync", ldxT, waits=[tA.get(("xT_free", i - 1))], sem="sx0", inc=16)
                t_x = p.tok("sx0")

                t_sq = p.step("scalar", lambda e: e.activation(out=scr[:], in_=xT_t[0][:], func=AF.Square),
                              waits=[t_x, tA.get(("scr_free", i - 1))], sem="S", inc=1)
                def em_stats(e):
                    for c in range(16):
                        e.matmul(psStat[0:1, 0:128], onc[:], xT_t[0][:, c * 128:(c + 1) * 128],
                                 start=(c == 0), stop=False)
                    inst = None
                    for c in range(16):
                        inst = e.matmul(psStat[0:1, 128:256], onc[:], scr[:, c * 128:(c + 1) * 128],
                                        start=False, stop=(c == 15))
                    return inst
                t_st = p.step("tensor", em_stats,
                              waits=[t_x, t_sq, t_ld0, tA.get(("psStat_free", i - 1))], sem="T", inc=1)
                t1a = p.step("vector", lambda e: e.tensor_scalar(
                    out=srow[0:1, 0:256], in0=psStat[0:1, 0:256],
                    scalar1=1.0 / HID, scalar2=None, op0=ALU.mult), waits=[t_st], sem="V", inc=1)
                t1 = p.step("vector", lambda e: e.tensor_tensor(
                    out=srow[0:1, 256:384], in0=srow[0:1, 0:128],
                    in1=srow[0:1, 0:128], op=ALU.mult), waits=[t1a], sem="V", inc=1)
                t2a = p.step("vector", lambda e: e.tensor_tensor(
                    out=srow[0:1, 128:256], in0=srow[0:1, 128:256],
                    in1=srow[0:1, 256:384], op=ALU.subtract), waits=[t1], sem="V", inc=1)
                t2 = p.step("vector", lambda e: e.tensor_scalar(
                    out=srow[0:1, 256:384], in0=srow[0:1, 128:256],
                    scalar1=EPS_LN, scalar2=None, op0=ALU.add), waits=[t2a], sem="V", inc=1)
                t3 = p.step("scalar", lambda e: e.activation(out=srow[0:1, 256:384],
                                                             in_=srow[0:1, 256:384], func=AF.Sqrt),
                            waits=[t2], sem="S", inc=1)
                t4a = p.step("vector", lambda e: e.reciprocal(
                    out=rowbuf[0:1, 0:128], in_=srow[0:1, 256:384]),
                    waits=[t3, tA.get(("rowbuf_used", i - 1)), tA.get(("rowbuf_used2", i - 1))],
                    sem="V", inc=1)
                t4 = p.step("vector", lambda e: e.scalar_tensor_tensor(
                    out=rowbuf[0:1, 128:256], in0=srow[0:1, 0:128], scalar=-1.0,
                    in1=rowbuf[0:1, 0:128], op0=ALU.mult, op1=ALU.mult),
                    waits=[t4a], sem="V", inc=1)
                t_rst = p.step("scalar", lambda e, i=i: e.dma_start(
                    out=AP(stats_dram.tensor, i * 256, [[256, 1], [1, 256]]), in_=rowbuf[0:1, :]),
                    waits=[t4], sem="sstr", inc=16)
                tA[("rows_stored", i)] = t_rst
                def em_stt(e, i=i):
                    e.matmul(psStat[0:128, 256:257], rowbuf[0:1, 0:128], id_f[0:1, 0:1],
                             is_transpose=True, start=True, stop=False)
                    e.matmul(psStat[0:128, 257:258], rowbuf[0:1, 128:256], id_f[0:1, 0:1],
                             is_transpose=True, start=False, stop=True)
                    return e.matmul(psBC[:, 0:256], onr[:],
                                    rowbuf[0:1, 0:256], start=True, stop=True)
                t_bc = p.step("tensor", em_stt, waits=[t4, tA.get(("psBC_free", i - 1))], sem="T", inc=1)
                tA[("rowbuf_used", i)] = t_rst
                tA[("rowbuf_used2", i)] = t_bc
                t_sc = p.step("vector", lambda e, i=i: e.tensor_copy(
                    out=stats_cols[:, i * 2:(i + 1) * 2], in_=psStat[0:128, 256:258]),
                    waits=[t_bc], sem="V", inc=1)

                t_ln1a = p.step("vector", lambda e: e.tensor_tensor(
                    out=scr[:], in0=xT_t[0][:],
                    in1=AP(psBC, 0, [[256, 128], [0, 16], [1, 128]]), op=ALU.mult),
                    waits=[t_bc, t_st], sem="V", inc=1)
                t_ln1 = p.step("vector", lambda e, par=par: e.tensor_tensor(
                    out=ln1xT[par][:], in0=scr[:],
                    in1=AP(psBC, 128, [[256, 128], [0, 16], [1, 128]]), op=ALU.add),
                    waits=[t_ln1a, tA.get(("ln1xT_used", i - 2))], sem="V", inc=1)
                tA[("xT_free", i)] = t_ln1a
                tA[("scr_free", i)] = t_ln1
                tA[("psBC_free", i)] = t_ln1

                F.update(t_ln1=t_ln1, t_ln1a=t_ln1a, t_st=t_st, t_sc=t_sc, par=par)
                return F

            def emit_frontA2(i, F):
                par = i % 2
                t_ln1 = F["t_ln1"]
                def em_kv(e, par=par):
                    inst = None
                    for c in range(16):
                        lh = ln1xT[par][:, c * 128:(c + 1) * 128]
                        e.matmul(psN[0][:], lh, wk_b[:, c * 512:(c + 1) * 512],
                                 start=(c == 0), stop=(c == 15))
                        inst = e.matmul(psN[1][:], lh, wv_b[:, c * 512:(c + 1) * 512],
                                        start=(c == 0), stop=(c == 15))
                    return inst
                t_kv = p.step("tensor", em_kv,
                              waits=[t_ln1, tok_wA, tA.get(("psN01_free2", i - 1))], sem="T", inc=1)
                def em_kvev(e, par=par):
                    e.tensor_tensor(out=kf[:], in0=psN[0][:], in1=biasA_s[:, 0:512], op=ALU.add)
                    return e.tensor_tensor(out=vf[par][:], in0=psN[1][:], in1=biasA_s[:, 512:1024], op=ALU.add)
                t_kvev = p.step("vector", em_kvev,
                                waits=[t_kv, tA.get(("kf_used", i - 1)), tA.get(("vf_used", i - 2))],
                                sem="V", inc=1)
                def em_relu(e):
                    e.activation(out=AP(x2k, 0, [[4 * 512, 128], [512, 4], [256, 2], [1, 128]]),
                                 in_=AP(kf, 0, [[512, 128], [128, 4], [0, 2], [1, 128]]),
                                 func=AF.Relu)
                    return e.activation(out=AP(x2k, 128, [[4 * 512, 128], [512, 4], [256, 2], [1, 128]]),
                                        in_=AP(kf, 0, [[512, 128], [128, 4], [0, 2], [1, 128]]),
                                        func=AF.Relu, scale=-1.0)
                t_relu = p.step("scalar", em_relu, waits=[t_kvev, tA.get(("x2k_used", i - 1))], sem="S", inc=1)
                tA[("kf_used", i)] = t_relu
                F.update(t_kvev=t_kvev, t_relu=t_relu)
                return F

            def emit_frontA2b(i, F):
                par = i % 2
                t_relu = F["t_relu"]
                def em_feat(e, par=par):
                    return e.tensor_tensor(
                        out=featk[par][:],
                        in0=AP(x2k, 256, [[4 * 512, 128], [512, 4], [0, 3], [1, 256]]),
                        in1=AP(x2k, 255, [[4 * 512, 128], [512, 4], [-1, 3], [1, 256]]),
                        op=ALU.mult)
                t_feat = p.step("vector", em_feat,
                                waits=[t_relu, tA.get(("featk_free", i - 2))], sem="V", inc=1)
                tA[("x2k_used", i)] = t_feat
                F.update(t_feat=t_feat)
                return F

            def emit_backA(i, F, Fnext):
                par = i % 2
                t_feat = F["t_feat"]

                def mk_tp(rnd):
                    def em_tp(e, rnd=rnd, par=par):
                        inst = None
                        for k in range(8):
                            ch = rnd * 8 + k
                            inst = e.matmul(psT[:, k * 128:(k + 1) * 128],
                                            featk[par][:, ch * 128:(ch + 1) * 128],
                                            id_b[:], is_transpose=True,
                                            start=(k == 0), stop=(k == 7))
                        return inst
                    return em_tp

                def mk_tpev(rnd):
                    def em_tev(e, rnd=rnd):
                        return e.copy(out=mkT[:, rnd * 1024:(rnd + 1) * 1024], in_=psT[:])
                    return em_tev

                t_tp0 = p.step("tensor", mk_tp(0), waits=[t_feat, t_ld0, tA.get(("mkT_used", i - 1))],
                               sem="T", inc=1)
                t_tpev0 = p.step("scalar", mk_tpev(0), waits=[t_tp0], sem="S", inc=1)
                t_mbev = []
                t_mbmm = []
                t_tpev = [t_tpev0]
                for half in range(2):
                    def em_mb(e, half=half, par=par):
                        inst = None
                        for c in range(16):
                            lh = ln1xT[par][:, c * 128:(c + 1) * 128]
                            for s in range(2):
                                sl = half * 2 + s
                                inst = e.matmul(psN[2 + s][:], lh,
                                                wmb_b[:, c * 2048 + sl * 512:c * 2048 + (sl + 1) * 512],
                                                start=(c == 0), stop=(c == 15))
                        return inst
                    wmb_w = [F["t_ln1"], tok_wA2]
                    if half == 0:
                        wmb_w.append(tA.get(("psN23_free", i - 1)))
                    else:
                        wmb_w.append(t_mbev[0])
                    t_mbmm.append(p.step("tensor", em_mb, waits=wmb_w, sem="T", inc=1))
                    rnd = half + 1
                    t_tp = p.step("tensor", mk_tp(rnd), waits=[t_feat, t_tpev[rnd - 1]], sem="T", inc=1)
                    t_tpev.append(p.step("scalar", mk_tpev(rnd), waits=[t_tp], sem="S", inc=1))
                    def em_mbev(e, half=half):
                        e.tensor_tensor(out=mbp[:, half * 1024:half * 1024 + 512],
                                        in0=psN[2][:], in1=biasA_s[:, 1024 + half * 1024:1536 + half * 1024],
                                        op=ALU.add)
                        return e.tensor_tensor(out=mbp[:, half * 1024 + 512:half * 1024 + 1024],
                                               in0=psN[3][:], in1=biasA_s[:, 1536 + half * 1024:2048 + half * 1024],
                                               op=ALU.add)
                    t_mbev.append(p.step("vector", em_mbev,
                                         waits=[t_mbmm[half], tA.get(("mbp_used", i - 1)) if half == 0 else None],
                                         sem="V", inc=1))
                tA[("ln1xT_used", i)] = t_mbmm[1]
                t_sig = p.step("scalar", lambda e: e.activation(out=mbp[:], in_=mbp[:], func=AF.Sigmoid),
                               waits=[t_mbev[1], tA.get(("mbs_used", i - 1))], sem="S", inc=1)
                tA[("mbp_used", i)] = t_sig

                t_n2 = [t_feat, tA.get(("sm_used", i - 1)), t_sig]
                for j in range(4):
                    t_n2 = [p.step("scalar", lambda e, j=j, par=par: e.activation(
                        out=x2k[:, 0:FD], in_=featk[par][:, j * FD:(j + 1) * FD],
                        func=AF.Square, accum_out=sm[:, j:j + 1]),
                        waits=t_n2, sem="S", inc=1)]
                t_n2 = t_n2[0]
                t_sqn = p.step("scalar", lambda e: e.activation(out=sm[:, 4:8], in_=sm[:, 0:4], func=AF.Sqrt),
                               waits=[t_n2], sem="S", inc=1)
                t_iva = p.step("vector", lambda e: e.tensor_scalar(
                    out=sm[:, 4:8], in0=sm[:, 4:8], scalar1=EPS_L2, scalar2=None, op0=ALU.max),
                    waits=[t_sqn], sem="V", inc=1)
                t_ivb = p.step("vector", lambda e: e.reciprocal(out=sm[:, 8:12], in_=sm[:, 4:8]),
                               waits=[t_iva], sem="V", inc=1)
                t_invn = p.step("vector", lambda e: e.tensor_copy(
                    out=sm[:, 16:32], in_=AP(sm, 8, [[80, 128], [1, 4], [0, 4]])),
                    waits=[t_ivb], sem="V", inc=1)

                t_numMM = []
                for j in range(4):
                    def em_num(e, j=j):
                        inst = None
                        for c in range(6):
                            lh = mkT[:, (j * 6 + c) * 128:(j * 6 + c) * 128 + 128]
                            e.matmul(psN[j][:], lh,
                                     mem4_b[:, c * 2048 + j * 512:c * 2048 + (j + 1) * 512],
                                     start=(c == 0), stop=(c == 5))
                            inst = e.matmul(psStat[:, 258 + j * 4:258 + (j + 1) * 4], lh,
                                            nn96b_b[:, c * 16 + j * 4:c * 16 + (j + 1) * 4],
                                            start=(j == 0 and c == 0), stop=(j == 3 and c == 5))
                        return inst
                    w = [t_tpev[min(2, (j * 6 + 5) // 8)], tok_wA2, tok_nn]
                    if Fnext:
                        w.append(Fnext["t_sc"])
                    if j < 2:
                        w.append(F["t_kvev"])
                        w.append(tA.get((f"dev_b{j}", i - 1)))
                    else:
                        w.append(t_mbev[1])
                    t_numMM.append(p.step("tensor", em_num, waits=w, sem="T", inc=1))
                tA[("mkT_used", i)] = t_numMM[3]

                t_dn1 = p.step("vector", lambda e: e.tensor_tensor(
                    out=sm[:, 32:48], in0=psStat[:, 258:274], in1=sm[:, 16:32], op=ALU.mult),
                    waits=[t_numMM[3], t_invn], sem="V", inc=1)
                t_dn2 = p.step("vector", lambda e: e.tensor_scalar(
                    out=sm[:, 32:48], in0=sm[:, 32:48], scalar1=EPS_AT, scalar2=None, op0=ALU.add),
                    waits=[t_dn1], sem="V", inc=1)
                t_dn3 = p.step("vector", lambda e: e.reciprocal(out=sm[:, 48:64], in_=sm[:, 32:48]),
                               waits=[t_dn2], sem="V", inc=1)
                t_den = p.step("vector", lambda e: e.tensor_tensor(
                    out=sm[:, 48:64], in0=sm[:, 48:64], in1=sm[:, 16:32], op=ALU.mult),
                    waits=[t_dn3], sem="V", inc=1)
                tA[("psStat_free", i + 1)] = t_dn1
                tc = p.step("vector", lambda e: e.tensor_tensor(
                    out=sm[:, 64:80], in0=sm[:, 16:32], in1=sm[:, 16:32], op=ALU.mult),
                    waits=[t_den], sem="V", inc=1)
                tc = p.step("vector", lambda e: e.tensor_tensor(
                    out=sm[:, 64:80], in0=sm[:, 64:80],
                    in1=AP(sm, 0, [[80, 128], [1, 4], [0, 4]]), op=ALU.mult),
                    waits=[tc], sem="V", inc=1)
                tc = p.step("vector", lambda e: e.reciprocal(out=sm[:, 64:80], in_=sm[:, 64:80]),
                            waits=[tc], sem="V", inc=1)
                tc = p.step("vector", lambda e: e.tensor_tensor(
                    out=sm[:, 64:80], in0=sm[:, 64:80], in1=sm[:, 32:48], op=ALU.mult),
                    waits=[tc], sem="V", inc=1)
                tc = p.step("vector", lambda e: e.tensor_scalar(
                    out=sm[:, 64:80], in0=sm[:, 64:80], scalar1=-1.0, scalar2=1.0,
                    op0=ALU.mult, op1=ALU.add), waits=[tc], sem="V", inc=1)
                tc = p.step("vector", lambda e: e.tensor_scalar(
                    out=sm[:, 64:80], in0=sm[:, 64:80], scalar1=0.0, scalar2=1.0,
                    op0=ALU.max, op1=ALU.min), waits=[tc], sem="V", inc=1)
                t_coef = p.step("vector", lambda e: e.tensor_tensor(
                    out=coefb[:], in0=sm[:, 64:80], in1=sm[:, 16:32], op=ALU.mult),
                    waits=[tc, tA.get(("coef_used", i - 1))], sem="V", inc=1)

                t_wmv = []
                for j in range(4):
                    tw = p.step("vector", lambda e, j=j: e.tensor_tensor(
                        out=pvt[:], in0=psN[j][:],
                        in1=AP(sm, 48 + j * 4, [[80, 128], [1, 4], [0, 128]]), op=ALU.mult),
                        waits=[t_den, t_sig, t_numMM[j], t_wmv[-1] if t_wmv else None],
                        sem="V", inc=1)
                    tw = p.step("vector", lambda e, j=j, par=par: e.tensor_tensor(
                        out=pvt[:], in0=pvt[:],
                        in1=AP(vf[par], j * 128, [[512, 128], [0, 4], [1, 128]]), op=ALU.subtract),
                        waits=[tw], sem="V", inc=1)
                    t_wmv.append(p.step("vector", lambda e, j=j, par=par: e.scalar_tensor_tensor(
                        out=wmv4[par][:, j * 512:(j + 1) * 512],
                        in0=pvt[:], scalar=sm[:, 8 + j:9 + j],
                        in1=mbp[:, j * 512:(j + 1) * 512],
                        op0=ALU.mult, op1=ALU.mult),
                        waits=[tw, tA.get(("wmv_free", i - 2))], sem="V", inc=1))
                tA[("vf_used", i)] = t_wmv[3]
                tA[("mbs_used", i)] = t_wmv[3]
                tA[("sm_used", i)] = t_wmv[3]
                tA[("psN01_free2", i)] = t_wmv[1]
                tA[("psN23_free", i)] = t_wmv[3]
                psN01_free = t_wmv[1]

                for j in range(4):
                    tA[("wmvprev", (i, j))] = t_wmv[j]
                tA[("back_refs", i)] = dict(t_wmv=t_wmv, t_coef=t_coef, t_numMM=t_numMM,
                                            psN01_free=psN01_free)

            def emit_tailA(i, Fnext=None):
                par = i % 2
                R = tA.pop(("back_refs", i))
                t_wmv, t_coef, t_numMM = R["t_wmv"], R["t_coef"], R["t_numMM"]
                psN01_free = R["psN01_free"]
                def em_dn(e, i=i, par=par):
                    inst = None
                    for j in range(4):
                        for c in range(6):
                            lh = featk[par][:, (j * 6 + c) * 128:(j * 6 + c) * 128 + 128]
                            inst = e.matmul(psDN[:, c * 16 + j * 4:c * 16 + (j + 1) * 4], lh,
                                            coefb[:, j * 4:(j + 1) * 4],
                                            start=(i == 0 and j == 0 and c == 0),
                                            stop=(i == NT - 1 and j == 3 and c == 5))
                    return inst
                t_dnm = p.step("tensor", em_dn, waits=[t_coef, t_numMM[3]], sem="T", inc=1)
                tA[("coef_used", i)] = t_dnm
                tA[("featk_free", i)] = t_dnm

                if i % SUP == SUP - 1:
                    si = i // SUP
                    t_delta_last = None
                    gi = 0
                    for j in range(4):
                        for rnd in range(6):
                            def em_dl(e, j=j, rnd=rnd, i=i):
                                inst = None
                                c = rnd
                                pb = psN[rnd % 2]
                                for t in range(SUP):
                                    tt = i - SUP + 1 + t
                                    inst = e.matmul(pb[:],
                                                    featk[tt % 2][:, (j * 6 + c) * 128:(j * 6 + c) * 128 + 128],
                                                    wmv4[tt % 2][:, j * 512:(j + 1) * 512],
                                                    start=(t == 0), stop=(t == SUP - 1))
                                return inst
                            w = [t_wmv[j], tA.get(("wmvprev", (i - 1, j)))]
                            if gi == 0:
                                w.append(psN01_free)
                            if gi < 2 and Fnext is not None:
                                w.append(Fnext["t_kvev"])
                            w.append(tA.get(("dev", (si, gi - 2))))
                            t_dl = p.step("tensor", em_dl, waits=w, sem="T", inc=1)
                            def em_dev(e, rnd=rnd, gi=gi):
                                return e.tensor_copy(out=dstg[gi % 2][:], in_=psN[rnd % 2][:])
                            t_dev = p.step("vector", em_dev,
                                           waits=[t_dl, tA.get(("dacc", (si, gi - 2))) or
                                                  tA.get(("dacc", (si - 1, 22 + gi)))],
                                           sem="V", inc=1)
                            tA[("dev", (si, gi))] = t_dev
                            tA[(f"dev_b{rnd % 2}", i)] = t_dev
                            def em_acc(e, j=j, rnd=rnd, si=si, gi=gi):
                                return e.dma_start(
                                    out=AP(ar_in.tensor, rnd * 2048 + j * 512,
                                           [[AR_COLS, 128], [1, 512]]),
                                    in_=dstg[gi % 2][:],
                                    accum_op=(ALU.bypass if si == 0 else ALU.add))
                            wacc = [t_dev]
                            if si >= 1:
                                wacc.append(tA[("acc", (si - 1, j, rnd))])
                            t_acc = p.step("gpsimd", em_acc, waits=wacc,
                                           sem="sda" if gi % 2 == 0 else "sda1", inc=16)
                            tA[("acc", (si, j, rnd))] = t_acc
                            tA[("dacc", (si, gi))] = t_acc
                            gi += 1
                        t_delta_last = tA[("dev", (si, gi - 1))]
                        tA[("acc_last", si)] = t_acc
                    tA[("psN01_free2", i)] = t_delta_last
                    tA[("featk_free", i)] = t_delta_last
                    tA[("featk_free", i - 1)] = t_delta_last
                    tA[("wmv_free", i)] = t_delta_last
                    tA[("wmv_free", i - 1)] = t_delta_last


            FA = {}
            for i in range(NT + 1):
                if i < NT:
                    FA[i] = emit_frontA(i)
                if i >= 1:
                    emit_backA(i - 1, FA[i - 1], FA.get(i))
                if i < NT:
                    FA[i].update(emit_frontA2(i, FA[i]))
                if i >= 1:
                    emit_tailA(i - 1, FA.get(i))
                if i < NT:
                    FA[i].update(emit_frontA2b(i, FA[i]))
                if i >= 1:
                    FA.pop(i - 1, None)

            t_dnev = p.step("vector", lambda e: e.tensor_copy(out=dstg[0][:, 0:96], in_=psDN[:]),
                            waits=[p.tok("T"), tA[("acc", (NSUP - 1, 3, 5))], tA[("acc", (NSUP - 1, 3, 4))]], sem="V", inc=1)
            t_dnst = p.step("gpsimd", lambda e: e.dma_start(
                out=AP(ar_in.tensor, 12288, [[AR_COLS, 128], [1, 96]]), in_=dstg[0][:, 0:96]),
                waits=[t_dnev], sem="sda", inc=16)
            if n_cores > 1:
                t_ar = p.step("gpsimd", lambda e: e.collective_compute(
                    "AllReduce", ALU.add, replica_groups=[list(range(n_cores))],
                    ins=[ar_in], outs=[ar_out]), waits=[t_dnst], sem="sar", inc=1)
            else:
                t_ar = p.step("gpsimd", lambda e: e.dma_start(out=ar_out, in_=ar_in),
                              waits=[t_dnst], sem="sar", inc=16)
            if debug_outs:
                t_dbgar = p.step("gpsimd", lambda e: e.dma_start(out=dbg_ar, in_=ar_out),
                                 waits=[t_ar], sem="sda", inc=16)

        t_rows_all = p.tok("sstr")
        t_bar1 = barrier()

        esQ = ExitStack()
        tQ = {}
        with esQ:
            wq_b = alloc("wq_b", (128, 16 * 2048), BF16, esQ)
            numa_rhs = alloc("numa_rhs", (128, NH * FC * 129), BF16, esQ)
            nn96_s = alloc("nn96_s", (128, 96), F32, esQ)
            arn = alloc("arn", (128, 96), F32, esQ)
            biasQ_s = alloc("biasQ_s", (128, 2048), BF16, esQ)
            xT_q = [alloc("xTq_0", (128, HID), F32, esQ)]
            ln1xTq = alloc("ln1xTq", (128, HID), BF16, esQ)
            qf = [alloc(f"qf_{k}", (128, 2048), F32, esQ) for k in range(2)]
            x2q = alloc("x2q", (128, 16 * 512), BF16, esQ)
            featq = alloc("featq", (128, NH * FD), BF16, esQ)
            mqT = alloc("mqT", (128, NH * FD), BF16, esQ)
            smq = alloc("smq", (128, 128), F32, esQ)
            mh_b = [alloc("mh_0", (128, 2048), F32, esQ)]
            arld = alloc("arld", (128, 512), F32, esQ)
            mmld = alloc("mmld", (128, 512), F32, esQ)


            psBCq = esQ.enter_context(nc.psum_tensor("psBCq", [128, 256], F32))
            psQb = [esQ.enter_context(nc.psum_tensor(f"psQ{k}", [128, 512], F32)) for k in range(2)]
            psTq = esQ.enter_context(nc.psum_tensor("psTq", [128, 1024], BF16))
            psAb = [esQ.enter_context(nc.psum_tensor(f"psA{k}", [128, 512], F32)) for k in range(4)]
            banks = psAb + psQb

            p.step("scalar", ld3d(wq_b, ATT_OFF["wq"], 2048, 16, 2048), waits=t_bar1 + [tok_nn], sem="sgQ", inc=16)
            p.step("sync", lambda e: e.dma_start(out=biasQ_s[:], in_=biasQ), waits=t_bar1, sem="sld", inc=16)
            tok_wQ = p.tok("sgQ")
            tok_bQ = p.tok("sld")
            t_arw = [p.tok("sar")]
            p.step("sync", lambda e: e.dma_start(out=nn96_s[:], in_=nn96), waits=t_bar1, sem="sarld", inc=16)
            cmb_done = {}
            for c2 in range(24):
                c, h4 = divmod(c2, 4)
                def em_arl(e, c=c, h4=h4):
                    return e.dma_start(out=arld[:],
                                       in_=AP(ar_out.tensor, c * 2048 + h4 * 512,
                                              [[AR_COLS, 128], [1, 512]]))
                def em_mml(e, c=c, h4=h4):
                    return e.dma_start(out=mmld[:],
                                       in_=AP(mem4.tensor, c * 128 * 2048 + h4 * 512,
                                              [[2048, 128], [1, 512]]))
                w = t_arw + t_bar1 + ([cmb_done[c2 - 1]] if c2 >= 1 else [])
                tl1 = p.step("sync", em_arl, waits=w, sem="sarld", inc=16)
                tl2 = p.step("sync", em_mml, waits=[tl1], sem="sarld", inc=16)
                def em_cmb(e, c=c, h4=h4):
                    return e.tensor_tensor(
                        out=AP(numa_rhs, h4 * 4 * FC * 129 + c * 129,
                               [[NH * FC * 129, 128], [FC * 129, 4], [1, 128]]),
                        in0=AP(mmld, 0, [[512, 128], [128, 4], [1, 128]]),
                        in1=AP(arld, 0, [[512, 128], [128, 4], [1, 128]]),
                        op=ALU.subtract)
                cmb_done[c2] = p.step("vector", em_cmb, waits=[tl2] + t_bar1, sem="V", inc=1)
            cmb_done[11] = cmb_done[23]
            t_arn = p.step("sync", lambda e: e.dma_start(
                out=arn[:], in_=AP(ar_out.tensor, 12288, [[AR_COLS, 128], [1, 96]])),
                waits=t_arw + t_bar1 + [p.tok("sarld")], sem="sarn", inc=16)
            def em_nn(e):
                return e.tensor_tensor(
                    out=AP(numa_rhs, 128, [[NH * FC * 129, 128], [FC * 129, 16], [129, 6]]),
                    in0=AP(nn96_s, 0, [[96, 128], [6, 16], [1, 6]]),
                    in1=AP(arn, 0, [[96, 128], [1, 16], [16, 6]]),
                    op=ALU.add)
            t_nn = p.step("vector", em_nn, waits=[t_arn, cmb_done[23]], sem="V", inc=1)
            t_rhs_ready = [t_nn, cmb_done[23]]

            def emit_frontQ(i):
                F = {}
                par = i % 2
                def ldxT(e, i=i):
                    return e.dma_start(out=xT_q[0][:],
                                       in_=AP(xT.tensor, i * 128,
                                              [[NTOK, 128], [128 * NTOK, 16], [1, 128]]))
                p.step("sync", ldxT, waits=[tQ.get(("xT_free", i - 1)), t_bar1[0]], sem="sx0", inc=16)
                t_x = p.tok("sx0")
                def ldrw(e, i=i, par=par):
                    return e.dma_start(out=rowb2[par][0:1, :],
                                       in_=AP(stats_dram.tensor, i * 256, [[256, 1], [1, 256]]))
                p.step("sync", ldrw, waits=[t_rows_all, tQ.get(("rowb_used", i - 2))],
                       sem=f"srw{par}", inc=16)
                t_rw = p.tok(f"srw{par}")
                def em_bc(e, par=par):
                    return e.matmul(psBCq[:, 0:256], onr[:],
                                    rowb2[par][0:1, 0:256], start=True, stop=True)
                t_bc = p.step("tensor", em_bc, waits=[tQ.get(("psBC_free", i - 1)), t_rw] + t_bar1,
                              sem="T", inc=1)
                tQ[("rowb_used", i)] = t_bc
                t_ln1a = p.step("vector", lambda e, par=par: e.tensor_tensor(
                    out=qf[par][:], in0=xT_q[0][:],
                    in1=AP(psBCq, 0, [[256, 128], [0, 16], [1, 128]]), op=ALU.mult),
                    waits=[t_bc, t_x, tQ.get(("qf_used", i - 2))], sem="V", inc=1)
                t_ln1 = p.step("vector", lambda e, par=par: e.tensor_tensor(
                    out=ln1xTq[:], in0=qf[par][:],
                    in1=AP(psBCq, 128, [[256, 128], [0, 16], [1, 128]]), op=ALU.add),
                    waits=[t_ln1a, tQ.get(("ln1_used", i - 1))], sem="V", inc=1)
                tQ[("psBC_free", i)] = t_ln1
                tQ[("xT_free", i)] = t_ln1a

                F.update(t_ln1=t_ln1, t_x=t_x)
                return F

            def emit_frontQ2(i, F):
                par = i % 2
                t_ln1 = F["t_ln1"]
                t_qev = []
                t_qmm = []
                for half in range(2):
                    def em_q(e, half=half):
                        inst = None
                        for c in range(16):
                            lh = ln1xTq[:, c * 128:(c + 1) * 128]
                            for s in range(2):
                                sl = half * 2 + s
                                inst = e.matmul(psQb[s][:], lh,
                                                wq_b[:, c * 2048 + sl * 512:c * 2048 + (sl + 1) * 512],
                                                start=(c == 0), stop=(c == 15))
                        return inst
                    w = [t_ln1, tok_wQ]
                    if half == 0:
                        w.append(tQ.get(("psQ_free", i - 1)))
                    else:
                        w.append(t_qev[0])
                    t_qmm.append(p.step("tensor", em_q, waits=w, sem="T", inc=1))
                    def em_qev(e, half=half, par=par):
                        e.tensor_tensor(out=qf[par][:, half * 1024:half * 1024 + 512], in0=psQb[0][:],
                                        in1=biasQ_s[:, half * 1024:half * 1024 + 512], op=ALU.add)
                        return e.tensor_tensor(out=qf[par][:, half * 1024 + 512:half * 1024 + 1024],
                                               in0=psQb[1][:],
                                               in1=biasQ_s[:, half * 1024 + 512:half * 1024 + 1024], op=ALU.add)
                    t_qev.append(p.step("vector", em_qev,
                                        waits=[t_qmm[half], tok_bQ], sem="V", inc=1))
                tQ[("ln1_used", i)] = t_qmm[1]
                tQ[("psQ_free", i)] = t_qev[1]
                return dict(t_qev1=t_qev[1])

            def emit_backQ(i, F):
                par = i % 2
                def em_relu(e, par=par):
                    e.activation(out=AP(x2q, 0, [[16 * 512, 128], [512, 16], [256, 2], [1, 128]]),
                                 in_=AP(qf[par], 0, [[2048, 128], [128, 16], [0, 2], [1, 128]]),
                                 func=AF.Relu)
                    return e.activation(out=AP(x2q, 128, [[16 * 512, 128], [512, 16], [256, 2], [1, 128]]),
                                        in_=AP(qf[par], 0, [[2048, 128], [128, 16], [0, 2], [1, 128]]),
                                        func=AF.Relu, scale=-1.0)
                t_relu = p.step("scalar", em_relu, waits=[F["t_qev1"], tQ.get(("x2q_used", i - 1))] + t_bar1,
                                sem="S", inc=1)
                tQ[("qf_used", i)] = t_relu
                def em_feat(e):
                    return e.tensor_tensor(
                        out=featq[:],
                        in0=AP(x2q, 256, [[16 * 512, 128], [512, 16], [0, 3], [1, 256]]),
                        in1=AP(x2q, 255, [[16 * 512, 128], [512, 16], [-1, 3], [1, 256]]),
                        op=ALU.mult)
                t_feat = p.step("vector", em_feat, waits=[t_relu, tQ.get(("featq_used", i - 1))], sem="V", inc=1)
                tQ[("x2q_used", i)] = t_feat
                t_n2 = [t_feat, tQ.get(("smq_used", i - 1))]
                for h in range(NH):
                    t_n2 = [p.step("scalar", lambda e, h=h: e.activation(
                        out=x2q[:, 0:FD], in_=featq[:, h * FD:(h + 1) * FD],
                        func=AF.Square, accum_out=smq[:, h:h + 1]),
                        waits=t_n2, sem="S", inc=1)]
                t_n2 = t_n2[0]
                t_sqn = p.step("scalar", lambda e: e.activation(out=smq[:, 16:32], in_=smq[:, 0:16], func=AF.Sqrt),
                               waits=[t_n2], sem="S", inc=1)
                t_iv = p.step("vector", lambda e: e.tensor_scalar(
                    out=smq[:, 16:32], in0=smq[:, 16:32], scalar1=EPS_L2, scalar2=None, op0=ALU.max),
                    waits=[t_sqn], sem="V", inc=1)
                t_invn = p.step("vector", lambda e: e.reciprocal(out=smq[:, 32:48], in_=smq[:, 16:32]),
                                waits=[t_iv], sem="V", inc=1)

                def mk_tpq(rnd):
                    def em_tp(e, rnd=rnd):
                        inst = None
                        for k in range(8):
                            ch = rnd * 8 + k
                            inst = e.matmul(psTq[:, k * 128:(k + 1) * 128],
                                            featq[:, ch * 128:(ch + 1) * 128],
                                            id_b[:], is_transpose=True,
                                            start=(k == 0), stop=(k == 7))
                        return inst
                    return em_tp

                def mk_tpqev(rnd):
                    def em_tev(e, rnd=rnd):
                        if rnd % 2:
                            return e.tensor_copy(out=mqT[:, rnd * 1024:(rnd + 1) * 1024], in_=psTq[:])
                        return e.copy(out=mqT[:, rnd * 1024:(rnd + 1) * 1024], in_=psTq[:])
                    return em_tev

                def mk_na(heads, b3):
                    def em_na(e, heads=heads, b3=b3):
                        pb = psAb[b3]
                        inst = None
                        first = True
                        for hh, h in enumerate(heads):
                            for c in range(6):
                                lh = mqT[:, (h * 6 + c) * 128:(h * 6 + c) * 128 + 128]
                                inst = e.matmul(pb[:, hh * 129:hh * 129 + 129], lh,
                                                numa_rhs[:, (h * 6 + c) * 129:(h * 6 + c) * 129 + 129],
                                                start=first, stop=(hh == len(heads) - 1 and c == 5))
                                first = False
                        return inst
                    return em_na

                tpev = {}
                prev = tQ.get(("mqT_used", i - 1))
                t_raw = {}
                t_na = []
                t_dgs = []
                na_blocks = [([0, 1, 2], 0), ([3, 4, 5], 1), ([6, 7], 2),
                             ([8, 9, 10], 0), ([11, 12, 13], 1), ([14, 15], 2)]
                next_na = 0
                for rnd in range(12):
                    t_tp = p.step("tensor", mk_tpq(rnd), waits=[t_feat, tpev.get(rnd - 1, prev)],
                                  sem="T", inc=1)
                    tpev[rnd] = p.step("vector" if rnd % 2 else "scalar", mk_tpqev(rnd),
                                       waits=[t_tp], sem="V" if rnd % 2 else "S", inc=1)
                    while next_na < 6 and (na_blocks[next_na][0][-1] * 6 + 5) // 8 <= rnd:
                        bi = next_na
                        heads, b3 = na_blocks[bi]
                        w = [tpev[rnd]] + t_rhs_ready
                        if bi < 3:
                            w.append(tQ.get(("na_used", (i - 1, bi % 3))))
                        else:
                            w.append(t_raw[bi - 3])
                            w.append(t_dgs[bi - 3])
                        t_na.append(p.step("tensor", mk_na(heads, b3), waits=w, sem="T", inc=1))
                        def em_raw(e, heads=heads, b3=b3):
                            return e.copy(
                                out=AP(mh_b[0], heads[0] * 128, [[2048, 128], [128, len(heads)], [1, 128]]),
                                in_=AP(psAb[b3], 0, [[512, 128], [129, len(heads)], [1, 128]]))
                        t_raw[bi] = p.step("scalar", em_raw,
                                           waits=[t_na[bi], tQ.get(("mh_free", i - 1)) if bi == 0 else None],
                                           sem="S", inc=1)
                        tQ[("na_used", (i, bi % 3))] = t_raw[bi]
                        def em_dgb(e, heads=heads, b3=b3):
                            return e.tensor_copy(out=smq[:, 48 + heads[0]:48 + heads[0] + len(heads)],
                                                 in_=AP(psAb[b3], 128, [[512, 128], [129, len(heads)]]))
                        t_dgs.append(p.step("vector", em_dgb,
                                            waits=[t_na[bi], tQ.get(("smq_used", i - 1)) if bi == 0 else None],
                                            sem="V", inc=1))
                        next_na += 1
                t_s = p.step("vector", lambda e: e.tensor_tensor(
                    out=smq[:, 64:80], in0=smq[:, 48:64], in1=smq[:, 32:48], op=ALU.mult),
                    waits=t_dgs + [t_invn], sem="V", inc=1)
                t_s = p.step("vector", lambda e: e.tensor_scalar(
                    out=smq[:, 64:80], in0=smq[:, 64:80], scalar1=EPS_AT, scalar2=None, op0=ALU.add),
                    waits=[t_s], sem="V", inc=1)
                t_s = p.step("vector", lambda e: e.reciprocal(out=smq[:, 80:96], in_=smq[:, 64:80]),
                             waits=[t_s], sem="V", inc=1)
                t_s = p.step("vector", lambda e: e.tensor_tensor(
                    out=smq[:, 96:112], in0=smq[:, 80:96], in1=smq[:, 32:48], op=ALU.mult),
                    waits=[t_s], sem="V", inc=1)
                def em_mh(e):
                    inst = None
                    for h in range(NH):
                        inst = e.tensor_scalar(out=mh_b[0][:, h * 128:(h + 1) * 128],
                                               in0=mh_b[0][:, h * 128:(h + 1) * 128],
                                               scalar1=smq[:, 96 + h:97 + h], scalar2=None,
                                               op0=ALU.mult)
                    return inst
                t_mh = p.step("vector", em_mh, waits=[t_s, t_raw[5]], sem="V", inc=1)
                tQ[("mqT_used", i)] = t_na[5]
                tQ[("smq_used", i)] = t_mh
                def em_mhst(e, i=i):
                    return e.dma_start(out=AP(mh_sp.tensor, i * 128 * 2048, [[2048, 128], [1, 2048]]),
                                       in_=mh_b[0][:])
                t_mhs = p.step("scalar", em_mhst, waits=[t_mh], sem="sstm", inc=16)
                tQ[("mh_free", i)] = t_mhs
                tQ[("mh_done", i)] = t_mhs

            FQ = {}
            for i in range(NT + 1):
                if i < NT:
                    FQ[i] = emit_frontQ(i)
                if i >= 1:
                    emit_backQ(i - 1, FQ[i - 1])
                if i < NT:
                    FQ[i].update(emit_frontQ2(i, FQ[i]))
                if i >= 1:
                    FQ.pop(i - 1, None)

        t_mh_all = p.tok("sstm")
        t_bar2 = barrier()

        esG = ExitStack()
        tG = {}
        with esG:
            wg_b = alloc("wg_b", (128, 16 * 2048), BF16, esG)
            biasG_s = alloc("biasG_s", (128, 2048), BF16, esG)
            xT_g = [alloc(f"xTg_{k}", (128, HID), F32, esG) for k in range(2)]
            x_g = [alloc(f"xg_{k}", (128, HID), F32, esG) for k in range(2)]
            mh_g = [alloc(f"mhg_{k}", (128, HID), F32, esG) for k in range(2)]
            ln1tg = alloc("ln1tg", (128, HID), F32, esG)
            ln1xTg = alloc("ln1xTg", (128, HID), BF16, esG)
            ln1f = alloc("ln1f", (128, HID), F32, esG)
            gsig = alloc("gsig", (128, HID), F32, esG)
            gpre = [alloc(f"gpre_{k}", (128, HID), F32, esG) for k in range(2)]
            att = alloc("att", (128, HID), F32, esG)
            attsq = alloc("attsq", (128, HID), F32, esG)
            y1b = alloc("y1b", (128, HID), F32, esG)
            ln2b = alloc("ln2b", (128, HID), BF16, esG)
            smg = alloc("smg", (128, 16), F32, esG)

            psBCg = esG.enter_context(nc.psum_tensor("psBCg", [128, 256], F32))
            psG = [esG.enter_context(nc.psum_tensor(f"psG{k}", [128, 512], F32)) for k in range(2)]

            p.step("scalar", ld3d(wg_b, ATT_OFF["wg"], 2048, 16, 2048), waits=t_bar2 + [tok_nn], sem="sgG", inc=16)
            tok_wG = p.tok("sgG")
            p.step("sync", lambda e: e.dma_start(out=biasG_s[:], in_=biasG), waits=t_bar2, sem="sld", inc=16)
            tok_bG = p.tok("sld")

            def emit_frontG(i):
                F = {}
                par = i % 2
                def ldxT(e, i=i, par=par):
                    return e.dma_start(out=xT_g[par][:],
                                       in_=AP(xT.tensor, i * 128,
                                              [[NTOK, 128], [128 * NTOK, 16], [1, 128]]))
                p.step("sync", ldxT, waits=[tG.get(("xT_free", i - 2)), t_bar2[0]], sem=f"sx{par}", inc=16)
                t_xT = p.tok(f"sx{par}")
                def ldx(e, i=i, par=par):
                    return e.dma_start(out=x_g[par][:],
                                       in_=AP(x_in.tensor, i * 128 * 2048, [[2048, 128], [1, 2048]]))
                p.step("sync", ldx, waits=[tG.get(("x_free", i - 2))], sem=f"sxx{par}", inc=16)
                t_xn = p.tok(f"sxx{par}")
                def ldmh(e, i=i, par=par):
                    return e.dma_start(out=mh_g[par][:],
                                       in_=AP(mh_sp.tensor, i * 128 * 2048, [[2048, 128], [1, 2048]]))
                p.step("sync", ldmh, waits=[t_mh_all, tG.get(("mhg_free", i - 2)),
                                            tG.get(("mhg_freeD", i - 2))],
                       sem=f"smh{par}", inc=16)
                t_mh = p.tok(f"smh{par}")
                def ldrw(e, i=i, par=par):
                    return e.dma_start(out=rowb2[par][0:1, :],
                                       in_=AP(stats_dram.tensor, i * 256, [[256, 1], [1, 256]]))
                p.step("sync", ldrw, waits=[tG.get(("rowb_used", i - 2)), t_rows_all, t_bar2[1]],
                       sem=f"srw{par}", inc=16)
                t_rw = p.tok(f"srw{par}")
                def em_bc(e, par=par):
                    return e.matmul(psBCg[:, 0:256], onr[:],
                                    rowb2[par][0:1, 0:256], start=True, stop=True)
                t_bc = p.step("tensor", em_bc, waits=[tG.get(("psBC_free", i - 1)), t_rw] + t_bar2,
                              sem="T", inc=1)
                tG[("rowb_used", i)] = t_bc
                t_ln1a = p.step("vector", lambda e, par=par: e.tensor_tensor(
                    out=ln1tg[:], in0=xT_g[par][:],
                    in1=AP(psBCg, 0, [[256, 128], [0, 16], [1, 128]]), op=ALU.mult),
                    waits=[t_bc, t_xT, tG.get(("ln1tg_used", i - 1))], sem="V", inc=1)
                t_ln1 = p.step("vector", lambda e: e.tensor_tensor(
                    out=ln1xTg[:], in0=ln1tg[:],
                    in1=AP(psBCg, 128, [[256, 128], [0, 16], [1, 128]]), op=ALU.add),
                    waits=[t_ln1a, tG.get(("ln1_used", i - 1))], sem="V", inc=1)
                tG[("psBC_free", i)] = t_ln1
                tG[("xT_free", i)] = t_ln1a
                tG[("ln1tg_used", i)] = t_ln1

                F.update(t_ln1=t_ln1, t_mh=t_mh, t_xn=t_xn)
                return F

            def emit_frontG2(i, F):
                par = i % 2
                t_ln1 = F["t_ln1"]
                t_gev = []
                t_gmm = []
                for half in range(2):
                    def em_g(e, half=half):
                        inst = None
                        for c in range(16):
                            lh = ln1xTg[:, c * 128:(c + 1) * 128]
                            for s in range(2):
                                sl = half * 2 + s
                                inst = e.matmul(psG[s][:], lh,
                                                wg_b[:, c * 2048 + sl * 512:c * 2048 + (sl + 1) * 512],
                                                start=(c == 0), stop=(c == 15))
                        return inst
                    w = [t_ln1, tok_wG]
                    if half == 0:
                        w.append(tG.get(("psG_free", i - 1)))
                    else:
                        w.append(t_gev[0])
                    t_gmm.append(p.step("tensor", em_g, waits=w, sem="T", inc=1))
                    def em_gev(e, half=half, par=par):
                        e.tensor_tensor(out=gpre[par][:, half * 1024:half * 1024 + 512], in0=psG[0][:],
                                        in1=biasG_s[:, half * 1024:half * 1024 + 512], op=ALU.add)
                        return e.tensor_tensor(out=gpre[par][:, half * 1024 + 512:half * 1024 + 1024],
                                               in0=psG[1][:],
                                               in1=biasG_s[:, half * 1024 + 512:half * 1024 + 1024], op=ALU.add)
                    t_gev.append(p.step("vector", em_gev,
                                        waits=[t_gmm[half], tok_bG,
                                               tG.get(("gpre_used", i - 2)) if half == 0 else None],
                                        sem="V", inc=1))
                tG[("ln1_used", i)] = t_gmm[1]
                tG[("psG_free", i)] = t_gev[1]
                return dict(t_gev1=t_gev[1])

            def emit_backG(i, F):
                par = i % 2
                t_sig = p.step("scalar", lambda e, par=par: e.activation(
                    out=gsig[:], in_=gpre[par][:], func=AF.Sigmoid),
                    waits=[F["t_gev1"], tG.get(("gsig_used", i - 1))], sem="S", inc=1)
                tG[("gpre_used", i)] = t_sig
                def em_ln1f(e, par=par, i=i):
                    return e.tensor_scalar(out=ln1f[:], in0=x_g[par][:],
                                           scalar1=stats_cols[:, i * 2:i * 2 + 1],
                                           scalar2=stats_cols[:, i * 2 + 1:i * 2 + 2],
                                           op0=ALU.mult, op1=ALU.add)
                t_ln1f = p.step("vector", em_ln1f, waits=[F["t_xn"], tG.get(("ln1f_used", i - 1))],
                                sem="V", inc=1)
                t_gm = p.step("vector", lambda e, par=par: e.tensor_tensor(
                    out=gsig[:], in0=gsig[:], in1=mh_g[par][:], op=ALU.mult),
                    waits=[t_sig, F["t_mh"]], sem="V", inc=1)
                t_att = p.step("vector", lambda e: e.tensor_tensor(
                    out=att[:], in0=ln1f[:], in1=gsig[:], op=ALU.add),
                    waits=[t_gm, t_ln1f, tG.get(("att_used", i - 1))], sem="V", inc=1)
                tG[("gsig_used", i)] = t_att
                tG[("ln1f_used", i)] = t_att
                tG[("mhg_free", i)] = t_gm
                if debug_outs:
                    tG[("mhg_freeD", i)] = p.step("scalar", lambda e, i=i, par=par: e.dma_start(
                        out=AP(dbg_mh.tensor, i * 128 * 2048, [[2048, 128], [1, 2048]]), in_=mh_g[par][:]),
                        waits=[t_gm], sem=f"sdbg{par}", inc=16)
                t_asq = p.step("scalar", lambda e: e.activation(out=attsq[:], in_=att[:], func=AF.Square,
                                                                accum_out=smg[:, 1:2]),
                               waits=[t_att, tG.get(("smg_used", i - 1))], sem="S", inc=1)
                t_as = p.step("vector", lambda e: e.reduce_sum(out=smg[:, 0:1], in_=att[:], axis=AX.X),
                              waits=[t_att], sem="V", inc=1)
                t_av = p.step("vector", lambda e: e.tensor_scalar(
                    out=smg[:, 2:4], in0=smg[:, 0:2], scalar1=1.0 / HID, scalar2=None, op0=ALU.mult),
                    waits=[t_as, t_asq], sem="V", inc=1)
                t_av = p.step("vector", lambda e: e.tensor_tensor(
                    out=smg[:, 4:5], in0=smg[:, 2:3], in1=smg[:, 2:3], op=ALU.mult),
                    waits=[t_av], sem="V", inc=1)
                t_av = p.step("vector", lambda e: e.tensor_tensor(
                    out=smg[:, 5:6], in0=smg[:, 3:4], in1=smg[:, 4:5], op=ALU.subtract),
                    waits=[t_av], sem="V", inc=1)
                t_av = p.step("vector", lambda e: e.tensor_scalar(
                    out=smg[:, 5:6], in0=smg[:, 5:6], scalar1=EPS_LN, scalar2=None, op0=ALU.add),
                    waits=[t_av], sem="V", inc=1)
                t_avs = p.step("scalar", lambda e: e.activation(out=smg[:, 6:7], in_=smg[:, 5:6], func=AF.Sqrt),
                               waits=[t_av], sem="S", inc=1)
                t_avr = p.step("vector", lambda e: e.reciprocal(out=smg[:, 7:8], in_=smg[:, 6:7]),
                               waits=[t_avs], sem="V", inc=1)
                t_y1a = p.step("vector", lambda e: e.tensor_scalar(
                    out=att[:], in0=att[:], scalar1=smg[:, 2:3], scalar2=smg[:, 7:8],
                    op0=ALU.subtract, op1=ALU.mult), waits=[t_avr], sem="V", inc=1)
                t_y1 = p.step("vector", lambda e, par=par: e.tensor_tensor(
                    out=y1b[:], in0=x_g[par][:], in1=att[:], op=ALU.add),
                    waits=[t_y1a, tG.get(("y1_used", i - 1)), tG.get(("y1_usedD", i - 1))],
                    sem="V", inc=1)
                tG[("att_used", i)] = t_y1
                tG[("x_free", i)] = t_y1
                t_y1sq = p.step("scalar", lambda e: e.activation(out=attsq[:], in_=y1b[:], func=AF.Square,
                                                                 accum_out=smg[:, 9:10]),
                                waits=[t_y1], sem="S", inc=1)
                tG[("smg_used", i)] = t_y1sq
                t_y1s = p.step("vector", lambda e: e.reduce_sum(out=smg[:, 8:9], in_=y1b[:], axis=AX.X),
                               waits=[t_y1], sem="V", inc=1)
                t_v2 = p.step("vector", lambda e: e.tensor_scalar(
                    out=smg[:, 10:12], in0=smg[:, 8:10], scalar1=1.0 / HID, scalar2=None, op0=ALU.mult),
                    waits=[t_y1s, t_y1sq], sem="V", inc=1)
                t_v2 = p.step("vector", lambda e: e.tensor_tensor(
                    out=smg[:, 12:13], in0=smg[:, 10:11], in1=smg[:, 10:11], op=ALU.mult),
                    waits=[t_v2], sem="V", inc=1)
                t_v2 = p.step("vector", lambda e: e.tensor_tensor(
                    out=smg[:, 13:14], in0=smg[:, 11:12], in1=smg[:, 12:13], op=ALU.subtract),
                    waits=[t_v2], sem="V", inc=1)
                t_v2 = p.step("vector", lambda e: e.tensor_scalar(
                    out=smg[:, 13:14], in0=smg[:, 13:14], scalar1=EPS_LN, scalar2=None, op0=ALU.add),
                    waits=[t_v2], sem="V", inc=1)
                t_v2s = p.step("scalar", lambda e: e.activation(out=smg[:, 14:15], in_=smg[:, 13:14], func=AF.Sqrt),
                               waits=[t_v2], sem="S", inc=1)
                t_v2r = p.step("vector", lambda e: e.reciprocal(out=smg[:, 15:16], in_=smg[:, 14:15]),
                               waits=[t_v2s], sem="V", inc=1)
                t_ln2 = p.step("vector", lambda e: e.tensor_scalar(
                    out=ln2b[:], in0=y1b[:], scalar1=smg[:, 10:11], scalar2=smg[:, 15:16],
                    op0=ALU.subtract, op1=ALU.mult),
                    waits=[t_v2r, tG.get(("ln2b_free", i - 1))], sem="V", inc=1)
                def em_y1st(e, i=i):
                    return e.dma_start(out=AP(y1_sp.tensor, i * 128 * 2048, [[2048, 128], [1, 2048]]), in_=y1b[:])
                t_y1st = p.step("scalar", em_y1st, waits=[t_y1], sem="ssty", inc=16)
                tG[("y1_used", i)] = t_y1st
                if debug_outs:
                    tG[("y1_usedD", i)] = p.step("scalar", lambda e, i=i: e.dma_start(
                        out=AP(dbg_y1.tensor, i * 128 * 2048, [[2048, 128], [1, 2048]]), in_=y1b[:]),
                        waits=[t_y1], sem="sdb2", inc=16)
                def em_l2st(e, i=i):
                    return e.dma_start(out=AP(ln2_sp.tensor, i * 128 * 2048, [[2048, 128], [1, 2048]]), in_=ln2b[:])
                t_l2st = p.step("scalar", em_l2st, waits=[t_ln2], sem="sstl", inc=16)
                tG[("ln2b_free", i)] = t_l2st
                tG[("ln2_done", i)] = t_l2st

            FG = {}
            for i in range(NT + 1):
                if i < NT:
                    FG[i] = emit_frontG(i)
                if i >= 1:
                    emit_backG(i - 1, FG[i - 1])
                if i < NT:
                    FG[i].update(emit_frontG2(i, FG[i]))
                if i >= 1:
                    FG.pop(i - 1, None)

        t_ln2_all = p.tok("sstl")
        t_y1_all = p.tok("ssty")
        t_bar3 = barrier()

        esF = ExitStack()
        tF = {}
        with esF:
            ICB = INTER // 2048
            ln2xT = alloc("ln2xT", (128, 16 * TG), BF16, esF)
            wgu_c = [alloc(f"wgu_{k}", (128, 16 * 128), BF16, esF) for k in range(4)]
            wd_cs = [alloc(f"wdcs_{k}", (128, 16 * 512), BF16, esF) for k in range(2)]
            hT = alloc("hT", (128, 16 * TG), BF16, esF)
            acc = alloc("acc", (128, 16 * TG), F32, esF)
            silu_t = alloc("silu_t", (128, 512), F32, esF)
            ffsq = alloc("ffsq", (128, 512), F32, esF)
            srow2 = alloc("srow2", (1, 1024), F32, esF)
            scp2 = alloc("scp2", (2, 128), F32, esF)
            stc2 = alloc("stc2", (128, 2), F32, esF)
            y1_f = [alloc("y1f_0", (128, 2048), F32, esF)]
            y2_b = [alloc("y2_0", (128, 2048), F32, esF)]

            psGU = [esF.enter_context(nc.psum_tensor(f"psGU{k}", [128, 512], F32)) for k in range(4)]
            psD = [esF.enter_context(nc.psum_tensor(f"psD{k}", [128, 512], F32)) for k in range(2)]
            psS2 = esF.enter_context(nc.psum_tensor("psS2", [128, 512], F32))
            psB2 = esF.enter_context(nc.psum_tensor("psB2", [128, 512], F32))

            for g in range(NG):
                for c in range(16):
                    def em_l2t(e, g=g, c=c):
                        return e.dma_start_transpose(
                            ln2xT[:, c * TG:(c + 1) * TG],
                            AP(ln2_sp.tensor, g * TG * 2048 + c * 128, [[2048, TG], [1, 128]]))
                    w = [t_ln2_all] + t_bar3
                    if g > 0:
                        w.append(tF[("ln2xT_free", g - 1)])
                    p.step("sync", em_l2t, waits=w, sem="sl2t", inc=16)
                t_l2t = p.tok("sl2t")

                h_hist = []
                for icb in range(ICB):
                    for s in range(16):
                        qb = s % 2
                        osl = icb * 2048 + s * 128
                        def em_wg(e, osl=osl, qb=qb):
                            return e.dma_start(
                                out=wgu_c[qb * 2][:],
                                in_=AP(ag_out.tensor, osl,
                                       [[2 * INTER, 128], [128 * 2 * INTER, 16], [1, 128]]))
                        def em_wu(e, osl=osl, qb=qb):
                            return e.dma_start(
                                out=wgu_c[qb * 2 + 1][:],
                                in_=AP(ag_out.tensor, INTER + osl,
                                       [[2 * INTER, 128], [128 * 2 * INTER, 16], [1, 128]]))
                        w = [t_ag, tF.get(("wgu_free", (g, icb, s - 2)))]
                        if s < 2:
                            w.append(tF.get(("wgu_free", (g, icb - 1, s + 14))) if icb > 0 else
                                     (tF.get(("wgu_free", (g - 1, ICB - 1, s + 14))) if g > 0 else None))
                        t_wg = p.step("sync", em_wg, waits=w, sem=f"sfg{qb * 2}", inc=16)
                        t_wu = p.step("sync", em_wu, waits=[], sem=f"sfg{qb * 2 + 1}", inc=16)
                        bs = psGU if s % 2 == 0 else [psD[0], psD[1], psS2, psB2]
                        def em_gu(e, qb=qb, bs=bs):
                            inst = None
                            for c in range(16):
                                for ts in range(TS):
                                    e.matmul(bs[ts][:, 0:TSW],
                                             wgu_c[qb * 2][:, c * 128:(c + 1) * 128],
                                             ln2xT[:, c * TG + ts * TSW:c * TG + (ts + 1) * TSW],
                                             start=(c == 0), stop=(c == 15))
                                for ts in range(TS):
                                    inst = e.matmul(bs[TS + ts][:, 0:TSW],
                                                    wgu_c[qb * 2 + 1][:, c * 128:(c + 1) * 128],
                                                    ln2xT[:, c * TG + ts * TSW:c * TG + (ts + 1) * TSW],
                                                    start=(c == 0), stop=(c == 15))
                            return inst
                        w = [t_wu, t_wg, t_l2t]
                        if icb == 0 and s == 0:
                            w.append(tF.get(("epi_done", g - 1)))
                        if s == 0:
                            w.append(tF.get(("down_done", (g, icb - 1))) if icb > 0 else
                                     (tF.get(("down_done", (g - 1, ICB - 1))) if g > 0 else None))
                        t_gu = p.step("tensor", em_gu, waits=w, sem="T", inc=1)
                        t_h_last = None
                        slb = [silu_t, ffsq]
                        for ts in range(TS):
                            t_si = p.step("scalar", lambda e, bs=bs, ts=ts: e.activation(
                                out=slb[ts][:, 0:TSW], in_=bs[ts][:, 0:TSW], func=AF.Silu),
                                waits=[t_gu, tF.get(("silu_free", ts))], sem="S", inc=1)
                            def em_h(e, s=s, ts=ts, bs=bs):
                                return e.tensor_tensor(
                                    out=hT[:, s * TG + ts * TSW:s * TG + (ts + 1) * TSW],
                                    in0=slb[ts][:, 0:TSW],
                                    in1=bs[TS + ts][:, 0:TSW], op=ALU.mult)
                            t_h = p.step("vector", em_h, waits=[t_si], sem="V", inc=1)
                            h_hist.append(t_h)
                            tF[("silu_free", ts)] = t_h
                            t_h_last = t_h
                        tF[("wgu_free", (g, icb, s))] = t_h_last
                    t_h_all = t_h_last

                    t_down_last = None
                    t_dmm_last = None
                    for hs in range(4):
                        qw = hs % 2
                        def em_wdl(e, g=g, icb=icb, hs=hs, qw=qw):
                            return e.dma_start(
                                out=wd_cs[qw][:],
                                in_=AP(ag_out.tensor, WGU_ELEMS + icb * 2048 * 2048 + hs * 512,
                                       [[2048, 128], [128 * 2048, 16], [1, 512]]))
                        w = [t_ag, tF.get(("wd_free", (g, icb, hs - 2)))]
                        if hs < 2:
                            w.append(tF.get(("wd_free", (g, icb - 1, hs + 2))) if icb > 0 else
                                     (tF.get(("wd_free", (g - 1, ICB - 1, hs + 2))) if g > 0 else None))
                        t_wd = p.step("sync", em_wdl, waits=w, sem=f"sfd{qw}", inc=16)
                        for hcl in range(4):
                            hc = hs * 4 + hcl
                            bp = [psD[0], psD[1]] if hc % 2 == 0 else [psS2, psB2]
                            def em_d(e, hcl=hcl, qw=qw, bp=bp):
                                inst = None
                                for ic in range(16):
                                    for ts in range(TS):
                                        inst = e.matmul(bp[ts][:, 0:TSW],
                                                        wd_cs[qw][:, ic * 512 + hcl * 128:ic * 512 + (hcl + 1) * 128],
                                                        hT[:, ic * TG + ts * TSW:ic * TG + (ts + 1) * TSW],
                                                        start=(ic == 0), stop=(ic == 15))
                                return inst
                            w = [t_wd, t_h_all,
                                 tF.get(("d_ev", (g, icb, (hc - 2) * TS))),
                                 tF.get(("d_ev", (g, icb, (hc - 2) * TS + TS - 1)))]
                            t_dmm = p.step("tensor", em_d, waits=w, sem="T", inc=1)
                            t_dmm_last = t_dmm
                            for ts in range(TS):
                                def em_dev(e, hc=hc, ts=ts, bp=bp, icb=icb):
                                    dst = acc[:, hc * TG + ts * TSW:hc * TG + (ts + 1) * TSW]
                                    if icb == 0:
                                        return e.tensor_copy(out=dst, in_=bp[ts][:, 0:TSW])
                                    return e.tensor_tensor(out=dst, in0=dst, in1=bp[ts][:, 0:TSW], op=ALU.add)
                                w2 = [t_dmm]
                                if icb == 0:
                                    w2.append(tF.get(("acc_free", g - 1)))
                                t_dev = p.step("vector", em_dev, waits=w2, sem="V", inc=1)
                                tF[("d_ev", (g, icb, hc * TS + ts))] = t_dev
                                t_down_last = t_dev
                        tF[("wd_free", (g, icb, hs))] = t_dmm_last
                    tF[("down_done", (g, icb))] = t_down_last
                    tF[("dmm_done", (g, icb))] = t_dmm_last
                tF[("ln2xT_free", g)] = h_hist[-1]
                t_acc_done = tF[("down_done", (g, ICB - 1))]

                t_epi_last = None
                for ts in range(TS):
                    t_mm_prev = tF.get(("psS2_free", (g, ts - 1))) or tF.get(("psS2_free", (g - 1, TS - 1)))
                    for hc in range(16):
                        t_q1 = p.step("scalar", lambda e, hc=hc, ts=ts: e.activation(
                            out=ffsq[:, 0:TSW], in_=acc[:, hc * TG + ts * TSW:hc * TG + (ts + 1) * TSW],
                            func=AF.Square),
                            waits=[t_acc_done, t_mm_prev], sem="S", inc=1)
                        def em_mm(e, hc=hc, ts=ts):
                            e.matmul(psS2[0:1, 0:TSW], onc[:],
                                     acc[:, hc * TG + ts * TSW:hc * TG + (ts + 1) * TSW],
                                     start=(hc == 0), stop=(hc == 15))
                            return e.matmul(psB2[0:1, 0:TSW], onc[:], ffsq[:, 0:TSW],
                                            start=(hc == 0), stop=(hc == 15))
                        t_mm_prev = p.step("tensor", em_mm, waits=[t_q1, t_acc_done], sem="T", inc=1)
                    t_stm = t_mm_prev
                    def em_f1(e, ts=ts):
                        e.tensor_scalar(out=srow2[0:1, 0:TSW], in0=psS2[0:1, 0:TSW],
                                        scalar1=1.0 / HID, scalar2=None, op0=ALU.mult)
                        return e.tensor_scalar(out=ffsq[0:1, 0:TSW], in0=psB2[0:1, 0:TSW],
                                               scalar1=1.0 / HID, scalar2=None, op0=ALU.mult)
                    t_f1 = p.step("vector", em_f1, waits=[t_stm], sem="V", inc=1)
                    t_f2 = p.step("vector", lambda e: e.tensor_tensor(
                        out=silu_t[0:1, 0:TSW], in0=srow2[0:1, 0:TSW],
                        in1=srow2[0:1, 0:TSW], op=ALU.mult), waits=[t_f1], sem="V", inc=1)
                    t_f2 = p.step("vector", lambda e: e.tensor_tensor(
                        out=ffsq[0:1, 0:TSW], in0=ffsq[0:1, 0:TSW],
                        in1=silu_t[0:1, 0:TSW], op=ALU.subtract), waits=[t_f2], sem="V", inc=1)
                    t_f2 = p.step("vector", lambda e: e.tensor_scalar(
                        out=ffsq[0:1, 0:TSW], in0=ffsq[0:1, 0:TSW],
                        scalar1=EPS_LN, scalar2=None, op0=ALU.add), waits=[t_f2], sem="V", inc=1)
                    t_f3 = p.step("scalar", lambda e: e.activation(out=ffsq[0:1, 0:TSW], in_=ffsq[0:1, 0:TSW],
                                                                   func=AF.Sqrt),
                                  waits=[t_f2], sem="S", inc=1)
                    t_f4 = p.step("vector", lambda e: e.reciprocal(out=srow2[0:1, TSW:2 * TSW],
                                                                   in_=ffsq[0:1, 0:TSW]),
                                  waits=[t_f3], sem="V", inc=1)
                    t_f5 = p.step("vector", lambda e: e.scalar_tensor_tensor(
                        out=srow2[0:1, 0:TSW], in0=srow2[0:1, 0:TSW], scalar=-1.0,
                        in1=srow2[0:1, TSW:2 * TSW], op0=ALU.mult, op1=ALU.mult),
                        waits=[t_f4], sem="V", inc=1)

                    t_sce = None
                    for tt in range(TSW // 128):
                        gi = g * TILES_G + ts * (TSW // 128) + tt
                        def em_sct(e, tt=tt):
                            e.matmul(psB2[0:128, 256:257],
                                     srow2[0:1, TSW + tt * 128:TSW + (tt + 1) * 128],
                                     id_f[0:1, 0:1], is_transpose=True, start=True, stop=False)
                            return e.matmul(psB2[0:128, 257:258],
                                            srow2[0:1, tt * 128:(tt + 1) * 128],
                                            id_f[0:1, 0:1], is_transpose=True, start=False, stop=True)
                        t_sct = p.step("tensor", em_sct,
                                       waits=[t_f5, t_f1 if tt == 0 else None,
                                              tF.get(("stc2ev_prev", 0))], sem="T", inc=1)
                        t_sce = p.step("vector", lambda e: e.tensor_copy(out=stc2[:], in_=psB2[0:128, 256:258]),
                                       waits=[t_sct, tF.get(("stc2_free", 0))], sem="V", inc=1)
                        tF[("scp2_free", 0)] = t_sce
                        tF[("stc2ev_prev", 0)] = t_sce
                        def em_y1l(e, gi=gi):
                            return e.dma_start(out=y1_f[0][:],
                                               in_=AP(y1_sp.tensor, gi * 128 * 2048, [[2048, 128], [1, 2048]]))
                        p.step("sync", em_y1l, waits=[t_y1_all, tF.get(("y1f_free", 0))],
                               sem="sy10", inc=16)
                        t_y1l = p.tok("sy10")
                        t_bt = []
                        for bb in range(4):
                            def em_bt(e, bb=bb, ts=ts, tt=tt):
                                inst = None
                                for k in range(4):
                                    hc = bb * 4 + k
                                    inst = e.matmul(psGU[bb][:, k * 128:(k + 1) * 128],
                                                    acc[:, hc * TG + ts * TSW + tt * 128:hc * TG + ts * TSW + (tt + 1) * 128],
                                                    id_f[:], is_transpose=True,
                                                    start=(k == 0), stop=(k == 3))
                                return inst
                            w = [t_acc_done, tF.get(("y2_prev", 0))]
                            if ts == 0 and tt == 0:
                                w.append(h_hist[-1])
                            t_bt.append(p.step("tensor", em_bt, waits=w, sem="T", inc=1))
                        def em_y2a(e):
                            inst = None
                            for bb in range(4):
                                inst = e.tensor_scalar(out=y2_b[0][:, bb * 512:(bb + 1) * 512], in0=psGU[bb][:],
                                                       scalar1=stc2[:, 0:1], scalar2=stc2[:, 1:2],
                                                       op0=ALU.mult, op1=ALU.add)
                            return inst
                        t_y2a = p.step("vector", em_y2a,
                                       waits=t_bt + [t_sce, tF.get(("y2st_prev", 0))],
                                       sem="V", inc=1)
                        t_y2 = p.step("vector", lambda e: e.tensor_tensor(
                            out=y2_b[0][:], in0=y2_b[0][:], in1=y1_f[0][:], op=ALU.add),
                            waits=[t_y2a, t_y1l], sem="V", inc=1)
                        tF[("y2_prev", 0)] = t_y2
                        tF[("y1f_free", 0)] = t_y2
                        tF[("stc2_free", 0)] = t_y2
                        def em_y2st(e, gi=gi):
                            return e.dma_start(out=AP(y2_out.tensor, gi * 128 * 2048, [[2048, 128], [1, 2048]]),
                                               in_=y2_b[0][:])
                        t_y2s = p.step("scalar", em_y2st, waits=[t_y2], sem="sout", inc=16)
                        tF[("y2st_prev", 0)] = t_y2s
                        t_epi_last = t_y2s
                    tF[("psS2_free", (g, ts))] = t_sce
                tF[("epi_done", g)] = t_epi_last
                tF[("acc_free", g)] = t_epi_last

            p.step("sync", lambda e: e.nop(),
                   waits=[p.tok("sout"), p.tok("sstm"), p.tok("ssty"), p.tok("sstl"),
                          p.tok("sstr"), p.tok("sdbg0"), p.tok("sdbg1"), p.tok("sdb2"), p.tok("sda")],
                   sem=None, inc=0)

        with nc.Block() as blk:
            blk.sync(lambda e: p.emit_engine("sync", e))
            blk.gpsimd(lambda e: p.emit_engine("gpsimd", e))
            blk.tensor(lambda e: p.emit_engine("tensor", e))
            blk.vector(lambda e: p.emit_engine("vector", e))
            blk.scalar(lambda e: p.emit_engine("scalar", e))

    return nc


_NC_CACHE = {}


def _prep_inputs(inputs, n_cores, n_tiles):
    NTOK_ALL = n_cores * n_tiles * 128
    f32 = np.float32
    X = np.asarray(inputs["L0_attn_hidden_states"], f32).reshape(NTOK_ALL, HID)
    XT = np.ascontiguousarray(X.T)
    Wq = np.asarray(inputs["L0_attn_Wq"], f32)
    Wk = np.asarray(inputs["L0_attn_Wk"], f32)
    Wv = np.asarray(inputs["L0_attn_Wv"], f32)
    Wg = np.asarray(inputs["L0_attn_Wg"], f32)
    Wmb = np.asarray(inputs["L0_attn_Wmb"], f32)
    bq = np.asarray(inputs["L0_attn_bq"], f32)
    bk = np.asarray(inputs["L0_attn_bk"], f32)
    bv = np.asarray(inputs["L0_attn_bv"], f32)
    bg = np.asarray(inputs["L0_attn_bg"], f32)
    bmb = np.asarray(inputs["L0_attn_bmb"], f32)
    memory = np.asarray(inputs["L0_attn_memory"], f32)
    norm = np.asarray(inputs["L0_attn_norm"], f32)
    Wgu = np.asarray(inputs["L0_ffn_w_gate_up"], f32)
    Wd = np.asarray(inputs["L0_ffn_w_down"], f32)

    mem4_h = np.ascontiguousarray(memory.transpose(1, 0, 2).reshape(FD, HID))
    shared = {
        "watt": np.concatenate([np.ascontiguousarray(Wk.T).ravel(),
                                np.ascontiguousarray(Wv.T).ravel(),
                                np.ascontiguousarray(Wmb.T).ravel(),
                                mem4_h.ravel(),
                                np.ascontiguousarray(Wq.T).ravel(),
                                np.ascontiguousarray(Wg.T).ravel()]),
        "mem4": mem4_h,
        "nn96": np.ascontiguousarray(norm.reshape(NH, FC, 128).transpose(2, 0, 1).reshape(128, 96)),
        "nn96b": np.ascontiguousarray(norm.reshape(NH, FC, 128).transpose(2, 1, 0).reshape(128, 96)),
        "biasA": np.ascontiguousarray(np.broadcast_to(
            np.concatenate([bk, bv, bmb])[None, :], (128, 3072))).astype(ml_dtypes.bfloat16),
        "biasQ": np.ascontiguousarray(np.broadcast_to(bq[None, :], (128, HID))).astype(ml_dtypes.bfloat16),
        "biasG": np.ascontiguousarray(np.broadcast_to(bg[None, :], (128, HID))).astype(ml_dtypes.bfloat16),
        "idb": np.eye(128, dtype=ml_dtypes.bfloat16),
        "id32": np.eye(128, dtype=f32),
        "ones_col": np.ones((128, 1), f32),
        "ones_row": np.ones((1, 128), f32),
    }
    wflat = np.concatenate([np.ascontiguousarray(Wgu.T).ravel(),
                            np.ascontiguousarray(Wd.T).ravel()])
    SLICE = WTOT // n_cores
    NTOK = n_tiles * 128
    in_maps = []
    for c in range(n_cores):
        m = dict(shared)
        m["xT"] = np.ascontiguousarray(XT[:, c * NTOK:(c + 1) * NTOK])
        m["x_in"] = np.ascontiguousarray(X[c * NTOK:(c + 1) * NTOK])
        m["wconv"] = np.ascontiguousarray(wflat[c * SLICE:(c + 1) * SLICE].reshape(128, SLICE // 128))
        in_maps.append(m)
    return in_maps


def kernel(**inputs):
    n_cores, n_tiles = 8, 16
    key = (n_cores, n_tiles)
    if key not in _NC_CACHE:
        _NC_CACHE[key] = build_program(n_cores=n_cores, n_tiles=n_tiles)
    nc = _NC_CACHE[key]
    in_maps = _prep_inputs(inputs, n_cores, n_tiles)
    res = run_bass_kernel_spmd(nc, in_maps, core_ids=list(range(n_cores)))
    out = np.concatenate([res.results[c]["y2"] for c in range(n_cores)], axis=0)
    return out.reshape(2, 8192, HID)


# version 20
# speedup vs baseline: 1.0519x; 1.0460x over previous
import numpy as np
import ml_dtypes
import concourse.bass as bass
import concourse.mybir as mybir
from concourse.bass_utils import run_bass_kernel_spmd
from concourse.ap import AP
from contextlib import ExitStack

import concourse.bass_utils as _bu
_orig_gwa = _bu.get_walrus_args
def _gwa(*a, **k):
    return [x.replace("--enable-ldw-opt=false", "--enable-ldw-opt=true") if isinstance(x, str) else x
            for x in _orig_gwa(*a, **k)]
_bu.get_walrus_args = _gwa

F32 = mybir.dt.float32
BF16 = mybir.dt.bfloat16
AF = mybir.ActivationFunctionType
ALU = mybir.AluOpType

HID = 2048
NH = 16
FD = 768
FC = 6
EPS_LN = 1e-5
EPS_AT = 1e-8
EPS_L2 = 1e-12
INTER = 8192
AR_COLS = 12288 + 96
WGU_ELEMS = HID * 2 * INTER
WD_ELEMS = INTER * HID
WTOT = WGU_ELEMS + WD_ELEMS
AX = mybir.AxisListType


class Prog:
    def __init__(self, nc, es, sem_names):
        self.nc = nc
        self.sems = {n: es.enter_context(nc.semaphore(n)) for n in sem_names}
        self.counts = {n: 0 for n in sem_names}
        self.steps = []
        self.last_wait = {}

    def step(self, eng, emit, waits=(), sem=None, inc=0):
        tok = None
        if sem is not None:
            self.counts[sem] += inc
            tok = (sem, self.counts[sem])
        self.steps.append((eng, tuple(w for w in waits if w is not None), emit, sem, inc))
        return tok

    def tok(self, sem):
        return (sem, self.counts[sem])

    def emit_engine(self, eng_name, eng):
        for e, waits, emit, sem, inc in self.steps:
            if e != eng_name:
                continue
            for (ws, wv) in waits:
                if wv <= 0:
                    continue
                if self.last_wait.get((eng_name, ws), -1) >= wv:
                    continue
                self.last_wait[(eng_name, ws)] = wv
                eng.wait_ge(self.sems[ws], wv)
            inst = emit(eng)
            if sem is not None and inc > 0:
                assert inst is not None
                inst.then_inc(self.sems[sem], inc)


def build_program(n_cores=8, n_tiles=16, debug_outs=False):
    NT = n_tiles
    NTOK = NT * 128
    SUP = 2
    NSUP = NT // SUP
    TG = min(1024, NTOK)
    NG = NTOK // TG
    TILES_G = TG // 128
    TS = (TG + 511) // 512
    TSW = TG // TS

    nc = bass.Bass("TRN2", target_bir_lowering=False)

    def ein(name, shape, dt=F32):
        return nc.dram_tensor(name, list(shape), dt, kind="ExternalInput").ap()

    xT = ein("xT", (HID, NTOK))
    x_in = ein("x_in", (NTOK, HID))
    wkT = ein("wkT", (HID, 512))
    wvT = ein("wvT", (HID, 512))
    wmbT = ein("wmbT", (HID, HID))
    wqT = ein("wqT", (HID, HID))
    wgT = ein("wgT", (HID, HID))
    mem4 = ein("mem4", (FD, HID))
    nn96 = ein("nn96", (128, 96))
    nn96b = ein("nn96b", (128, 96))
    biasA = ein("biasA", (128, 3072), BF16)
    biasQ = ein("biasQ", (128, HID), BF16)
    biasG = ein("biasG", (128, HID), BF16)
    wconv = ein("wconv", (128, WTOT // n_cores // 128))
    idb = ein("idb", (128, 128), BF16)
    id32 = ein("id32", (128, 128), F32)
    ones_col = ein("ones_col", (128, 1), F32)
    ones_row = ein("ones_row", (1, 128), F32)

    y2_out = nc.dram_tensor("y2", [NTOK, HID], F32, kind="ExternalOutput").ap()
    if debug_outs:
        dbg_ar = nc.dram_tensor("dbg_ar", [128, AR_COLS], F32, kind="ExternalOutput").ap()
        dbg_mh = nc.dram_tensor("dbg_mh", [NTOK, HID], F32, kind="ExternalOutput").ap()
        dbg_y1 = nc.dram_tensor("dbg_y1", [NTOK, HID], F32, kind="ExternalOutput").ap()

    ag_in = nc.dram_tensor("ag_in", [WTOT // n_cores], BF16, kind="Internal").ap()
    ag_out = nc.dram_tensor("ag_out", [WTOT], BF16, kind="Internal").ap()
    ar_in = nc.dram_tensor("ar_in", [128, AR_COLS], F32, kind="Internal").ap()
    ar_out = nc.dram_tensor("ar_out", [128, AR_COLS], F32, kind="Internal").ap()
    mh_sp = nc.dram_tensor("mh_sp", [NTOK, HID], F32, kind="Internal").ap()
    y1_sp = nc.dram_tensor("y1_sp", [NTOK, HID], F32, kind="Internal").ap()
    ln2_sp = nc.dram_tensor("ln2_sp", [NTOK, HID], BF16, kind="Internal").ap()
    stats_dram = nc.dram_tensor("stats_dram", [NT, 256], F32, kind="Internal").ap()

    sem_names = ["sx0", "sx1", "sxx0", "sxx1", "sld", "sgA", "sgQ", "sgG",
                 "scv0", "scv1", "scs0", "scs1", "sag", "sar", "sda",
                 "sstm", "ssty", "sstl", "sstr", "sdbg", "smh0", "smh1", "sda1",
                 "sy10", "sl2t", "sfg0", "sfg1", "sfg2", "sfg3", "sfd0", "sfd1",
                 "sout", "sarld", "sarn", "srw0", "srw1", "sdbg0", "sdbg1", "sdb2",
                 "T", "V", "S"]

    es = ExitStack()
    with es:
        p = Prog(nc, es, sem_names)

        def alloc(name, shape, dt=F32, stack=None):
            return (stack or es).enter_context(nc.sbuf_tensor(name, list(shape), dt))

        id_b = alloc("id_b", (128, 128), BF16)
        id_f = alloc("id_f", (128, 128), F32)
        onc = alloc("onc", (128, 1), F32)
        onr = alloc("onr", (1, 128), F32)
        stats_cols = alloc("stats_cols", (128, NT * 2), F32)
        srow = alloc("srow", (1, 384), F32)
        rowbuf = alloc("rowbuf", (1, 256), F32)
        rowb2 = [alloc(f"rowb2_{k}", (1, 256), F32) for k in range(2)]
        dummyV = alloc("dummyV", (1, 8), F32)
        dummyS = alloc("dummyS", (1, 8), F32)

        p.step("sync", lambda e: e.dma_start(out=id_b[:], in_=idb), sem="sld", inc=16)
        p.step("sync", lambda e: e.dma_start(out=id_f[:], in_=id32), sem="sld", inc=16)
        p.step("sync", lambda e: e.dma_start(out=onc[:], in_=ones_col), sem="sld", inc=16)
        p.step("sync", lambda e: e.dma_start(out=onr[:], in_=ones_row), sem="sld", inc=16)
        t_ld0 = p.tok("sld")

        def cast3d(dst, src, cols, nchunk, stride):
            return lambda e: e.dma_start(
                out=dst[:, 0:nchunk * cols],
                in_=AP(src.tensor, 0, [[stride, 128], [128 * stride, nchunk], [1, cols]]))

        def barrier():
            stores = [p.tok(s) for s in ("sstm", "ssty", "sstl", "sstr",
                                         "sdbg0", "sdbg1", "sdb2", "sda", "scs0", "scs1")]
            tT = p.step("tensor", lambda e: e.ldweights(id_b[:]),
                waits=[p.tok("V"), p.tok("S")] + stores, sem="T", inc=1)
            tV = p.step("vector", lambda e: e.memset(dummyV[0:1, 0:1], 0.0),
                        waits=[p.tok("T")], sem="V", inc=1)
            tS = p.step("scalar", lambda e: e.memzero(dummyS[0:1, 0:1]),
                        waits=[p.tok("T")], sem="S", inc=1)
            return [tT, tV, tS, p.tok("V"), p.tok("S")]


        esA = ExitStack()
        tA = {}
        with esA:
            wk_b = alloc("wk_b", (128, 16 * 512), BF16, esA)
            wv_b = alloc("wv_b", (128, 16 * 512), BF16, esA)
            wmb_b = alloc("wmb_b", (128, 16 * 2048), BF16, esA)
            mem4_b = alloc("mem4_b", (128, FC * 2048), BF16, esA)
            nn96b_b = alloc("nn96b_b", (128, 96), BF16, esA)
            biasA_s = alloc("biasA_s", (128, 3072), BF16, esA)
            xT_t = [alloc("xT_0", (128, HID), F32, esA)]
            scr = alloc("scr", (128, HID), F32, esA)
            ln1xT = [alloc(f"ln1xT_{k}", (128, HID), BF16, esA) for k in range(2)]
            kf = alloc("kf", (128, 512), F32, esA)
            vf = [alloc(f"vf_{k}", (128, 512), F32, esA) for k in range(2)]
            mbp = alloc("mbp", (128, 2048), F32, esA)
            x2k = alloc("x2k", (128, 4 * 512), BF16, esA)
            featk = [alloc(f"featk_{k}", (128, 4 * FD), BF16, esA) for k in range(2)]
            mkT = alloc("mkT", (128, 4 * FD), BF16, esA)
            wmv4 = [alloc(f"wmv4_{k}", (128, 4 * 512), BF16, esA) for k in range(2)]
            coefb = alloc("coefb", (128, 16), BF16, esA)
            sm = alloc("sm", (128, 80), F32, esA)
            pvt = alloc("pvt", (128, 512), F32, esA)
            dstg = [alloc(f"dstg_{k}", (128, 512), F32, esA) for k in range(2)]

            psStat = esA.enter_context(nc.psum_tensor("psStat", [128, 512], F32))
            psBC = esA.enter_context(nc.psum_tensor("psBC", [128, 256], F32))
            psN = [esA.enter_context(nc.psum_tensor(f"psN{k}", [128, 512], F32)) for k in range(4)]
            psT = esA.enter_context(nc.psum_tensor("psT", [128, 1024], BF16))
            psDN = esA.enter_context(nc.psum_tensor("psDN", [128, 96], F32))

            p.step("gpsimd", cast3d(wk_b, wkT, 512, 16, 512), sem="sgA", inc=16)
            p.step("gpsimd", cast3d(wv_b, wvT, 512, 16, 512), sem="sgA", inc=16)
            p.step("gpsimd", cast3d(wmb_b, wmbT, 2048, 16, 2048), sem="sgA", inc=16)
            p.step("gpsimd", cast3d(mem4_b, mem4, 2048, 6, 2048), sem="sgA", inc=16)
            p.step("gpsimd", lambda e: e.dma_start(out=nn96b_b[:], in_=nn96b), sem="sgA", inc=16)
            p.step("gpsimd", lambda e: e.dma_start(out=biasA_s[:], in_=biasA), sem="sgA", inc=16)
            tok_wA = p.tok("sgA")

            SLICE = WTOT // n_cores
            t_cv = p.step("gpsimd", lambda e: e.dma_start(out=ag_in, in_=wconv), sem="scv0", inc=16)
            t_conv = [t_cv]
            if n_cores > 1:
                t_ag = p.step("gpsimd", lambda e: e.collective_compute(
                    "AllGather", ALU.bypass, replica_groups=[list(range(n_cores))],
                    ins=[ag_in], outs=[ag_out]), waits=t_conv, sem="sag", inc=1)
            else:
                t_ag = p.step("gpsimd", lambda e: e.dma_start(out=ag_out, in_=ag_in),
                              waits=t_conv, sem="sag", inc=16)

            def emit_frontA(i):
                F = {}
                par = i % 2
                def ldxT(e, i=i):
                    return e.dma_start(out=xT_t[0][:],
                                       in_=AP(xT.tensor, i * 128,
                                              [[NTOK, 128], [128 * NTOK, 16], [1, 128]]))
                p.step("sync", ldxT, waits=[tA.get(("xT_free", i - 1))], sem="sx0", inc=16)
                t_x = p.tok("sx0")

                t_sq = p.step("scalar", lambda e: e.activation(out=scr[:], in_=xT_t[0][:], func=AF.Square),
                              waits=[t_x, tA.get(("scr_free", i - 1))], sem="S", inc=1)
                def em_stats(e):
                    for c in range(16):
                        e.matmul(psStat[0:1, 0:128], onc[:], xT_t[0][:, c * 128:(c + 1) * 128],
                                 start=(c == 0), stop=False)
                    inst = None
                    for c in range(16):
                        inst = e.matmul(psStat[0:1, 128:256], onc[:], scr[:, c * 128:(c + 1) * 128],
                                        start=False, stop=(c == 15))
                    return inst
                t_st = p.step("tensor", em_stats,
                              waits=[t_x, t_sq, t_ld0, tA.get(("psStat_free", i - 1))], sem="T", inc=1)
                t1a = p.step("vector", lambda e: e.tensor_scalar(
                    out=srow[0:1, 0:256], in0=psStat[0:1, 0:256],
                    scalar1=1.0 / HID, scalar2=None, op0=ALU.mult), waits=[t_st], sem="V", inc=1)
                t1 = p.step("vector", lambda e: e.tensor_tensor(
                    out=srow[0:1, 256:384], in0=srow[0:1, 0:128],
                    in1=srow[0:1, 0:128], op=ALU.mult), waits=[t1a], sem="V", inc=1)
                t2a = p.step("vector", lambda e: e.tensor_tensor(
                    out=srow[0:1, 128:256], in0=srow[0:1, 128:256],
                    in1=srow[0:1, 256:384], op=ALU.subtract), waits=[t1], sem="V", inc=1)
                t2 = p.step("vector", lambda e: e.tensor_scalar(
                    out=srow[0:1, 256:384], in0=srow[0:1, 128:256],
                    scalar1=EPS_LN, scalar2=None, op0=ALU.add), waits=[t2a], sem="V", inc=1)
                t3 = p.step("scalar", lambda e: e.activation(out=srow[0:1, 256:384],
                                                             in_=srow[0:1, 256:384], func=AF.Sqrt),
                            waits=[t2], sem="S", inc=1)
                t4a = p.step("vector", lambda e: e.reciprocal(
                    out=rowbuf[0:1, 0:128], in_=srow[0:1, 256:384]),
                    waits=[t3, tA.get(("rowbuf_used", i - 1)), tA.get(("rowbuf_used2", i - 1))],
                    sem="V", inc=1)
                t4 = p.step("vector", lambda e: e.scalar_tensor_tensor(
                    out=rowbuf[0:1, 128:256], in0=srow[0:1, 0:128], scalar=-1.0,
                    in1=rowbuf[0:1, 0:128], op0=ALU.mult, op1=ALU.mult),
                    waits=[t4a], sem="V", inc=1)
                t_rst = p.step("scalar", lambda e, i=i: e.dma_start(
                    out=AP(stats_dram.tensor, i * 256, [[256, 1], [1, 256]]), in_=rowbuf[0:1, :]),
                    waits=[t4], sem="sstr", inc=16)
                tA[("rows_stored", i)] = t_rst
                def em_stt(e, i=i):
                    e.matmul(psStat[0:128, 256:257], rowbuf[0:1, 0:128], id_f[0:1, 0:1],
                             is_transpose=True, start=True, stop=False)
                    e.matmul(psStat[0:128, 257:258], rowbuf[0:1, 128:256], id_f[0:1, 0:1],
                             is_transpose=True, start=False, stop=True)
                    return e.matmul(psBC[:, 0:256], onr[:],
                                    rowbuf[0:1, 0:256], start=True, stop=True)
                t_bc = p.step("tensor", em_stt, waits=[t4, tA.get(("psBC_free", i - 1))], sem="T", inc=1)
                tA[("rowbuf_used", i)] = t_rst
                tA[("rowbuf_used2", i)] = t_bc
                t_sc = p.step("vector", lambda e, i=i: e.tensor_copy(
                    out=stats_cols[:, i * 2:(i + 1) * 2], in_=psStat[0:128, 256:258]),
                    waits=[t_bc], sem="V", inc=1)

                t_ln1a = p.step("vector", lambda e: e.tensor_tensor(
                    out=scr[:], in0=xT_t[0][:],
                    in1=AP(psBC, 0, [[256, 128], [0, 16], [1, 128]]), op=ALU.mult),
                    waits=[t_bc, t_st], sem="V", inc=1)
                t_ln1 = p.step("vector", lambda e, par=par: e.tensor_tensor(
                    out=ln1xT[par][:], in0=scr[:],
                    in1=AP(psBC, 128, [[256, 128], [0, 16], [1, 128]]), op=ALU.add),
                    waits=[t_ln1a, tA.get(("ln1xT_used", i - 2))], sem="V", inc=1)
                tA[("xT_free", i)] = t_ln1a
                tA[("scr_free", i)] = t_ln1
                tA[("psBC_free", i)] = t_ln1

                F.update(t_ln1=t_ln1, t_ln1a=t_ln1a, t_st=t_st, t_sc=t_sc, par=par)
                return F

            def emit_frontA2(i, F):
                par = i % 2
                t_ln1 = F["t_ln1"]
                def em_kv(e, par=par):
                    inst = None
                    for c in range(16):
                        lh = ln1xT[par][:, c * 128:(c + 1) * 128]
                        e.matmul(psN[0][:], lh, wk_b[:, c * 512:(c + 1) * 512],
                                 start=(c == 0), stop=(c == 15))
                        inst = e.matmul(psN[1][:], lh, wv_b[:, c * 512:(c + 1) * 512],
                                        start=(c == 0), stop=(c == 15))
                    return inst
                t_kv = p.step("tensor", em_kv,
                              waits=[t_ln1, tok_wA, tA.get(("psN01_free2", i - 1))], sem="T", inc=1)
                def em_kvev(e, par=par):
                    e.tensor_tensor(out=kf[:], in0=psN[0][:], in1=biasA_s[:, 0:512], op=ALU.add)
                    return e.tensor_tensor(out=vf[par][:], in0=psN[1][:], in1=biasA_s[:, 512:1024], op=ALU.add)
                t_kvev = p.step("vector", em_kvev,
                                waits=[t_kv, tA.get(("kf_used", i - 1)), tA.get(("vf_used", i - 2))],
                                sem="V", inc=1)
                def em_relu(e):
                    e.activation(out=AP(x2k, 0, [[4 * 512, 128], [512, 4], [256, 2], [1, 128]]),
                                 in_=AP(kf, 0, [[512, 128], [128, 4], [0, 2], [1, 128]]),
                                 func=AF.Relu)
                    return e.activation(out=AP(x2k, 128, [[4 * 512, 128], [512, 4], [256, 2], [1, 128]]),
                                        in_=AP(kf, 0, [[512, 128], [128, 4], [0, 2], [1, 128]]),
                                        func=AF.Relu, scale=-1.0)
                t_relu = p.step("scalar", em_relu, waits=[t_kvev, tA.get(("x2k_used", i - 1))], sem="S", inc=1)
                tA[("kf_used", i)] = t_relu
                F.update(t_kvev=t_kvev, t_relu=t_relu)
                return F

            def emit_frontA2b(i, F):
                par = i % 2
                t_relu = F["t_relu"]
                def em_feat(e, par=par):
                    return e.tensor_tensor(
                        out=featk[par][:],
                        in0=AP(x2k, 256, [[4 * 512, 128], [512, 4], [0, 3], [1, 256]]),
                        in1=AP(x2k, 255, [[4 * 512, 128], [512, 4], [-1, 3], [1, 256]]),
                        op=ALU.mult)
                t_feat = p.step("vector", em_feat,
                                waits=[t_relu, tA.get(("featk_free", i - 2))], sem="V", inc=1)
                tA[("x2k_used", i)] = t_feat
                F.update(t_feat=t_feat)
                return F

            def emit_backA(i, F, Fnext):
                par = i % 2
                t_feat = F["t_feat"]

                def mk_tp(rnd):
                    def em_tp(e, rnd=rnd, par=par):
                        inst = None
                        for k in range(8):
                            ch = rnd * 8 + k
                            inst = e.matmul(psT[:, k * 128:(k + 1) * 128],
                                            featk[par][:, ch * 128:(ch + 1) * 128],
                                            id_b[:], is_transpose=True,
                                            start=(k == 0), stop=(k == 7))
                        return inst
                    return em_tp

                def mk_tpev(rnd):
                    def em_tev(e, rnd=rnd):
                        if rnd == 1:
                            return e.tensor_copy(out=mkT[:, rnd * 1024:(rnd + 1) * 1024], in_=psT[:])
                        return e.copy(out=mkT[:, rnd * 1024:(rnd + 1) * 1024], in_=psT[:])
                    return em_tev

                t_tp0 = p.step("tensor", mk_tp(0), waits=[t_feat, t_ld0, tA.get(("mkT_used", i - 1))],
                               sem="T", inc=1)
                t_tpev0 = p.step("scalar", mk_tpev(0), waits=[t_tp0], sem="S", inc=1)
                t_mbev = []
                t_mbmm = []
                t_tpev = [t_tpev0]
                for half in range(2):
                    def em_mb(e, half=half, par=par):
                        inst = None
                        for c in range(16):
                            lh = ln1xT[par][:, c * 128:(c + 1) * 128]
                            for s in range(2):
                                sl = half * 2 + s
                                inst = e.matmul(psN[2 + s][:], lh,
                                                wmb_b[:, c * 2048 + sl * 512:c * 2048 + (sl + 1) * 512],
                                                start=(c == 0), stop=(c == 15))
                        return inst
                    wmb_w = [F["t_ln1"], tok_wA]
                    if half == 0:
                        wmb_w.append(tA.get(("psN23_free", i - 1)))
                    else:
                        wmb_w.append(t_mbev[0])
                    t_mbmm.append(p.step("tensor", em_mb, waits=wmb_w, sem="T", inc=1))
                    rnd = half + 1
                    t_tp = p.step("tensor", mk_tp(rnd), waits=[t_feat, t_tpev[rnd - 1]], sem="T", inc=1)
                    t_tpev.append(p.step("vector" if rnd == 1 else "scalar", mk_tpev(rnd),
                                          waits=[t_tp], sem="V" if rnd == 1 else "S", inc=1))
                    def em_mbev(e, half=half):
                        e.tensor_tensor(out=mbp[:, half * 1024:half * 1024 + 512],
                                        in0=psN[2][:], in1=biasA_s[:, 1024 + half * 1024:1536 + half * 1024],
                                        op=ALU.add)
                        return e.tensor_tensor(out=mbp[:, half * 1024 + 512:half * 1024 + 1024],
                                               in0=psN[3][:], in1=biasA_s[:, 1536 + half * 1024:2048 + half * 1024],
                                               op=ALU.add)
                    t_mbev.append(p.step("vector", em_mbev,
                                         waits=[t_mbmm[half], tA.get(("mbp_used", i - 1)) if half == 0 else None],
                                         sem="V", inc=1))
                tA[("ln1xT_used", i)] = t_mbmm[1]
                t_sig = p.step("scalar", lambda e: e.activation(out=mbp[:], in_=mbp[:], func=AF.Sigmoid),
                               waits=[t_mbev[1], tA.get(("mbs_used", i - 1))], sem="S", inc=1)
                tA[("mbp_used", i)] = t_sig

                t_n2 = [t_feat, tA.get(("sm_used", i - 1)), t_sig]
                for j in range(4):
                    t_n2 = [p.step("scalar", lambda e, j=j, par=par: e.activation(
                        out=x2k[:, 0:FD], in_=featk[par][:, j * FD:(j + 1) * FD],
                        func=AF.Square, accum_out=sm[:, j:j + 1]),
                        waits=t_n2, sem="S", inc=1)]
                t_n2 = t_n2[0]
                t_sqn = p.step("scalar", lambda e: e.activation(out=sm[:, 4:8], in_=sm[:, 0:4], func=AF.Sqrt),
                               waits=[t_n2], sem="S", inc=1)
                t_iva = p.step("vector", lambda e: e.tensor_scalar(
                    out=sm[:, 4:8], in0=sm[:, 4:8], scalar1=EPS_L2, scalar2=None, op0=ALU.max),
                    waits=[t_sqn], sem="V", inc=1)
                t_ivb = p.step("vector", lambda e: e.reciprocal(out=sm[:, 8:12], in_=sm[:, 4:8]),
                               waits=[t_iva], sem="V", inc=1)
                t_invn = p.step("vector", lambda e: e.tensor_copy(
                    out=sm[:, 16:32], in_=AP(sm, 8, [[80, 128], [1, 4], [0, 4]])),
                    waits=[t_ivb], sem="V", inc=1)

                t_numMM = []
                for j in range(4):
                    def em_num(e, j=j):
                        inst = None
                        for c in range(6):
                            lh = mkT[:, (j * 6 + c) * 128:(j * 6 + c) * 128 + 128]
                            e.matmul(psN[j][:], lh,
                                     mem4_b[:, c * 2048 + j * 512:c * 2048 + (j + 1) * 512],
                                     start=(c == 0), stop=(c == 5))
                            inst = e.matmul(psStat[:, 258 + j * 4:258 + (j + 1) * 4], lh,
                                            nn96b_b[:, c * 16 + j * 4:c * 16 + (j + 1) * 4],
                                            start=(j == 0 and c == 0), stop=(j == 3 and c == 5))
                        return inst
                    w = [t_tpev[min(2, (j * 6 + 5) // 8)], tok_wA]
                    if Fnext:
                        w.append(Fnext["t_sc"])
                    if j < 2:
                        w.append(F["t_kvev"])
                        w.append(tA.get((f"dev_b{j}", i - 1)))
                    else:
                        w.append(t_mbev[1])
                    t_numMM.append(p.step("tensor", em_num, waits=w, sem="T", inc=1))
                tA[("mkT_used", i)] = t_numMM[3]

                t_dn1 = p.step("vector", lambda e: e.tensor_tensor(
                    out=sm[:, 32:48], in0=psStat[:, 258:274], in1=sm[:, 16:32], op=ALU.mult),
                    waits=[t_numMM[3], t_invn], sem="V", inc=1)
                t_dn2 = p.step("vector", lambda e: e.tensor_scalar(
                    out=sm[:, 32:48], in0=sm[:, 32:48], scalar1=EPS_AT, scalar2=None, op0=ALU.add),
                    waits=[t_dn1], sem="V", inc=1)
                t_dn3 = p.step("vector", lambda e: e.reciprocal(out=sm[:, 48:64], in_=sm[:, 32:48]),
                               waits=[t_dn2], sem="V", inc=1)
                t_den = p.step("vector", lambda e: e.tensor_tensor(
                    out=sm[:, 48:64], in0=sm[:, 48:64], in1=sm[:, 16:32], op=ALU.mult),
                    waits=[t_dn3], sem="V", inc=1)
                tA[("psStat_free", i + 1)] = t_dn1
                tc = p.step("vector", lambda e: e.tensor_tensor(
                    out=sm[:, 64:80], in0=sm[:, 16:32], in1=sm[:, 16:32], op=ALU.mult),
                    waits=[t_den], sem="V", inc=1)
                tc = p.step("vector", lambda e: e.tensor_tensor(
                    out=sm[:, 64:80], in0=sm[:, 64:80],
                    in1=AP(sm, 0, [[80, 128], [1, 4], [0, 4]]), op=ALU.mult),
                    waits=[tc], sem="V", inc=1)
                tc = p.step("vector", lambda e: e.reciprocal(out=sm[:, 64:80], in_=sm[:, 64:80]),
                            waits=[tc], sem="V", inc=1)
                tc = p.step("vector", lambda e: e.tensor_tensor(
                    out=sm[:, 64:80], in0=sm[:, 64:80], in1=sm[:, 32:48], op=ALU.mult),
                    waits=[tc], sem="V", inc=1)
                tc = p.step("vector", lambda e: e.tensor_scalar(
                    out=sm[:, 64:80], in0=sm[:, 64:80], scalar1=-1.0, scalar2=1.0,
                    op0=ALU.mult, op1=ALU.add), waits=[tc], sem="V", inc=1)
                tc = p.step("vector", lambda e: e.tensor_scalar(
                    out=sm[:, 64:80], in0=sm[:, 64:80], scalar1=0.0, scalar2=1.0,
                    op0=ALU.max, op1=ALU.min), waits=[tc], sem="V", inc=1)
                t_coef = p.step("vector", lambda e: e.tensor_tensor(
                    out=coefb[:], in0=sm[:, 64:80], in1=sm[:, 16:32], op=ALU.mult),
                    waits=[tc, tA.get(("coef_used", i - 1))], sem="V", inc=1)

                t_wmv = []
                for j in range(4):
                    tw = p.step("vector", lambda e, j=j: e.tensor_tensor(
                        out=pvt[:], in0=psN[j][:],
                        in1=AP(sm, 48 + j * 4, [[80, 128], [1, 4], [0, 128]]), op=ALU.mult),
                        waits=[t_den, t_sig, t_numMM[j], t_wmv[-1] if t_wmv else None],
                        sem="V", inc=1)
                    tw = p.step("vector", lambda e, j=j, par=par: e.tensor_tensor(
                        out=pvt[:], in0=pvt[:],
                        in1=AP(vf[par], j * 128, [[512, 128], [0, 4], [1, 128]]), op=ALU.subtract),
                        waits=[tw], sem="V", inc=1)
                    t_wmv.append(p.step("vector", lambda e, j=j, par=par: e.scalar_tensor_tensor(
                        out=wmv4[par][:, j * 512:(j + 1) * 512],
                        in0=pvt[:], scalar=sm[:, 8 + j:9 + j],
                        in1=mbp[:, j * 512:(j + 1) * 512],
                        op0=ALU.mult, op1=ALU.mult),
                        waits=[tw, tA.get(("wmv_free", i - 2))], sem="V", inc=1))
                tA[("vf_used", i)] = t_wmv[3]
                tA[("mbs_used", i)] = t_wmv[3]
                tA[("sm_used", i)] = t_wmv[3]
                tA[("psN01_free2", i)] = t_wmv[1]
                tA[("psN23_free", i)] = t_wmv[3]
                psN01_free = t_wmv[1]

                for j in range(4):
                    tA[("wmvprev", (i, j))] = t_wmv[j]
                tA[("back_refs", i)] = dict(t_wmv=t_wmv, t_coef=t_coef, t_numMM=t_numMM,
                                            psN01_free=psN01_free)

            def emit_tailA(i, Fnext=None):
                par = i % 2
                R = tA.pop(("back_refs", i))
                t_wmv, t_coef, t_numMM = R["t_wmv"], R["t_coef"], R["t_numMM"]
                psN01_free = R["psN01_free"]
                def em_dn(e, i=i, par=par):
                    inst = None
                    for j in range(4):
                        for c in range(6):
                            lh = featk[par][:, (j * 6 + c) * 128:(j * 6 + c) * 128 + 128]
                            inst = e.matmul(psDN[:, c * 16 + j * 4:c * 16 + (j + 1) * 4], lh,
                                            coefb[:, j * 4:(j + 1) * 4],
                                            start=(i == 0 and j == 0 and c == 0),
                                            stop=(i == NT - 1 and j == 3 and c == 5))
                    return inst
                t_dnm = p.step("tensor", em_dn, waits=[t_coef, t_numMM[3]], sem="T", inc=1)
                tA[("coef_used", i)] = t_dnm
                tA[("featk_free", i)] = t_dnm

                if i % SUP == SUP - 1:
                    si = i // SUP
                    t_delta_last = None
                    gi = 0
                    for j in range(4):
                        for rnd in range(6):
                            def em_dl(e, j=j, rnd=rnd, i=i):
                                inst = None
                                c = rnd
                                pb = psN[rnd % 2]
                                for t in range(SUP):
                                    tt = i - SUP + 1 + t
                                    inst = e.matmul(pb[:],
                                                    featk[tt % 2][:, (j * 6 + c) * 128:(j * 6 + c) * 128 + 128],
                                                    wmv4[tt % 2][:, j * 512:(j + 1) * 512],
                                                    start=(t == 0), stop=(t == SUP - 1))
                                return inst
                            w = [t_wmv[j], tA.get(("wmvprev", (i - 1, j)))]
                            if gi == 0:
                                w.append(psN01_free)
                            if gi < 2 and Fnext is not None:
                                w.append(Fnext["t_kvev"])
                            w.append(tA.get(("dev", (si, gi - 2))))
                            t_dl = p.step("tensor", em_dl, waits=w, sem="T", inc=1)
                            def em_dev(e, rnd=rnd, gi=gi):
                                return e.tensor_copy(out=dstg[gi % 2][:], in_=psN[rnd % 2][:])
                            t_dev = p.step("vector", em_dev,
                                           waits=[t_dl, tA.get(("dacc", (si, gi - 2))) or
                                                  tA.get(("dacc", (si - 1, 22 + gi)))],
                                           sem="V", inc=1)
                            tA[("dev", (si, gi))] = t_dev
                            tA[(f"dev_b{rnd % 2}", i)] = t_dev
                            def em_acc(e, j=j, rnd=rnd, si=si, gi=gi):
                                return e.dma_start(
                                    out=AP(ar_in.tensor, rnd * 2048 + j * 512,
                                           [[AR_COLS, 128], [1, 512]]),
                                    in_=dstg[gi % 2][:],
                                    accum_op=(ALU.bypass if si == 0 else ALU.add))
                            wacc = [t_dev]
                            if si >= 1:
                                wacc.append(tA[("acc", (si - 1, j, rnd))])
                            t_acc = p.step("gpsimd", em_acc, waits=wacc,
                                           sem="sda" if gi % 2 == 0 else "sda1", inc=16)
                            tA[("acc", (si, j, rnd))] = t_acc
                            tA[("dacc", (si, gi))] = t_acc
                            gi += 1
                        t_delta_last = tA[("dev", (si, gi - 1))]
                        tA[("acc_last", si)] = t_acc
                    tA[("psN01_free2", i)] = t_delta_last
                    tA[("featk_free", i)] = t_delta_last
                    tA[("featk_free", i - 1)] = t_delta_last
                    tA[("wmv_free", i)] = t_delta_last
                    tA[("wmv_free", i - 1)] = t_delta_last


            FA = {}
            for i in range(NT + 1):
                if i < NT:
                    FA[i] = emit_frontA(i)
                if i >= 1:
                    emit_backA(i - 1, FA[i - 1], FA.get(i))
                if i < NT:
                    FA[i].update(emit_frontA2(i, FA[i]))
                if i >= 1:
                    emit_tailA(i - 1, FA.get(i))
                if i < NT:
                    FA[i].update(emit_frontA2b(i, FA[i]))
                if i >= 1:
                    FA.pop(i - 1, None)

            t_dnev = p.step("vector", lambda e: e.tensor_copy(out=dstg[0][:, 0:96], in_=psDN[:]),
                            waits=[p.tok("T"), tA[("acc", (NSUP - 1, 3, 5))], tA[("acc", (NSUP - 1, 3, 4))]], sem="V", inc=1)
            t_dnst = p.step("gpsimd", lambda e: e.dma_start(
                out=AP(ar_in.tensor, 12288, [[AR_COLS, 128], [1, 96]]), in_=dstg[0][:, 0:96]),
                waits=[t_dnev], sem="sda", inc=16)
            if n_cores > 1:
                t_ar = p.step("gpsimd", lambda e: e.collective_compute(
                    "AllReduce", ALU.add, replica_groups=[list(range(n_cores))],
                    ins=[ar_in], outs=[ar_out]), waits=[t_dnst], sem="sar", inc=1)
            else:
                t_ar = p.step("gpsimd", lambda e: e.dma_start(out=ar_out, in_=ar_in),
                              waits=[t_dnst], sem="sar", inc=16)
            if debug_outs:
                t_dbgar = p.step("gpsimd", lambda e: e.dma_start(out=dbg_ar, in_=ar_out),
                                 waits=[t_ar], sem="sda", inc=16)

        t_rows_all = p.tok("sstr")
        t_bar1 = barrier()

        esQ = ExitStack()
        tQ = {}
        with esQ:
            wq_b = alloc("wq_b", (128, 16 * 2048), BF16, esQ)
            numa_rhs = alloc("numa_rhs", (128, NH * FC * 129), BF16, esQ)
            nn96_s = alloc("nn96_s", (128, 96), F32, esQ)
            arn = alloc("arn", (128, 96), F32, esQ)
            biasQ_s = alloc("biasQ_s", (128, 2048), BF16, esQ)
            xT_q = [alloc("xTq_0", (128, HID), F32, esQ)]
            ln1xTq = alloc("ln1xTq", (128, HID), BF16, esQ)
            qf = [alloc(f"qf_{k}", (128, 2048), F32, esQ) for k in range(2)]
            x2q = alloc("x2q", (128, 16 * 512), BF16, esQ)
            featq = alloc("featq", (128, NH * FD), BF16, esQ)
            mqT = alloc("mqT", (128, NH * FD), BF16, esQ)
            smq = alloc("smq", (128, 128), F32, esQ)
            mh_b = [alloc("mh_0", (128, 2048), F32, esQ)]
            arld = alloc("arld", (128, 512), F32, esQ)
            mmld = alloc("mmld", (128, 512), F32, esQ)


            psBCq = esQ.enter_context(nc.psum_tensor("psBCq", [128, 256], F32))
            psQb = [esQ.enter_context(nc.psum_tensor(f"psQ{k}", [128, 512], F32)) for k in range(2)]
            psTq = esQ.enter_context(nc.psum_tensor("psTq", [128, 1024], BF16))
            psAb = [esQ.enter_context(nc.psum_tensor(f"psA{k}", [128, 512], F32)) for k in range(4)]
            banks = psAb + psQb

            p.step("gpsimd", cast3d(wq_b, wqT, 2048, 16, 2048), waits=t_bar1, sem="sgQ", inc=16)
            p.step("sync", lambda e: e.dma_start(out=biasQ_s[:], in_=biasQ), waits=t_bar1, sem="sld", inc=16)
            tok_wQ = p.tok("sgQ")
            tok_bQ = p.tok("sld")
            t_arw = [p.tok("sar")]
            p.step("sync", lambda e: e.dma_start(out=nn96_s[:], in_=nn96), waits=t_bar1, sem="sarld", inc=16)
            cmb_done = {}
            for c2 in range(24):
                c, h4 = divmod(c2, 4)
                def em_arl(e, c=c, h4=h4):
                    return e.dma_start(out=arld[:],
                                       in_=AP(ar_out.tensor, c * 2048 + h4 * 512,
                                              [[AR_COLS, 128], [1, 512]]))
                def em_mml(e, c=c, h4=h4):
                    return e.dma_start(out=mmld[:],
                                       in_=AP(mem4.tensor, c * 128 * 2048 + h4 * 512,
                                              [[2048, 128], [1, 512]]))
                w = t_arw + t_bar1 + ([cmb_done[c2 - 1]] if c2 >= 1 else [])
                tl1 = p.step("sync", em_arl, waits=w, sem="sarld", inc=16)
                tl2 = p.step("sync", em_mml, waits=[tl1], sem="sarld", inc=16)
                def em_cmb(e, c=c, h4=h4):
                    return e.tensor_tensor(
                        out=AP(numa_rhs, h4 * 4 * FC * 129 + c * 129,
                               [[NH * FC * 129, 128], [FC * 129, 4], [1, 128]]),
                        in0=AP(mmld, 0, [[512, 128], [128, 4], [1, 128]]),
                        in1=AP(arld, 0, [[512, 128], [128, 4], [1, 128]]),
                        op=ALU.subtract)
                cmb_done[c2] = p.step("vector", em_cmb, waits=[tl2] + t_bar1, sem="V", inc=1)
            cmb_done[11] = cmb_done[23]
            t_arn = p.step("sync", lambda e: e.dma_start(
                out=arn[:], in_=AP(ar_out.tensor, 12288, [[AR_COLS, 128], [1, 96]])),
                waits=t_arw + t_bar1 + [p.tok("sarld")], sem="sarn", inc=16)
            def em_nn(e):
                return e.tensor_tensor(
                    out=AP(numa_rhs, 128, [[NH * FC * 129, 128], [FC * 129, 16], [129, 6]]),
                    in0=AP(nn96_s, 0, [[96, 128], [6, 16], [1, 6]]),
                    in1=AP(arn, 0, [[96, 128], [1, 16], [16, 6]]),
                    op=ALU.add)
            t_nn = p.step("vector", em_nn, waits=[t_arn, cmb_done[23]], sem="V", inc=1)
            t_rhs_ready = [t_nn, cmb_done[23]]

            def emit_frontQ(i):
                F = {}
                par = i % 2
                def ldxT(e, i=i):
                    return e.dma_start(out=xT_q[0][:],
                                       in_=AP(xT.tensor, i * 128,
                                              [[NTOK, 128], [128 * NTOK, 16], [1, 128]]))
                p.step("sync", ldxT, waits=[tQ.get(("xT_free", i - 1)), t_bar1[0]], sem="sx0", inc=16)
                t_x = p.tok("sx0")
                def ldrw(e, i=i, par=par):
                    return e.dma_start(out=rowb2[par][0:1, :],
                                       in_=AP(stats_dram.tensor, i * 256, [[256, 1], [1, 256]]))
                p.step("sync", ldrw, waits=[t_rows_all, tQ.get(("rowb_used", i - 2))],
                       sem=f"srw{par}", inc=16)
                t_rw = p.tok(f"srw{par}")
                def em_bc(e, par=par):
                    return e.matmul(psBCq[:, 0:256], onr[:],
                                    rowb2[par][0:1, 0:256], start=True, stop=True)
                t_bc = p.step("tensor", em_bc, waits=[tQ.get(("psBC_free", i - 1)), t_rw] + t_bar1,
                              sem="T", inc=1)
                tQ[("rowb_used", i)] = t_bc
                t_ln1a = p.step("vector", lambda e, par=par: e.tensor_tensor(
                    out=qf[par][:], in0=xT_q[0][:],
                    in1=AP(psBCq, 0, [[256, 128], [0, 16], [1, 128]]), op=ALU.mult),
                    waits=[t_bc, t_x, tQ.get(("qf_used", i - 2))], sem="V", inc=1)
                t_ln1 = p.step("vector", lambda e, par=par: e.tensor_tensor(
                    out=ln1xTq[:], in0=qf[par][:],
                    in1=AP(psBCq, 128, [[256, 128], [0, 16], [1, 128]]), op=ALU.add),
                    waits=[t_ln1a, tQ.get(("ln1_used", i - 1))], sem="V", inc=1)
                tQ[("psBC_free", i)] = t_ln1
                tQ[("xT_free", i)] = t_ln1a

                F.update(t_ln1=t_ln1, t_x=t_x)
                return F

            def emit_frontQ2(i, F):
                par = i % 2
                t_ln1 = F["t_ln1"]
                t_qev = []
                t_qmm = []
                for half in range(2):
                    def em_q(e, half=half):
                        inst = None
                        for c in range(16):
                            lh = ln1xTq[:, c * 128:(c + 1) * 128]
                            for s in range(2):
                                sl = half * 2 + s
                                inst = e.matmul(psQb[s][:], lh,
                                                wq_b[:, c * 2048 + sl * 512:c * 2048 + (sl + 1) * 512],
                                                start=(c == 0), stop=(c == 15))
                        return inst
                    w = [t_ln1, tok_wQ]
                    if half == 0:
                        w.append(tQ.get(("psQ_free", i - 1)))
                    else:
                        w.append(t_qev[0])
                    t_qmm.append(p.step("tensor", em_q, waits=w, sem="T", inc=1))
                    def em_qev(e, half=half, par=par):
                        e.tensor_tensor(out=qf[par][:, half * 1024:half * 1024 + 512], in0=psQb[0][:],
                                        in1=biasQ_s[:, half * 1024:half * 1024 + 512], op=ALU.add)
                        return e.tensor_tensor(out=qf[par][:, half * 1024 + 512:half * 1024 + 1024],
                                               in0=psQb[1][:],
                                               in1=biasQ_s[:, half * 1024 + 512:half * 1024 + 1024], op=ALU.add)
                    t_qev.append(p.step("vector", em_qev,
                                        waits=[t_qmm[half], tok_bQ], sem="V", inc=1))
                tQ[("ln1_used", i)] = t_qmm[1]
                tQ[("psQ_free", i)] = t_qev[1]
                return dict(t_qev1=t_qev[1])

            def emit_backQ(i, F):
                par = i % 2
                def em_relu(e, par=par):
                    e.activation(out=AP(x2q, 0, [[16 * 512, 128], [512, 16], [256, 2], [1, 128]]),
                                 in_=AP(qf[par], 0, [[2048, 128], [128, 16], [0, 2], [1, 128]]),
                                 func=AF.Relu)
                    return e.activation(out=AP(x2q, 128, [[16 * 512, 128], [512, 16], [256, 2], [1, 128]]),
                                        in_=AP(qf[par], 0, [[2048, 128], [128, 16], [0, 2], [1, 128]]),
                                        func=AF.Relu, scale=-1.0)
                t_relu = p.step("scalar", em_relu, waits=[F["t_qev1"], tQ.get(("x2q_used", i - 1))] + t_bar1,
                                sem="S", inc=1)
                tQ[("qf_used", i)] = t_relu
                def em_feat(e):
                    return e.tensor_tensor(
                        out=featq[:],
                        in0=AP(x2q, 256, [[16 * 512, 128], [512, 16], [0, 3], [1, 256]]),
                        in1=AP(x2q, 255, [[16 * 512, 128], [512, 16], [-1, 3], [1, 256]]),
                        op=ALU.mult)
                t_feat = p.step("vector", em_feat, waits=[t_relu, tQ.get(("featq_used", i - 1))], sem="V", inc=1)
                tQ[("x2q_used", i)] = t_feat
                t_n2 = [t_feat, tQ.get(("smq_used", i - 1))]
                for h in range(NH):
                    t_n2 = [p.step("scalar", lambda e, h=h: e.activation(
                        out=x2q[:, 0:FD], in_=featq[:, h * FD:(h + 1) * FD],
                        func=AF.Square, accum_out=smq[:, h:h + 1]),
                        waits=t_n2, sem="S", inc=1)]
                t_n2 = t_n2[0]
                t_sqn = p.step("scalar", lambda e: e.activation(out=smq[:, 16:32], in_=smq[:, 0:16], func=AF.Sqrt),
                               waits=[t_n2], sem="S", inc=1)
                t_iv = p.step("vector", lambda e: e.tensor_scalar(
                    out=smq[:, 16:32], in0=smq[:, 16:32], scalar1=EPS_L2, scalar2=None, op0=ALU.max),
                    waits=[t_sqn], sem="V", inc=1)
                t_invn = p.step("vector", lambda e: e.reciprocal(out=smq[:, 32:48], in_=smq[:, 16:32]),
                                waits=[t_iv], sem="V", inc=1)

                def mk_tpq(rnd):
                    def em_tp(e, rnd=rnd):
                        inst = None
                        for k in range(8):
                            ch = rnd * 8 + k
                            inst = e.matmul(psTq[:, k * 128:(k + 1) * 128],
                                            featq[:, ch * 128:(ch + 1) * 128],
                                            id_b[:], is_transpose=True,
                                            start=(k == 0), stop=(k == 7))
                        return inst
                    return em_tp

                def mk_tpqev(rnd):
                    def em_tev(e, rnd=rnd):
                        if rnd % 2:
                            return e.tensor_copy(out=mqT[:, rnd * 1024:(rnd + 1) * 1024], in_=psTq[:])
                        return e.copy(out=mqT[:, rnd * 1024:(rnd + 1) * 1024], in_=psTq[:])
                    return em_tev

                def mk_na(heads, b3):
                    def em_na(e, heads=heads, b3=b3):
                        pb = psAb[b3]
                        inst = None
                        first = True
                        for hh, h in enumerate(heads):
                            for c in range(6):
                                lh = mqT[:, (h * 6 + c) * 128:(h * 6 + c) * 128 + 128]
                                inst = e.matmul(pb[:, hh * 129:hh * 129 + 129], lh,
                                                numa_rhs[:, (h * 6 + c) * 129:(h * 6 + c) * 129 + 129],
                                                start=first, stop=(hh == len(heads) - 1 and c == 5))
                                first = False
                        return inst
                    return em_na

                tpev = {}
                prev = tQ.get(("mqT_used", i - 1))
                t_raw = {}
                t_na = []
                t_dgs = []
                na_blocks = [([0, 1, 2], 0), ([3, 4, 5], 1), ([6, 7], 2),
                             ([8, 9, 10], 0), ([11, 12, 13], 1), ([14, 15], 2)]
                next_na = 0
                for rnd in range(12):
                    t_tp = p.step("tensor", mk_tpq(rnd), waits=[t_feat, tpev.get(rnd - 1, prev)],
                                  sem="T", inc=1)
                    tpev[rnd] = p.step("vector" if rnd % 2 else "scalar", mk_tpqev(rnd),
                                       waits=[t_tp], sem="V" if rnd % 2 else "S", inc=1)
                    while next_na < 6 and (na_blocks[next_na][0][-1] * 6 + 5) // 8 <= rnd:
                        bi = next_na
                        heads, b3 = na_blocks[bi]
                        w = [tpev[rnd]] + t_rhs_ready
                        if bi < 3:
                            w.append(tQ.get(("na_used", (i - 1, bi % 3))))
                        else:
                            w.append(t_raw[bi - 3])
                            w.append(t_dgs[bi - 3])
                        t_na.append(p.step("tensor", mk_na(heads, b3), waits=w, sem="T", inc=1))
                        def em_raw(e, heads=heads, b3=b3):
                            return e.copy(
                                out=AP(mh_b[0], heads[0] * 128, [[2048, 128], [128, len(heads)], [1, 128]]),
                                in_=AP(psAb[b3], 0, [[512, 128], [129, len(heads)], [1, 128]]))
                        t_raw[bi] = p.step("scalar", em_raw,
                                           waits=[t_na[bi], tQ.get(("mh_free", i - 1)) if bi == 0 else None],
                                           sem="S", inc=1)
                        tQ[("na_used", (i, bi % 3))] = t_raw[bi]
                        def em_dgb(e, heads=heads, b3=b3):
                            return e.tensor_copy(out=smq[:, 48 + heads[0]:48 + heads[0] + len(heads)],
                                                 in_=AP(psAb[b3], 128, [[512, 128], [129, len(heads)]]))
                        t_dgs.append(p.step("vector", em_dgb,
                                            waits=[t_na[bi], tQ.get(("smq_used", i - 1)) if bi == 0 else None],
                                            sem="V", inc=1))
                        next_na += 1
                t_s = p.step("vector", lambda e: e.tensor_tensor(
                    out=smq[:, 64:80], in0=smq[:, 48:64], in1=smq[:, 32:48], op=ALU.mult),
                    waits=t_dgs + [t_invn], sem="V", inc=1)
                t_s = p.step("vector", lambda e: e.tensor_scalar(
                    out=smq[:, 64:80], in0=smq[:, 64:80], scalar1=EPS_AT, scalar2=None, op0=ALU.add),
                    waits=[t_s], sem="V", inc=1)
                t_s = p.step("vector", lambda e: e.reciprocal(out=smq[:, 80:96], in_=smq[:, 64:80]),
                             waits=[t_s], sem="V", inc=1)
                t_s = p.step("vector", lambda e: e.tensor_tensor(
                    out=smq[:, 96:112], in0=smq[:, 80:96], in1=smq[:, 32:48], op=ALU.mult),
                    waits=[t_s], sem="V", inc=1)
                def em_mh(e):
                    inst = None
                    for h in range(NH):
                        inst = e.tensor_scalar(out=mh_b[0][:, h * 128:(h + 1) * 128],
                                               in0=mh_b[0][:, h * 128:(h + 1) * 128],
                                               scalar1=smq[:, 96 + h:97 + h], scalar2=None,
                                               op0=ALU.mult)
                    return inst
                t_mh = p.step("vector", em_mh, waits=[t_s, t_raw[5]], sem="V", inc=1)
                tQ[("mqT_used", i)] = t_na[5]
                tQ[("smq_used", i)] = t_mh
                def em_mhst(e, i=i):
                    return e.dma_start(out=AP(mh_sp.tensor, i * 128 * 2048, [[2048, 128], [1, 2048]]),
                                       in_=mh_b[0][:])
                t_mhs = p.step("scalar", em_mhst, waits=[t_mh], sem="sstm", inc=16)
                tQ[("mh_free", i)] = t_mhs
                tQ[("mh_done", i)] = t_mhs

            FQ = {}
            for i in range(NT + 1):
                if i < NT:
                    FQ[i] = emit_frontQ(i)
                if i >= 1:
                    emit_backQ(i - 1, FQ[i - 1])
                if i < NT:
                    FQ[i].update(emit_frontQ2(i, FQ[i]))
                if i >= 1:
                    FQ.pop(i - 1, None)

        t_mh_all = p.tok("sstm")
        t_bar2 = barrier()

        esG = ExitStack()
        tG = {}
        with esG:
            wg_b = alloc("wg_b", (128, 16 * 2048), BF16, esG)
            biasG_s = alloc("biasG_s", (128, 2048), BF16, esG)
            xT_g = [alloc(f"xTg_{k}", (128, HID), F32, esG) for k in range(2)]
            x_g = [alloc(f"xg_{k}", (128, HID), F32, esG) for k in range(2)]
            mh_g = [alloc(f"mhg_{k}", (128, HID), F32, esG) for k in range(2)]
            ln1tg = alloc("ln1tg", (128, HID), F32, esG)
            ln1xTg = alloc("ln1xTg", (128, HID), BF16, esG)
            ln1f = alloc("ln1f", (128, HID), F32, esG)
            gsig = alloc("gsig", (128, HID), F32, esG)
            gpre = [alloc(f"gpre_{k}", (128, HID), F32, esG) for k in range(2)]
            att = alloc("att", (128, HID), F32, esG)
            attsq = alloc("attsq", (128, HID), F32, esG)
            y1b = alloc("y1b", (128, HID), F32, esG)
            ln2b = alloc("ln2b", (128, HID), BF16, esG)
            smg = alloc("smg", (128, 16), F32, esG)

            psBCg = esG.enter_context(nc.psum_tensor("psBCg", [128, 256], F32))
            psG = [esG.enter_context(nc.psum_tensor(f"psG{k}", [128, 512], F32)) for k in range(2)]

            p.step("gpsimd", cast3d(wg_b, wgT, 2048, 16, 2048), waits=t_bar2, sem="sgG", inc=16)
            tok_wG = p.tok("sgG")
            p.step("sync", lambda e: e.dma_start(out=biasG_s[:], in_=biasG), waits=t_bar2, sem="sld", inc=16)
            tok_bG = p.tok("sld")

            def emit_frontG(i):
                F = {}
                par = i % 2
                def ldxT(e, i=i, par=par):
                    return e.dma_start(out=xT_g[par][:],
                                       in_=AP(xT.tensor, i * 128,
                                              [[NTOK, 128], [128 * NTOK, 16], [1, 128]]))
                p.step("sync", ldxT, waits=[tG.get(("xT_free", i - 2)), t_bar2[0]], sem=f"sx{par}", inc=16)
                t_xT = p.tok(f"sx{par}")
                def ldx(e, i=i, par=par):
                    return e.dma_start(out=x_g[par][:],
                                       in_=AP(x_in.tensor, i * 128 * 2048, [[2048, 128], [1, 2048]]))
                p.step("sync", ldx, waits=[tG.get(("x_free", i - 2))], sem=f"sxx{par}", inc=16)
                t_xn = p.tok(f"sxx{par}")
                def ldmh(e, i=i, par=par):
                    return e.dma_start(out=mh_g[par][:],
                                       in_=AP(mh_sp.tensor, i * 128 * 2048, [[2048, 128], [1, 2048]]))
                p.step("sync", ldmh, waits=[t_mh_all, tG.get(("mhg_free", i - 2)),
                                            tG.get(("mhg_freeD", i - 2))],
                       sem=f"smh{par}", inc=16)
                t_mh = p.tok(f"smh{par}")
                def ldrw(e, i=i, par=par):
                    return e.dma_start(out=rowb2[par][0:1, :],
                                       in_=AP(stats_dram.tensor, i * 256, [[256, 1], [1, 256]]))
                p.step("sync", ldrw, waits=[tG.get(("rowb_used", i - 2)), t_rows_all, t_bar2[1]],
                       sem=f"srw{par}", inc=16)
                t_rw = p.tok(f"srw{par}")
                def em_bc(e, par=par):
                    return e.matmul(psBCg[:, 0:256], onr[:],
                                    rowb2[par][0:1, 0:256], start=True, stop=True)
                t_bc = p.step("tensor", em_bc, waits=[tG.get(("psBC_free", i - 1)), t_rw] + t_bar2,
                              sem="T", inc=1)
                tG[("rowb_used", i)] = t_bc
                t_ln1a = p.step("vector", lambda e, par=par: e.tensor_tensor(
                    out=ln1tg[:], in0=xT_g[par][:],
                    in1=AP(psBCg, 0, [[256, 128], [0, 16], [1, 128]]), op=ALU.mult),
                    waits=[t_bc, t_xT, tG.get(("ln1tg_used", i - 1))], sem="V", inc=1)
                t_ln1 = p.step("vector", lambda e: e.tensor_tensor(
                    out=ln1xTg[:], in0=ln1tg[:],
                    in1=AP(psBCg, 128, [[256, 128], [0, 16], [1, 128]]), op=ALU.add),
                    waits=[t_ln1a, tG.get(("ln1_used", i - 1))], sem="V", inc=1)
                tG[("psBC_free", i)] = t_ln1
                tG[("xT_free", i)] = t_ln1a
                tG[("ln1tg_used", i)] = t_ln1

                F.update(t_ln1=t_ln1, t_mh=t_mh, t_xn=t_xn)
                return F

            def emit_frontG2(i, F):
                par = i % 2
                t_ln1 = F["t_ln1"]
                t_gev = []
                t_gmm = []
                for half in range(2):
                    def em_g(e, half=half):
                        inst = None
                        for c in range(16):
                            lh = ln1xTg[:, c * 128:(c + 1) * 128]
                            for s in range(2):
                                sl = half * 2 + s
                                inst = e.matmul(psG[s][:], lh,
                                                wg_b[:, c * 2048 + sl * 512:c * 2048 + (sl + 1) * 512],
                                                start=(c == 0), stop=(c == 15))
                        return inst
                    w = [t_ln1, tok_wG]
                    if half == 0:
                        w.append(tG.get(("psG_free", i - 1)))
                    else:
                        w.append(t_gev[0])
                    t_gmm.append(p.step("tensor", em_g, waits=w, sem="T", inc=1))
                    def em_gev(e, half=half, par=par):
                        e.tensor_tensor(out=gpre[par][:, half * 1024:half * 1024 + 512], in0=psG[0][:],
                                        in1=biasG_s[:, half * 1024:half * 1024 + 512], op=ALU.add)
                        return e.tensor_tensor(out=gpre[par][:, half * 1024 + 512:half * 1024 + 1024],
                                               in0=psG[1][:],
                                               in1=biasG_s[:, half * 1024 + 512:half * 1024 + 1024], op=ALU.add)
                    t_gev.append(p.step("vector", em_gev,
                                        waits=[t_gmm[half], tok_bG,
                                               tG.get(("gpre_used", i - 2)) if half == 0 else None],
                                        sem="V", inc=1))
                tG[("ln1_used", i)] = t_gmm[1]
                tG[("psG_free", i)] = t_gev[1]
                return dict(t_gev1=t_gev[1])

            def emit_backG(i, F):
                par = i % 2
                t_sig = p.step("scalar", lambda e, par=par: e.activation(
                    out=gsig[:], in_=gpre[par][:], func=AF.Sigmoid),
                    waits=[F["t_gev1"], tG.get(("gsig_used", i - 1))], sem="S", inc=1)
                tG[("gpre_used", i)] = t_sig
                def em_ln1f(e, par=par, i=i):
                    return e.tensor_scalar(out=ln1f[:], in0=x_g[par][:],
                                           scalar1=stats_cols[:, i * 2:i * 2 + 1],
                                           scalar2=stats_cols[:, i * 2 + 1:i * 2 + 2],
                                           op0=ALU.mult, op1=ALU.add)
                t_ln1f = p.step("vector", em_ln1f, waits=[F["t_xn"], tG.get(("ln1f_used", i - 1))],
                                sem="V", inc=1)
                t_gm = p.step("vector", lambda e, par=par: e.tensor_tensor(
                    out=gsig[:], in0=gsig[:], in1=mh_g[par][:], op=ALU.mult),
                    waits=[t_sig, F["t_mh"]], sem="V", inc=1)
                t_att = p.step("vector", lambda e: e.tensor_tensor(
                    out=att[:], in0=ln1f[:], in1=gsig[:], op=ALU.add),
                    waits=[t_gm, t_ln1f, tG.get(("att_used", i - 1))], sem="V", inc=1)
                tG[("gsig_used", i)] = t_att
                tG[("ln1f_used", i)] = t_att
                tG[("mhg_free", i)] = t_gm
                if debug_outs:
                    tG[("mhg_freeD", i)] = p.step("scalar", lambda e, i=i, par=par: e.dma_start(
                        out=AP(dbg_mh.tensor, i * 128 * 2048, [[2048, 128], [1, 2048]]), in_=mh_g[par][:]),
                        waits=[t_gm], sem=f"sdbg{par}", inc=16)
                t_asq = p.step("scalar", lambda e: e.activation(out=attsq[:], in_=att[:], func=AF.Square,
                                                                accum_out=smg[:, 1:2]),
                               waits=[t_att, tG.get(("smg_used", i - 1))], sem="S", inc=1)
                t_as = p.step("vector", lambda e: e.reduce_sum(out=smg[:, 0:1], in_=att[:], axis=AX.X),
                              waits=[t_att], sem="V", inc=1)
                t_av = p.step("vector", lambda e: e.tensor_scalar(
                    out=smg[:, 2:4], in0=smg[:, 0:2], scalar1=1.0 / HID, scalar2=None, op0=ALU.mult),
                    waits=[t_as, t_asq], sem="V", inc=1)
                t_av = p.step("vector", lambda e: e.tensor_tensor(
                    out=smg[:, 4:5], in0=smg[:, 2:3], in1=smg[:, 2:3], op=ALU.mult),
                    waits=[t_av], sem="V", inc=1)
                t_av = p.step("vector", lambda e: e.tensor_tensor(
                    out=smg[:, 5:6], in0=smg[:, 3:4], in1=smg[:, 4:5], op=ALU.subtract),
                    waits=[t_av], sem="V", inc=1)
                t_av = p.step("vector", lambda e: e.tensor_scalar(
                    out=smg[:, 5:6], in0=smg[:, 5:6], scalar1=EPS_LN, scalar2=None, op0=ALU.add),
                    waits=[t_av], sem="V", inc=1)
                t_avs = p.step("scalar", lambda e: e.activation(out=smg[:, 6:7], in_=smg[:, 5:6], func=AF.Sqrt),
                               waits=[t_av], sem="S", inc=1)
                t_avr = p.step("vector", lambda e: e.reciprocal(out=smg[:, 7:8], in_=smg[:, 6:7]),
                               waits=[t_avs], sem="V", inc=1)
                t_y1a = p.step("vector", lambda e: e.tensor_scalar(
                    out=att[:], in0=att[:], scalar1=smg[:, 2:3], scalar2=smg[:, 7:8],
                    op0=ALU.subtract, op1=ALU.mult), waits=[t_avr], sem="V", inc=1)
                t_y1 = p.step("vector", lambda e, par=par: e.tensor_tensor(
                    out=y1b[:], in0=x_g[par][:], in1=att[:], op=ALU.add),
                    waits=[t_y1a, tG.get(("y1_used", i - 1)), tG.get(("y1_usedD", i - 1))],
                    sem="V", inc=1)
                tG[("att_used", i)] = t_y1
                tG[("x_free", i)] = t_y1
                t_y1sq = p.step("scalar", lambda e: e.activation(out=attsq[:], in_=y1b[:], func=AF.Square,
                                                                 accum_out=smg[:, 9:10]),
                                waits=[t_y1], sem="S", inc=1)
                tG[("smg_used", i)] = t_y1sq
                t_y1s = p.step("vector", lambda e: e.reduce_sum(out=smg[:, 8:9], in_=y1b[:], axis=AX.X),
                               waits=[t_y1], sem="V", inc=1)
                t_v2 = p.step("vector", lambda e: e.tensor_scalar(
                    out=smg[:, 10:12], in0=smg[:, 8:10], scalar1=1.0 / HID, scalar2=None, op0=ALU.mult),
                    waits=[t_y1s, t_y1sq], sem="V", inc=1)
                t_v2 = p.step("vector", lambda e: e.tensor_tensor(
                    out=smg[:, 12:13], in0=smg[:, 10:11], in1=smg[:, 10:11], op=ALU.mult),
                    waits=[t_v2], sem="V", inc=1)
                t_v2 = p.step("vector", lambda e: e.tensor_tensor(
                    out=smg[:, 13:14], in0=smg[:, 11:12], in1=smg[:, 12:13], op=ALU.subtract),
                    waits=[t_v2], sem="V", inc=1)
                t_v2 = p.step("vector", lambda e: e.tensor_scalar(
                    out=smg[:, 13:14], in0=smg[:, 13:14], scalar1=EPS_LN, scalar2=None, op0=ALU.add),
                    waits=[t_v2], sem="V", inc=1)
                t_v2s = p.step("scalar", lambda e: e.activation(out=smg[:, 14:15], in_=smg[:, 13:14], func=AF.Sqrt),
                               waits=[t_v2], sem="S", inc=1)
                t_v2r = p.step("vector", lambda e: e.reciprocal(out=smg[:, 15:16], in_=smg[:, 14:15]),
                               waits=[t_v2s], sem="V", inc=1)
                t_ln2 = p.step("vector", lambda e: e.tensor_scalar(
                    out=ln2b[:], in0=y1b[:], scalar1=smg[:, 10:11], scalar2=smg[:, 15:16],
                    op0=ALU.subtract, op1=ALU.mult),
                    waits=[t_v2r, tG.get(("ln2b_free", i - 1))], sem="V", inc=1)
                def em_y1st(e, i=i):
                    return e.dma_start(out=AP(y1_sp.tensor, i * 128 * 2048, [[2048, 128], [1, 2048]]), in_=y1b[:])
                t_y1st = p.step("scalar", em_y1st, waits=[t_y1], sem="ssty", inc=16)
                tG[("y1_used", i)] = t_y1st
                if debug_outs:
                    tG[("y1_usedD", i)] = p.step("scalar", lambda e, i=i: e.dma_start(
                        out=AP(dbg_y1.tensor, i * 128 * 2048, [[2048, 128], [1, 2048]]), in_=y1b[:]),
                        waits=[t_y1], sem="sdb2", inc=16)
                def em_l2st(e, i=i):
                    return e.dma_start(out=AP(ln2_sp.tensor, i * 128 * 2048, [[2048, 128], [1, 2048]]), in_=ln2b[:])
                t_l2st = p.step("scalar", em_l2st, waits=[t_ln2], sem="sstl", inc=16)
                tG[("ln2b_free", i)] = t_l2st
                tG[("ln2_done", i)] = t_l2st

            FG = {}
            for i in range(NT + 1):
                if i < NT:
                    FG[i] = emit_frontG(i)
                if i >= 1:
                    emit_backG(i - 1, FG[i - 1])
                if i < NT:
                    FG[i].update(emit_frontG2(i, FG[i]))
                if i >= 1:
                    FG.pop(i - 1, None)

        t_ln2_all = p.tok("sstl")
        t_y1_all = p.tok("ssty")
        t_bar3 = barrier()

        esF = ExitStack()
        tF = {}
        with esF:
            ICB = INTER // 2048
            ln2xT = alloc("ln2xT", (128, 16 * TG), BF16, esF)
            wgu_c = [alloc(f"wgu_{k}", (128, 16 * 128), BF16, esF) for k in range(4)]
            wd_cs = [alloc(f"wdcs_{k}", (128, 16 * 512), BF16, esF) for k in range(2)]
            hT = alloc("hT", (128, 16 * TG), BF16, esF)
            acc = alloc("acc", (128, 16 * TG), F32, esF)
            silu_t = alloc("silu_t", (128, 512), F32, esF)
            ffsq = alloc("ffsq", (128, 512), F32, esF)
            srow2 = alloc("srow2", (1, 1024), F32, esF)
            scp2 = alloc("scp2", (2, 128), F32, esF)
            stc2 = alloc("stc2", (128, 2), F32, esF)
            y1_f = [alloc("y1f_0", (128, 2048), F32, esF)]
            y2_b = [alloc("y2_0", (128, 2048), F32, esF)]

            psGU = [esF.enter_context(nc.psum_tensor(f"psGU{k}", [128, 512], F32)) for k in range(4)]
            psD = [esF.enter_context(nc.psum_tensor(f"psD{k}", [128, 512], F32)) for k in range(2)]
            psS2 = esF.enter_context(nc.psum_tensor("psS2", [128, 512], F32))
            psB2 = esF.enter_context(nc.psum_tensor("psB2", [128, 512], F32))

            for g in range(NG):
                for c in range(16):
                    def em_l2t(e, g=g, c=c):
                        return e.dma_start_transpose(
                            ln2xT[:, c * TG:(c + 1) * TG],
                            AP(ln2_sp.tensor, g * TG * 2048 + c * 128, [[2048, TG], [1, 128]]))
                    w = [t_ln2_all] + t_bar3
                    if g > 0:
                        w.append(tF[("ln2xT_free", g - 1)])
                    p.step("sync", em_l2t, waits=w, sem="sl2t", inc=16)
                t_l2t = p.tok("sl2t")

                h_hist = []
                for icb in range(ICB):
                    for s in range(16):
                        qb = s % 2
                        osl = icb * 2048 + s * 128
                        def em_wg(e, osl=osl, qb=qb):
                            return e.dma_start(
                                out=wgu_c[qb * 2][:],
                                in_=AP(ag_out.tensor, osl,
                                       [[2 * INTER, 128], [128 * 2 * INTER, 16], [1, 128]]))
                        def em_wu(e, osl=osl, qb=qb):
                            return e.dma_start(
                                out=wgu_c[qb * 2 + 1][:],
                                in_=AP(ag_out.tensor, INTER + osl,
                                       [[2 * INTER, 128], [128 * 2 * INTER, 16], [1, 128]]))
                        w = [t_ag, tF.get(("wgu_free", (g, icb, s - 2)))]
                        if s < 2:
                            w.append(tF.get(("wgu_free", (g, icb - 1, s + 14))) if icb > 0 else
                                     (tF.get(("wgu_free", (g - 1, ICB - 1, s + 14))) if g > 0 else None))
                        t_wg = p.step("sync", em_wg, waits=w, sem=f"sfg{qb * 2}", inc=16)
                        t_wu = p.step("sync", em_wu, waits=[], sem=f"sfg{qb * 2 + 1}", inc=16)
                        bs = psGU if s % 2 == 0 else [psD[0], psD[1], psS2, psB2]
                        def em_gu(e, qb=qb, bs=bs):
                            inst = None
                            for c in range(16):
                                for ts in range(TS):
                                    e.matmul(bs[ts][:, 0:TSW],
                                             wgu_c[qb * 2][:, c * 128:(c + 1) * 128],
                                             ln2xT[:, c * TG + ts * TSW:c * TG + (ts + 1) * TSW],
                                             start=(c == 0), stop=(c == 15))
                                for ts in range(TS):
                                    inst = e.matmul(bs[TS + ts][:, 0:TSW],
                                                    wgu_c[qb * 2 + 1][:, c * 128:(c + 1) * 128],
                                                    ln2xT[:, c * TG + ts * TSW:c * TG + (ts + 1) * TSW],
                                                    start=(c == 0), stop=(c == 15))
                            return inst
                        w = [t_wu, t_wg, t_l2t]
                        if icb == 0 and s == 0:
                            w.append(tF.get(("epi_done", g - 1)))
                        if s == 0:
                            w.append(tF.get(("down_done", (g, icb - 1))) if icb > 0 else
                                     (tF.get(("down_done", (g - 1, ICB - 1))) if g > 0 else None))
                        t_gu = p.step("tensor", em_gu, waits=w, sem="T", inc=1)
                        t_h_last = None
                        slb = [silu_t, ffsq]
                        for ts in range(TS):
                            t_si = p.step("scalar", lambda e, bs=bs, ts=ts: e.activation(
                                out=slb[ts][:, 0:TSW], in_=bs[ts][:, 0:TSW], func=AF.Silu),
                                waits=[t_gu, tF.get(("silu_free", ts))], sem="S", inc=1)
                            def em_h(e, s=s, ts=ts, bs=bs):
                                return e.tensor_tensor(
                                    out=hT[:, s * TG + ts * TSW:s * TG + (ts + 1) * TSW],
                                    in0=slb[ts][:, 0:TSW],
                                    in1=bs[TS + ts][:, 0:TSW], op=ALU.mult)
                            t_h = p.step("vector", em_h, waits=[t_si], sem="V", inc=1)
                            h_hist.append(t_h)
                            tF[("silu_free", ts)] = t_h
                            t_h_last = t_h
                        tF[("wgu_free", (g, icb, s))] = t_h_last
                    t_h_all = t_h_last

                    t_down_last = None
                    t_dmm_last = None
                    for hs in range(4):
                        qw = hs % 2
                        def em_wdl(e, g=g, icb=icb, hs=hs, qw=qw):
                            return e.dma_start(
                                out=wd_cs[qw][:],
                                in_=AP(ag_out.tensor, WGU_ELEMS + icb * 2048 * 2048 + hs * 512,
                                       [[2048, 128], [128 * 2048, 16], [1, 512]]))
                        w = [t_ag, tF.get(("wd_free", (g, icb, hs - 2)))]
                        if hs < 2:
                            w.append(tF.get(("wd_free", (g, icb - 1, hs + 2))) if icb > 0 else
                                     (tF.get(("wd_free", (g - 1, ICB - 1, hs + 2))) if g > 0 else None))
                        t_wd = p.step("sync", em_wdl, waits=w, sem=f"sfd{qw}", inc=16)
                        for hcl in range(4):
                            hc = hs * 4 + hcl
                            bp = [psD[0], psD[1]] if hc % 2 == 0 else [psS2, psB2]
                            def em_d(e, hcl=hcl, qw=qw, bp=bp):
                                inst = None
                                for ic in range(16):
                                    for ts in range(TS):
                                        inst = e.matmul(bp[ts][:, 0:TSW],
                                                        wd_cs[qw][:, ic * 512 + hcl * 128:ic * 512 + (hcl + 1) * 128],
                                                        hT[:, ic * TG + ts * TSW:ic * TG + (ts + 1) * TSW],
                                                        start=(ic == 0), stop=(ic == 15))
                                return inst
                            w = [t_wd, t_h_all,
                                 tF.get(("d_ev", (g, icb, (hc - 2) * TS))),
                                 tF.get(("d_ev", (g, icb, (hc - 2) * TS + TS - 1)))]
                            t_dmm = p.step("tensor", em_d, waits=w, sem="T", inc=1)
                            t_dmm_last = t_dmm
                            for ts in range(TS):
                                def em_dev(e, hc=hc, ts=ts, bp=bp, icb=icb):
                                    dst = acc[:, hc * TG + ts * TSW:hc * TG + (ts + 1) * TSW]
                                    if icb == 0:
                                        return e.tensor_copy(out=dst, in_=bp[ts][:, 0:TSW])
                                    return e.tensor_tensor(out=dst, in0=dst, in1=bp[ts][:, 0:TSW], op=ALU.add)
                                w2 = [t_dmm]
                                if icb == 0:
                                    w2.append(tF.get(("acc_free", g - 1)))
                                t_dev = p.step("vector", em_dev, waits=w2, sem="V", inc=1)
                                tF[("d_ev", (g, icb, hc * TS + ts))] = t_dev
                                t_down_last = t_dev
                        tF[("wd_free", (g, icb, hs))] = t_dmm_last
                    tF[("down_done", (g, icb))] = t_down_last
                    tF[("dmm_done", (g, icb))] = t_dmm_last
                tF[("ln2xT_free", g)] = h_hist[-1]
                t_acc_done = tF[("down_done", (g, ICB - 1))]

                t_epi_last = None
                for ts in range(TS):
                    t_mm_prev = tF.get(("psS2_free", (g, ts - 1))) or tF.get(("psS2_free", (g - 1, TS - 1)))
                    for hc in range(16):
                        t_q1 = p.step("scalar", lambda e, hc=hc, ts=ts: e.activation(
                            out=ffsq[:, 0:TSW], in_=acc[:, hc * TG + ts * TSW:hc * TG + (ts + 1) * TSW],
                            func=AF.Square),
                            waits=[t_acc_done, t_mm_prev], sem="S", inc=1)
                        def em_mm(e, hc=hc, ts=ts):
                            e.matmul(psS2[0:1, 0:TSW], onc[:],
                                     acc[:, hc * TG + ts * TSW:hc * TG + (ts + 1) * TSW],
                                     start=(hc == 0), stop=(hc == 15))
                            return e.matmul(psB2[0:1, 0:TSW], onc[:], ffsq[:, 0:TSW],
                                            start=(hc == 0), stop=(hc == 15))
                        t_mm_prev = p.step("tensor", em_mm, waits=[t_q1, t_acc_done], sem="T", inc=1)
                    t_stm = t_mm_prev
                    def em_f1(e, ts=ts):
                        e.tensor_scalar(out=srow2[0:1, 0:TSW], in0=psS2[0:1, 0:TSW],
                                        scalar1=1.0 / HID, scalar2=None, op0=ALU.mult)
                        return e.tensor_scalar(out=ffsq[0:1, 0:TSW], in0=psB2[0:1, 0:TSW],
                                               scalar1=1.0 / HID, scalar2=None, op0=ALU.mult)
                    t_f1 = p.step("vector", em_f1, waits=[t_stm], sem="V", inc=1)
                    t_f2 = p.step("vector", lambda e: e.tensor_tensor(
                        out=silu_t[0:1, 0:TSW], in0=srow2[0:1, 0:TSW],
                        in1=srow2[0:1, 0:TSW], op=ALU.mult), waits=[t_f1], sem="V", inc=1)
                    t_f2 = p.step("vector", lambda e: e.tensor_tensor(
                        out=ffsq[0:1, 0:TSW], in0=ffsq[0:1, 0:TSW],
                        in1=silu_t[0:1, 0:TSW], op=ALU.subtract), waits=[t_f2], sem="V", inc=1)
                    t_f2 = p.step("vector", lambda e: e.tensor_scalar(
                        out=ffsq[0:1, 0:TSW], in0=ffsq[0:1, 0:TSW],
                        scalar1=EPS_LN, scalar2=None, op0=ALU.add), waits=[t_f2], sem="V", inc=1)
                    t_f3 = p.step("scalar", lambda e: e.activation(out=ffsq[0:1, 0:TSW], in_=ffsq[0:1, 0:TSW],
                                                                   func=AF.Sqrt),
                                  waits=[t_f2], sem="S", inc=1)
                    t_f4 = p.step("vector", lambda e: e.reciprocal(out=srow2[0:1, TSW:2 * TSW],
                                                                   in_=ffsq[0:1, 0:TSW]),
                                  waits=[t_f3], sem="V", inc=1)
                    t_f5 = p.step("vector", lambda e: e.scalar_tensor_tensor(
                        out=srow2[0:1, 0:TSW], in0=srow2[0:1, 0:TSW], scalar=-1.0,
                        in1=srow2[0:1, TSW:2 * TSW], op0=ALU.mult, op1=ALU.mult),
                        waits=[t_f4], sem="V", inc=1)

                    t_sce = None
                    for tt in range(TSW // 128):
                        gi = g * TILES_G + ts * (TSW // 128) + tt
                        def em_sct(e, tt=tt):
                            e.matmul(psB2[0:128, 256:257],
                                     srow2[0:1, TSW + tt * 128:TSW + (tt + 1) * 128],
                                     id_f[0:1, 0:1], is_transpose=True, start=True, stop=False)
                            return e.matmul(psB2[0:128, 257:258],
                                            srow2[0:1, tt * 128:(tt + 1) * 128],
                                            id_f[0:1, 0:1], is_transpose=True, start=False, stop=True)
                        t_sct = p.step("tensor", em_sct,
                                       waits=[t_f5, t_f1 if tt == 0 else None,
                                              tF.get(("stc2ev_prev", 0))], sem="T", inc=1)
                        t_sce = p.step("vector", lambda e: e.tensor_copy(out=stc2[:], in_=psB2[0:128, 256:258]),
                                       waits=[t_sct, tF.get(("stc2_free", 0))], sem="V", inc=1)
                        tF[("scp2_free", 0)] = t_sce
                        tF[("stc2ev_prev", 0)] = t_sce
                        def em_y1l(e, gi=gi):
                            return e.dma_start(out=y1_f[0][:],
                                               in_=AP(y1_sp.tensor, gi * 128 * 2048, [[2048, 128], [1, 2048]]))
                        p.step("sync", em_y1l, waits=[t_y1_all, tF.get(("y1f_free", 0))],
                               sem="sy10", inc=16)
                        t_y1l = p.tok("sy10")
                        t_bt = []
                        for bb in range(4):
                            def em_bt(e, bb=bb, ts=ts, tt=tt):
                                inst = None
                                for k in range(4):
                                    hc = bb * 4 + k
                                    inst = e.matmul(psGU[bb][:, k * 128:(k + 1) * 128],
                                                    acc[:, hc * TG + ts * TSW + tt * 128:hc * TG + ts * TSW + (tt + 1) * 128],
                                                    id_f[:], is_transpose=True,
                                                    start=(k == 0), stop=(k == 3))
                                return inst
                            w = [t_acc_done, tF.get(("y2_prev", 0))]
                            if ts == 0 and tt == 0:
                                w.append(h_hist[-1])
                            t_bt.append(p.step("tensor", em_bt, waits=w, sem="T", inc=1))
                        def em_y2a(e):
                            inst = None
                            for bb in range(4):
                                inst = e.tensor_scalar(out=y2_b[0][:, bb * 512:(bb + 1) * 512], in0=psGU[bb][:],
                                                       scalar1=stc2[:, 0:1], scalar2=stc2[:, 1:2],
                                                       op0=ALU.mult, op1=ALU.add)
                            return inst
                        t_y2a = p.step("vector", em_y2a,
                                       waits=t_bt + [t_sce, tF.get(("y2st_prev", 0))],
                                       sem="V", inc=1)
                        t_y2 = p.step("vector", lambda e: e.tensor_tensor(
                            out=y2_b[0][:], in0=y2_b[0][:], in1=y1_f[0][:], op=ALU.add),
                            waits=[t_y2a, t_y1l], sem="V", inc=1)
                        tF[("y2_prev", 0)] = t_y2
                        tF[("y1f_free", 0)] = t_y2
                        tF[("stc2_free", 0)] = t_y2
                        def em_y2st(e, gi=gi):
                            return e.dma_start(out=AP(y2_out.tensor, gi * 128 * 2048, [[2048, 128], [1, 2048]]),
                                               in_=y2_b[0][:])
                        t_y2s = p.step("scalar", em_y2st, waits=[t_y2], sem="sout", inc=16)
                        tF[("y2st_prev", 0)] = t_y2s
                        t_epi_last = t_y2s
                    tF[("psS2_free", (g, ts))] = t_sce
                tF[("epi_done", g)] = t_epi_last
                tF[("acc_free", g)] = t_epi_last

            p.step("sync", lambda e: e.nop(),
                   waits=[p.tok("sout"), p.tok("sstm"), p.tok("ssty"), p.tok("sstl"),
                          p.tok("sstr"), p.tok("sdbg0"), p.tok("sdbg1"), p.tok("sdb2"), p.tok("sda")],
                   sem=None, inc=0)

        with nc.Block() as blk:
            blk.sync(lambda e: p.emit_engine("sync", e))
            blk.gpsimd(lambda e: p.emit_engine("gpsimd", e))
            blk.tensor(lambda e: p.emit_engine("tensor", e))
            blk.vector(lambda e: p.emit_engine("vector", e))
            blk.scalar(lambda e: p.emit_engine("scalar", e))

    return nc


_NC_CACHE = {}


def _prep_inputs(inputs, n_cores, n_tiles):
    NTOK_ALL = n_cores * n_tiles * 128
    f32 = np.float32
    X = np.asarray(inputs["L0_attn_hidden_states"], f32).reshape(NTOK_ALL, HID)
    XT = np.ascontiguousarray(X.T)
    Wq = np.asarray(inputs["L0_attn_Wq"], f32)
    Wk = np.asarray(inputs["L0_attn_Wk"], f32)
    Wv = np.asarray(inputs["L0_attn_Wv"], f32)
    Wg = np.asarray(inputs["L0_attn_Wg"], f32)
    Wmb = np.asarray(inputs["L0_attn_Wmb"], f32)
    bq = np.asarray(inputs["L0_attn_bq"], f32)
    bk = np.asarray(inputs["L0_attn_bk"], f32)
    bv = np.asarray(inputs["L0_attn_bv"], f32)
    bg = np.asarray(inputs["L0_attn_bg"], f32)
    bmb = np.asarray(inputs["L0_attn_bmb"], f32)
    memory = np.asarray(inputs["L0_attn_memory"], f32)
    norm = np.asarray(inputs["L0_attn_norm"], f32)
    Wgu = np.asarray(inputs["L0_ffn_w_gate_up"], f32)
    Wd = np.asarray(inputs["L0_ffn_w_down"], f32)

    shared = {
        "wkT": np.ascontiguousarray(Wk.T),
        "wvT": np.ascontiguousarray(Wv.T),
        "wmbT": np.ascontiguousarray(Wmb.T),
        "wqT": np.ascontiguousarray(Wq.T),
        "wgT": np.ascontiguousarray(Wg.T),
        "mem4": np.ascontiguousarray(memory.transpose(1, 0, 2).reshape(FD, HID)),
        "nn96": np.ascontiguousarray(norm.reshape(NH, FC, 128).transpose(2, 0, 1).reshape(128, 96)),
        "nn96b": np.ascontiguousarray(norm.reshape(NH, FC, 128).transpose(2, 1, 0).reshape(128, 96)),
        "biasA": np.ascontiguousarray(np.broadcast_to(
            np.concatenate([bk, bv, bmb])[None, :], (128, 3072))).astype(ml_dtypes.bfloat16),
        "biasQ": np.ascontiguousarray(np.broadcast_to(bq[None, :], (128, HID))).astype(ml_dtypes.bfloat16),
        "biasG": np.ascontiguousarray(np.broadcast_to(bg[None, :], (128, HID))).astype(ml_dtypes.bfloat16),
        "idb": np.eye(128, dtype=ml_dtypes.bfloat16),
        "id32": np.eye(128, dtype=f32),
        "ones_col": np.ones((128, 1), f32),
        "ones_row": np.ones((1, 128), f32),
    }
    wflat = np.concatenate([np.ascontiguousarray(Wgu.T).ravel(),
                            np.ascontiguousarray(Wd.T).ravel()])
    SLICE = WTOT // n_cores
    NTOK = n_tiles * 128
    in_maps = []
    for c in range(n_cores):
        m = dict(shared)
        m["xT"] = np.ascontiguousarray(XT[:, c * NTOK:(c + 1) * NTOK])
        m["x_in"] = np.ascontiguousarray(X[c * NTOK:(c + 1) * NTOK])
        m["wconv"] = np.ascontiguousarray(wflat[c * SLICE:(c + 1) * SLICE].reshape(128, SLICE // 128))
        in_maps.append(m)
    return in_maps


def kernel(**inputs):
    n_cores, n_tiles = 8, 16
    key = (n_cores, n_tiles)
    if key not in _NC_CACHE:
        _NC_CACHE[key] = build_program(n_cores=n_cores, n_tiles=n_tiles)
    nc = _NC_CACHE[key]
    in_maps = _prep_inputs(inputs, n_cores, n_tiles)
    res = run_bass_kernel_spmd(nc, in_maps, core_ids=list(range(n_cores)))
    out = np.concatenate([res.results[c]["y2"] for c in range(n_cores)], axis=0)
    return out.reshape(2, 8192, HID)
